# version 05
# baseline (speedup 1.0000x reference)
import contextlib
import math
import numpy as np

import concourse.bass as bass
import concourse.mybir as mybir
from concourse.bass_utils import run_bass_kernel_spmd

F32 = mybir.dt.float32
BF16 = mybir.dt.bfloat16
I32 = mybir.dt.int32
AF = mybir.ActivationFunctionType
ALU = mybir.AluOpType
AX = mybir.AxisListType

ENGINES = ("sync", "gpsimd", "vector", "scalar", "tensor")

T = 4096
D = 1024
NCHUNK = 8
NTILE = 4
H0 = 16
HD = 64
E_ROUTED = 14
E_SHARED = 2
HID0 = 2048
RMS_EPS = 1.1920929e-07
LN_EPS = 1e-5
MOE_EPS = 1e-9
N_HEADS1 = 16
N_KV1 = 4
SINK = 4
WIN = 256
HID1 = 4096
INV2PI = 1.0 / (2.0 * math.pi)
TWOPI = 2.0 * math.pi


class Sem:
    def __init__(self, nc, es, name):
        self.h = es.enter_context(nc.semaphore(name))
        self.c = 0

    def at(self):
        return (self, self.c)


class Ctx:
    def __init__(self, nc):
        self.nc = nc
        self.steps = []
        self.es_global = contextlib.ExitStack()
        self.dram = {}
        self.nbar = 0
        self.sems = {}
        self.consts = {}
        self.scope = {}

    def sem(self, name):
        s = Sem(self.nc, self.es_global, name)
        self.sems[name] = s
        return s

    def sb(self, es, name, shape, dtype, side=None):
        return es.enter_context(self.nc.sbuf_tensor(name, shape, dtype, side=side))

    def ps(self, es, name, shape, dtype=F32, side=None):
        return es.enter_context(self.nc.psum_tensor(name, shape, dtype, side=side))

    CHAINED = {"vector": "s_v", "scalar": "s_s", "gpsimd": "s_g"}

    def emit(self, eng, fn, waits=(), inc=None, inc_n=1, chain=True):
        w = [(s.h, v) for (s, v) in waits if v > 0]
        extra = []
        if inc_n >= 16:
            chain = False
        if fn is not None and eng in self.CHAINED and chain:
            se = self.sems[self.CHAINED[eng]]
            if inc is se:
                inc = None
            assert inc is None, f"chained {eng} op cannot carry another inc"
            if se.c > 0:
                w.append((se.h, se.c))
            extra.append(se)
            se.c += 1
        self.steps.append((eng, fn, w, inc.h if inc is not None else None, inc_n, extra))
        if inc is not None:
            inc.c += inc_n

    def make_transposer(self, ps_pair, ident):
        state = {"i": 0, "guard": [0, 0]}
        s_v, s_t = self.sems["s_v"], self.sems["s_t"]

        def emit_tr(src_ap, dst_ap, tw=(), cw=()):
            par = state["i"] % 2
            state["i"] += 1
            twl = list(tw)
            if state["guard"][par] > 0:
                twl.append((s_v, state["guard"][par]))
            self.emit("tensor", lambda e, par=par, src_ap=src_ap: e.transpose(
                ps_pair[par][:, :], src_ap, ident[:, :]), waits=twl, inc=s_t)
            cwl = list(cw) + [s_t.at()]
            self.emit("vector", lambda e, par=par, dst_ap=dst_ap: e.tensor_copy(
                dst_ap, ps_pair[par][:, :]), waits=cwl)
            state["guard"][par] = s_v.c
        return emit_tr

    def barrier(self):
        bar = self.sems["bar"]
        dma_waits = [(self.sems[n], self.sems[n].c)
                     for n in ("s_ldw", "s_lda", "s_st", "s_lds", "s_wb0", "s_wb1",
                               "s_wb2", "s_wfl") if n in self.sems]
        self.nbar += 1
        tgt = 5 * self.nbar
        bps, bsb, bsb2 = self.consts["barbufs"]
        self.emit("sync", lambda e: e.nop(), waits=dma_waits, inc=bar)
        self.emit("gpsimd", lambda e: e.memset(bsb[0:1, 0:1], 0),
                  waits=dma_waits + [self.sems["s_g"].at()], inc=bar, chain=False)
        self.emit("vector", lambda e: e.memset(bsb[0:1, 1:2], 0),
                  waits=[self.sems["s_v"].at()], inc=bar, chain=False)
        self.emit("scalar", lambda e: e.copy(bsb[0:1, 2:3], bsb[0:1, 3:4]),
                  waits=[self.sems["s_s"].at(), self.sems["s_g"].at()], inc=bar, chain=False)
        self.emit("tensor", lambda e: e.matmul(bps[0:1, 0:1], bsb2[0:1, 0:1], bsb2[0:1, 0:1],
                                               start=True, stop=True),
                  waits=[self.sems["s_g"].at(), self.sems["s_v"].at(),
                         self.sems["s_s"].at()], inc=bar)
        for eng in ENGINES:
            self.emit(eng, None, waits=[(bar, tgt)])

    def replay(self):
        with self.nc.Block() as block:
            for eng_name in ENGINES:
                def make_body(eng_name=eng_name):
                    def body(eng):
                        for (e, fn, w, inch, incn, extra) in self.steps:
                            if e != eng_name:
                                continue
                            for (h, v) in w:
                                eng.wait_ge(h, v)
                            if fn is not None:
                                inst = fn(eng)
                                if inch is not None:
                                    assert inst is not None, "inc on step with no inst"
                                    inst.then_inc(inch, incn)
                                for s in extra:
                                    assert inst is not None
                                    inst.then_inc(s.h, 1)
                    return body
                getattr(block, eng_name)(make_body())


def rearr_tok(ap, ntile=NTILE):
    return ap.rearrange("(t p) d -> p t d", p=128)


def rearr_feat_w(ap):
    return ap.rearrange("(c p) n -> p c n", p=128)



def setup_consts(ctx):
    nc = ctx.nc
    es = ctx.es_global
    c = ctx.consts
    bps = ctx.ps(es, "bar_ps", [1, 1], F32)
    bsb = ctx.sb(es, "bar_sb", [1, 8], F32)
    bsb2 = ctx.sb(es, "bar_sb2", [1, 1], BF16)
    c["barbufs"] = (bps, bsb, bsb2)

    s_ldw = ctx.sem("s_ldw")
    s_lda = ctx.sem("s_lda")
    s_st = ctx.sem("s_st")
    s_lds = ctx.sem("s_lds")
    for i in range(3):
        ctx.sem(f"s_wb{i}")
    ctx.sem("s_wfl")
    ctx.sem("bar")
    s_v = ctx.sem("s_v")
    s_s = ctx.sem("s_s")
    s_t = ctx.sem("s_t")
    s_g = ctx.sem("s_g")

    ctx.emit("gpsimd", lambda e: e.memset(bsb[0:1, :], 0.0))
    ctx.emit("gpsimd", lambda e: e.memset(bsb2[0:1, :], 0.0))

    es_tmp = contextlib.ExitStack()
    ident = ctx.sb(es, "ident", [128, 128], BF16)
    iota_i = ctx.sb(es_tmp, "iota_i", [128, 512], I32, side="right")
    iota_f = ctx.sb(es_tmp, "iota_f", [128, 512], F32, side="right")
    jcol_f = ctx.sb(es_tmp, "jcol_f", [128, 512], F32, side="right")
    ones_k1 = ctx.sb(es, "ones_k1", [1, 128], BF16)
    c["ident"] = ident
    c["ones_k1"] = ones_k1

    identF = ctx.sb(es, "identF", [128, 128], F32)
    c["identF"] = identF
    ctx.emit("gpsimd", lambda e: e.iota(iota_i[:, 0:128], [[1, 128]], channel_multiplier=-1), inc=s_g)
    ctx.emit("vector", lambda e: e.tensor_single_scalar(ident[:, :], iota_i[:, 0:128], 0, ALU.is_equal),
             waits=[s_g.at()], inc=s_v)
    ctx.emit("vector", lambda e: e.tensor_copy(identF[:, :], ident[:, :]))
    ctx.emit("gpsimd", lambda e: e.memset(ones_k1[:, :], 1.0), inc=s_g)

    blk8 = ctx.sb(es, "blk8", [128, 8, 16], BF16)
    sel8 = ctx.sb(es, "sel8", [16, 8, 128], BF16)
    blkI = ctx.sb(es_tmp, "blkI", [128, 8, 16], BF16, side="right")
    c["blk8"] = blk8
    c["sel8"] = sel8
    ctx.emit("gpsimd", lambda e: e.memset(blk8[:, :, :], 0.0), inc=s_g)
    ctx.emit("gpsimd", lambda e: e.memset(blkI[:, :, :], 0.0), inc=s_g)
    wz = [s_g.at()]
    for ec in range(8):
        ctx.emit("gpsimd", lambda e, ec=ec: e.memset(blk8[0:64, ec, 2 * ec:2 * ec + 1], 1.0 / 64),
                 waits=wz if ec == 0 else [])
        ctx.emit("gpsimd", lambda e, ec=ec: e.memset(blk8[64:128, ec, 2 * ec + 1:2 * ec + 2], 1.0 / 64))
        ctx.emit("gpsimd", lambda e, ec=ec: e.memset(blkI[0:64, ec, 2 * ec:2 * ec + 1], 1.0))
        ctx.emit("gpsimd", lambda e, ec=ec: e.memset(blkI[64:128, ec, 2 * ec + 1:2 * ec + 2], 1.0), inc=s_g)
    ps_tr16 = ctx.ps(es_tmp, "ps_tr16", [16, 8, 128], BF16)
    for ec in range(8):
        ctx.emit("tensor", lambda e, ec=ec: e.transpose(ps_tr16[:, ec, :], blkI[:, ec, 0:16], ident[:, :]),
                 waits=[s_g.at(), s_v.at()], inc=s_t)
    ctx.emit("vector", lambda e: e.tensor_copy(sel8[:, :, :], ps_tr16[:, :, :]), waits=[s_t.at()], inc=s_v)

    router_sb = ctx.sb(es, "router_sb", [128, 8, 14], F32)
    c["router_sb"] = router_sb
    ctx.emit("gpsimd", lambda e: e.dma_start(router_sb[:, :, :],
                                             rearr_feat_w(ctx.dram["L0_ffn_router_w"].ap())), inc=s_ldw, inc_n=16)

    w_tok = ctx.sb(es, "w_tok", [128, 32, 14], F32)
    c["w_tok"] = w_tok
    cLN = ctx.sb(es, "cLN", [128, 1], F32)
    cRMS = ctx.sb(es, "cRMS", [128, 1], F32)
    c["cLN"], c["cRMS"] = cLN, cRMS
    ctx.emit("gpsimd", lambda e: e.memset(cLN[:, :], LN_EPS))
    ctx.emit("gpsimd", lambda e: e.memset(cRMS[:, :], RMS_EPS), inc=s_g)

    rms_all = ctx.sb(es, "rms_all", [128, 16, 8], F32)
    c["rms_all"] = rms_all
    for e_i in range(E_ROUTED):
        src = ctx.dram["L0_ffn_rms_w"].ap()[e_i:e_i + 1, :].rearrange("q (c p) -> q p c", p=128)
        ctx.emit("gpsimd", lambda e, e_i=e_i, src=src: e.dma_start(
            rms_all[:, e_i, :], src, allow_slow_non_contiguous=True), inc=s_ldw, inc_n=16)
    for e_i in range(E_SHARED):
        src = ctx.dram["L0_ffn_rms_w_s"].ap()[e_i:e_i + 1, :].rearrange("q (c p) -> q p c", p=128)
        ctx.emit("gpsimd", lambda e, e_i=e_i, src=src: e.dma_start(
            rms_all[:, E_ROUTED + e_i, :], src, allow_slow_non_contiguous=True), inc=s_ldw, inc_n=16)

    b1v = ctx.sb(es, "b1v", [128, 32], F32)
    thv = ctx.sb(es_tmp, "thv", [128, 32], F32, side="right")
    wrv = ctx.sb(es_tmp, "wrv", [128, 32], F32, side="right")
    wiv = ctx.sb(es_tmp, "wiv", [128, 32], F32, side="right")
    b2v = ctx.sb(es, "b2v", [128, 8], F32)
    qbias = ctx.sb(es, "qbias", [128, 32], F32)
    Cv = ctx.sb(es, "Cv", [128, 32], F32)
    c["b1v"] = b1v
    c["qbias"] = qbias
    c["Cv"] = Cv
    c["b2v"] = b2v
    for (nm, dst, n) in [("L1_ffn_b1", b1v, 32), ("L1_ffn_quantum_phases", thv, 32),
                         ("L1_ffn_quantum_weights_real", wrv, 32),
                         ("L1_ffn_quantum_weights_imag", wiv, 32), ("L1_ffn_b2", b2v, 8)]:
        for m in range(n):
            ctx.emit("scalar", lambda e, nm=nm, dst=dst, m=m: e.dma_start(
                dst[:, m:m + 1], ctx.dram[nm].ap()[m * 128:(m + 1) * 128].rearrange("(p q) -> p q", q=1)),
                inc=s_lds, inc_n=16)
    t1 = ctx.sb(es_tmp, "qc_t1", [128, 32], F32, side="right")
    t2 = ctx.sb(es_tmp, "qc_t2", [128, 32], F32, side="right")
    t3i = ctx.sb(es_tmp, "qc_t3i", [128, 32], I32, side="right")
    sth = ctx.sb(es_tmp, "qc_sth", [128, 32], F32, side="right")
    cth = ctx.sb(es_tmp, "qc_cth", [128, 32], F32, side="right")
    w0 = [(s_lds, s_lds.c)]

    def fold_full(buf):
        ctx.emit("vector", lambda e: e.tensor_copy(t3i[:, :], buf[:, :]))
        ctx.emit("vector", lambda e: e.tensor_copy(t2[:, :], t3i[:, :]))
        ctx.emit("vector", lambda e: e.tensor_tensor(buf[:, :], buf[:, :], t2[:, :], ALU.subtract))
        ctx.emit("vector", lambda e: e.tensor_single_scalar(t2[:, :], buf[:, :], 0.5, ALU.is_gt))
        ctx.emit("vector", lambda e: e.scalar_tensor_tensor(buf[:, :], t2[:, :], -1.0, buf[:, :], ALU.mult, ALU.add))
        ctx.emit("vector", lambda e: e.tensor_single_scalar(t2[:, :], buf[:, :], -0.5, ALU.is_lt))
        ctx.emit("vector", lambda e: e.scalar_tensor_tensor(buf[:, :], t2[:, :], 1.0, buf[:, :], ALU.mult, ALU.add))

    ctx.emit("vector", lambda e: e.scalar_tensor_tensor(qbias[:, :], b1v[:, :], 0.1, thv[:, :], ALU.mult, ALU.add), waits=w0)
    ctx.emit("vector", lambda e: e.tensor_single_scalar(qbias[:, :], qbias[:, :], INV2PI, ALU.mult))
    fold_full(qbias)
    ctx.emit("vector", lambda e: e.tensor_single_scalar(t1[:, :], thv[:, :], INV2PI, ALU.mult))
    fold_full(t1)
    ctx.emit("scalar", lambda e: e.activation(sth[:, :], t1[:, :], AF.Sin, scale=TWOPI), waits=[s_v.at()])
    ctx.emit("vector", lambda e: e.tensor_scalar(t1[:, :], thv[:, :], INV2PI, 0.25, ALU.mult, ALU.add),
             waits=[s_s.at()])
    fold_full(t1)
    ctx.emit("scalar", lambda e: e.activation(cth[:, :], t1[:, :], AF.Sin, scale=TWOPI), waits=[s_v.at()])
    ctx.emit("vector", lambda e: e.tensor_tensor(t1[:, :], cth[:, :], wrv[:, :], ALU.mult), waits=[s_s.at()])
    ctx.emit("vector", lambda e: e.tensor_tensor(t2[:, :], sth[:, :], wiv[:, :], ALU.mult))
    ctx.emit("vector", lambda e: e.tensor_tensor(t1[:, :], t1[:, :], t2[:, :], ALU.add))
    ctx.emit("vector", lambda e: e.scalar_tensor_tensor(Cv[:, :], t1[:, :], 0.1, wiv[:, :], ALU.mult, ALU.mult))

    ctx.barrier()
    es_tmp.close()


def setup_l0_weights(ctx):
    c = ctx.consts
    s_ldw = ctx.sems["s_ldw"]
    s_v, s_t = ctx.sems["s_v"], ctx.sems["s_t"]
    ident = c["ident"]
    es_tmp = contextlib.ExitStack()
    es = ctx.scope["ab"]
    wnat = ctx.sb(es_tmp, "wnat_stage", [128, 8, 1024], BF16, side="right")
    ps_trw = [ctx.ps(es_tmp, "ps_trwA", [128, 128], BF16, side="right"),
              ctx.ps(es_tmp, "ps_trwB", [128, 128], BF16, side="right")]
    names = [("L0_attn_Wq", "WqT"), ("L0_attn_Wk", "WkT"), ("L0_attn_Wv", "WvT"),
             ("L0_attn_Wu", "WuT"), ("L0_attn_Wo", "WoT")]
    ntr = 0
    ncp = 0
    for (src_name, dst_name) in names:
        dst = ctx.sb(es, dst_name, [128, 8, 1024], BF16)
        c[dst_name] = dst
        ctx.emit("gpsimd", lambda e, s=src_name: e.dma_start(
            wnat[:, :, :], rearr_feat_w(ctx.dram[s].ap())),
            waits=[s_t.at()], inc=s_ldw, inc_n=16)
        for ce in range(8):
            for kd in range(8):
                par = ntr % 2
                twl = []
                if ce == 0 and kd == 0:
                    twl.append((s_ldw, s_ldw.c))
                    twl.append((s_v, s_v.c))
                if ntr >= 2:
                    twl.append((s_v, ncp - 1))
                ctx.emit("tensor", lambda e, ce=ce, kd=kd, par=par: e.transpose(
                    ps_trw[par][:, :], wnat[:, ce, kd * 128:(kd + 1) * 128], ident[:, :]),
                    waits=twl, inc=s_t)
                ntr += 1
                ctx.emit("vector", lambda e, dst=dst, ce=ce, kd=kd, par=par: e.tensor_copy(
                    dst[:, kd, ce * 128:(ce + 1) * 128], ps_trw[par][:, :]),
                    waits=[(s_t, s_t.c)])
                ncp = s_v.c


    ctx.barrier()
    es_tmp.close()


def setup_l1_weights(ctx):
    c = ctx.consts
    es = ctx.scope["de"]
    s_ldw = ctx.sems["s_ldw"]
    for (nm, dst_name, ecols) in [("L1_attn_Wk", "Wk1", 256), ("L1_attn_Wv", "Wv1", 256)]:
        dst = ctx.sb(es, dst_name, [128, 8, ecols], BF16)
        c[dst_name] = dst
        ctx.emit("gpsimd", lambda e, nm=nm, dst=dst: e.dma_start(
            dst[:, :, :], rearr_feat_w(ctx.dram[nm].ap())), inc=s_ldw, inc_n=16)
    wq1 = ctx.sb(es, "Wq1", [128, 8, 1024], BF16)
    wo1 = ctx.sb(es, "Wo1", [128, 8, 1024], BF16)
    c["Wq1"], c["Wo1"] = wq1, wo1
    for s in range(8):
        grp, j = s // 4, s % 4
        for pos in range(2):
            h = grp * 8 + pos * 4 + j
            ctx.emit("gpsimd", lambda e, s=s, pos=pos, h=h: e.dma_start(
                wq1[:, :, s * 128 + pos * 64:s * 128 + pos * 64 + 64],
                ctx.dram["L1_attn_Wq"].ap()[:, h * 64:(h + 1) * 64].rearrange(
                    "(c p) e -> p c e", p=128)), inc=s_ldw, inc_n=16)
            ctx.emit("gpsimd", lambda e, s=s, pos=pos, h=h: e.dma_start(
                wo1[pos * 64:pos * 64 + 64, s, :],
                ctx.dram["L1_attn_Wo"].ap()[h * 64:(h + 1) * 64, :]), inc=s_ldw, inc_n=16)






def phase_l0_a(ctx):
    nc = ctx.nc
    es_ab = ctx.scope["ab"]
    c = ctx.consts
    s_lda, s_ldw, s_st = ctx.sems["s_lda"], ctx.sems["s_ldw"], ctx.sems["s_st"]
    s_v, s_s, s_t = ctx.sems["s_v"], ctx.sems["s_s"], ctx.sems["s_t"]

    kv_ps = ctx.ps(es_ab, "kv_ps", [128, 512], F32)
    kv_acc = ctx.sb(es_ab, "kv_acc", [128, 512], F32)
    ctx.consts["kv_ps"] = kv_ps
    ctx.consts["kv_acc"] = kv_acc
    es = contextlib.ExitStack()
    x_tok2 = ctx.sb(es, "a_x_tok", [128, 2, NTILE, D], F32)
    stats = ctx.sb(es, "a_stats", [128, 2, 6, NTILE], F32)
    xnt321 = ctx.sb(es, "a_xnt32", [128, 1, NTILE, D], F32)
    sq_buf = xnt321[:, 0, 0, :]
    xnf32 = ctx.sb(es, "a_xnf32", [128, 8, 128], F32)
    xnb_tok2 = ctx.sb(es, "a_xnb_tok", [128, 2, NTILE, D], BF16)
    rms4f = ctx.sb(es, "a_rms4f", [4, 128], F32)
    rmsr_b = ctx.sb(es, "a_rmsr", [1, 512], BF16)
    xnb_feat = ctx.sb(es, "a_xnb_feat", [128, 8, 512], BF16)
    xrb_feat = ctx.sb(es, "a_xrb_feat", [128, 8, 512], BF16)
    k_tok = ctx.sb(es, "a_k_tok", [128, NTILE, D], BF16)
    v_tok = ctx.sb(es, "a_v_tok", [128, NTILE, D], BF16)
    q_stage = ctx.sb(es, "a_q_stage", [128, 8, 512], BF16)
    u_stage = ctx.sb(es, "a_u_stage", [128, 8, 512], BF16)
    gexp = ctx.sb(es, "a_gexp", [128, 16], F32)
    gtmp = ctx.sb(es, "a_gtmp", [128, 16], F32)
    gkeep = ctx.sb(es, "a_gkeep", [128, 16], F32)
    gstat = ctx.sb(es, "a_gstat", [128, 4], F32)
    w_tok = ctx.consts["w_tok"]

    ps_tr = [ctx.ps(es, "a_ps_trA", [128, 128], BF16),
             ctx.ps(es, "a_ps_trB", [128, 128], BF16)]
    ps_misc = ctx.ps(es, "a_ps_misc", [128, 512], F32)
    ps_proj = [ctx.ps(es, f"a_ps_proj{i}", [128, 512], F32) for i in range(3)]

    ident = c["ident"]
    emit_tr = ctx.make_transposer(ps_tr, ident)
    ntr = 0
    nproj = 0
    bank_guard = [None, None, None]
    qu_store_guard = [0]
    xr_store_guard = [0]
    gate_guard = [0]
    kv_evac_guard = [0]
    xnt_guard = [0]
    feat_guard = [0]
    rms_mm_guard = [0]
    ctx.emit("vector", lambda e: e.memset(kv_acc[:, :], 0.0))

    x_guard = [[], []]
    lda_chunk = [0] * NCHUNK
    for ch in range(NCHUNK):
        par = ch % 2
        x_tok = x_tok2[:, par]
        xnt32 = xnt321[:, 0]
        xnb_tok = xnb_tok2[:, par]
        ssq, ssm = stats[:, par, 0], stats[:, par, 1]
        mean, rstd = stats[:, par, 2], stats[:, par, 3]
        rmsf, tstat = stats[:, par, 4], stats[:, par, 5]
        xsrc = ctx.dram["L0_attn_x"].ap()[ch * 512:(ch + 1) * 512, :]
        ctx.emit("sync", lambda e, xsrc=xsrc, par=par: e.dma_start(
            x_tok2[:, par, :, :], rearr_tok(xsrc)),
            waits=x_guard[par], inc=s_lda, inc_n=16)
        lda_chunk[ch] = s_lda.c
        wx = [s_lda.at()]

        for t in range(NTILE):
            sqw = list(wx) if t == 0 else []
            if t == 0 and xnt_guard[0] > 0:
                sqw.append((s_t, xnt_guard[0]))
            ctx.emit("scalar", lambda e, t=t, x_tok=x_tok, ssq=ssq: e.activation(
                sq_buf, x_tok[:, t, :], AF.Square, accum_out=ssq[:, t:t + 1]),
                waits=sqw, inc=s_s)
            ctx.emit("vector", lambda e, t=t, x_tok=x_tok, ssm=ssm: e.tensor_reduce(
                ssm[:, t:t + 1], x_tok[:, t, :], AX.X, ALU.add), waits=wx if t == 0 else [], inc=s_v)
        ctx.emit("vector", lambda e, mean=mean, ssm=ssm: e.tensor_single_scalar(
            mean[:, :], ssm[:, :], 1.0 / D, ALU.mult), waits=[s_s.at(), s_v.at()])
        ctx.emit("vector", lambda e, tstat=tstat, mean=mean: e.tensor_tensor(
            tstat[:, :], mean[:, :], mean[:, :], ALU.mult))
        ctx.emit("vector", lambda e, tstat=tstat, ssq=ssq: e.scalar_tensor_tensor(
            tstat[:, :], ssq[:, :], 1.0 / D, tstat[:, :], ALU.mult, ALU.subtract), inc=s_v)
        ctx.emit("scalar", lambda e, rstd=rstd, tstat=tstat: e.activation(
            rstd[:, :], tstat[:, :], AF.Ln, bias=c["cLN"][:, :]), waits=[s_v.at()], inc=s_s)
        ctx.emit("scalar", lambda e, rstd=rstd: e.activation(
            rstd[:, :], rstd[:, :], AF.Exp, scale=-0.5), inc=s_s)
        ctx.emit("vector", lambda e, rmsf=rmsf, rstd=rstd: e.tensor_tensor(
            rmsf[:, :], rstd[:, :], rstd[:, :], ALU.mult), waits=[s_s.at()])
        ctx.emit("vector", lambda e, rmsf=rmsf, tstat=tstat: e.tensor_tensor(
            rmsf[:, :], rmsf[:, :], tstat[:, :], ALU.mult), inc=s_v)
        ctx.emit("scalar", lambda e, rmsf=rmsf: e.activation(
            rmsf[:, :], rmsf[:, :], AF.Ln, bias=c["cRMS"][:, :]), waits=[s_v.at()], inc=s_s)
        ctx.emit("scalar", lambda e, rmsf=rmsf: e.activation(
            rmsf[:, :], rmsf[:, :], AF.Exp, scale=-0.5), inc=s_s)
        wstat = [s_s.at()]

        for t in range(NTILE):
            wl = list(wstat) if t == 0 else []
            if t == 0 and xnt_guard[0] > 0:
                wl.append((s_t, xnt_guard[0]))
            ctx.emit("vector", lambda e, t=t, xnt32=xnt32, x_tok=x_tok, mean=mean, rstd=rstd: e.tensor_scalar(
                xnt32[:, t, :], x_tok[:, t, :], mean[:, t:t + 1], rstd[:, t:t + 1],
                ALU.subtract, ALU.mult), waits=wl)
            ctx.emit("vector", lambda e, t=t, xnb_tok=xnb_tok, xnt32=xnt32: e.tensor_copy(
                xnb_tok[:, t, :], xnt32[:, t, :]))
        wxn = [s_v.at()]
        x_guard[par] = [(s_v, s_v.c), (s_s, s_s.c)]
        for t in range(NTILE):
            for k in range(8):
                tw = wxn if (t == 0 and k == 0) else []
                cw = []
                if t == 0 and k == 0 and feat_guard[0] > 0:
                    cw = [(s_t, feat_guard[0])]
                emit_tr(xnb_tok[:, t, k * 128:(k + 1) * 128],
                        xnb_feat[:, k, t * 128:(t + 1) * 128], tw=tw, cw=cw)
        wfeat = [s_v.at()]

        ctx.emit("tensor", lambda e, rmsf=rmsf: e.transpose(
            ps_misc[0:4, 0:128], rmsf[:, 0:4], c["identF"][:, :]),
            waits=([(s_s, gate_guard[0])] if gate_guard[0] > 0 else []) + [(s_v, s_v.c)],
            inc=s_t)
        ctx.emit("vector", lambda e: e.tensor_copy(rms4f[:, :], ps_misc[0:4, 0:128]),
                 waits=[s_t.at(), (s_ldw, s_ldw.c)])
        for tt in range(NTILE):
            ctx.emit("gpsimd", lambda e, tt=tt: e.dma_start(
                rmsr_b[0:1, tt * 128:(tt + 1) * 128], rms4f[tt:tt + 1, :]),
                waits=([(s_v, s_v.c), (s_t, rms_mm_guard[0])] if tt == 0 else []),
                inc=s_ldw, inc_n=16)
        bankx = nproj % 3
        nproj += 1
        bwl = [(s_ldw, s_ldw.c)]
        if bank_guard[bankx]:
            bwl.append(bank_guard[bankx])
        ctx.emit("tensor", lambda e, bankx=bankx: e.matmul(
            ps_proj[bankx][:, :], c["ones_k1"][:, :], rmsr_b[0:1, :], start=True, stop=True),
            waits=bwl, inc=s_t)
        rms_mm_guard[0] = s_t.c
        xrw = [s_t.at()]
        for k in range(8):
            cwx = list(xrw) if k == 0 else []
            if k == 0 and xr_store_guard[0] > 0:
                cwx.append((s_st, xr_store_guard[0]))
            ctx.emit("vector", lambda e, k=k, bankx=bankx: e.tensor_tensor(
                xrb_feat[:, k, :], xnb_feat[:, k, :], ps_proj[bankx][:, :], ALU.mult),
                waits=cwx)
        bank_guard[bankx] = (s_v, s_v.c)

        ctx.emit("sync", lambda e, ch=ch: e.dma_start(
            ctx.dram["xr_feat"].ap()[:, :, ch * 512:(ch + 1) * 512], xrb_feat[:, :, :]),
            waits=[s_v.at()], inc=s_st, inc_n=16)

        for (wt, stage, act, scale) in ((c["WqT"], q_stage, AF.Relu, 0.125),
                                        (c["WuT"], u_stage, AF.Silu, 1.0)):
            for m in range(8):
                bank = nproj % 3
                wlist = list(wfeat) if m == 0 else []
                if bank_guard[bank]:
                    wlist.append(bank_guard[bank])
                if m == 0 and qu_store_guard[0] > 0:
                    pass
                for k in range(8):
                    ctx.emit("tensor", lambda e, wt=wt, m=m, k=k, bank=bank: e.matmul(
                        ps_proj[bank][:, :], wt[:, k, m * 128:(m + 1) * 128], xnb_feat[:, k, :],
                        start=(k == 0), stop=(k == 7)),
                        waits=wlist if k == 0 else [], inc=(s_t if k == 7 else None))
                nproj += 1
                ctx.emit("scalar", lambda e, stage=stage, m=m, act=act, scale=scale, bank=bank: e.activation(
                    stage[:, m, :], ps_proj[bank][:, :], act, scale=scale),
                    waits=[s_t.at()] + ([(s_st, qu_store_guard[0])] if (m == 0 and qu_store_guard[0] > 0) else []),
                    inc=s_s)
                bank_guard[bank] = (s_s, s_s.c)
        ctx.emit("sync", lambda e, ch=ch: e.dma_start(
            ctx.dram["q_feat"].ap()[:, :, ch * 512:(ch + 1) * 512], q_stage[:, :, :]),
            waits=[s_s.at()], inc=s_st, inc_n=16)
        ctx.emit("sync", lambda e, ch=ch: e.dma_start(
            ctx.dram["u_feat"].ap()[:, :, ch * 512:(ch + 1) * 512], u_stage[:, :, :]),
            waits=[s_s.at()], inc=s_st, inc_n=16)
        qu_store_guard[0] = s_st.c
        xr_store_guard[0] = s_st.c

        for (wt, dst, act, scale) in ((c["WkT"], k_tok, AF.Relu, 0.125),
                                      (c["WvT"], v_tok, AF.Identity, 1.0)):
            for t in range(NTILE):
                for n in range(2):
                    bank = nproj % 3
                    wlist = [bank_guard[bank]] if bank_guard[bank] else []
                    for k in range(8):
                        ctx.emit("tensor", lambda e, wt=wt, t=t, n=n, k=k, bank=bank: e.matmul(
                            ps_proj[bank][:, :], xnb_feat[:, k, t * 128:(t + 1) * 128],
                            wt[:, k, n * 512:(n + 1) * 512], start=(k == 0), stop=(k == 7)),
                            waits=wlist if k == 0 else [], inc=(s_t if k == 7 else None))
                    nproj += 1
                    ctx.emit("scalar", lambda e, dst=dst, t=t, n=n, act=act, scale=scale, bank=bank: e.activation(
                        dst[:, t, n * 512:(n + 1) * 512], ps_proj[bank][:, :], act, scale=scale),
                        waits=[s_t.at()], inc=s_s)
                    bank_guard[bank] = (s_s, s_s.c)
        wkv = [s_s.at()]

        for h in range(H0):
            po = (h % 2) * 64
            fo = (h // 2) * 64
            for t in range(NTILE):
                wl = list(wkv) if (h == 0 and t == 0) else []
                if h == 0 and t == 0 and kv_evac_guard[0] > 0:
                    wl.append((s_v, kv_evac_guard[0]))
                ctx.emit("tensor", lambda e, t=t, h=h, po=po, fo=fo: e.matmul(
                    kv_ps[po:po + 64, fo:fo + 64], k_tok[:, t, h * 64:(h + 1) * 64],
                    v_tok[:, t, h * 64:(h + 1) * 64], start=(t == 0), stop=(t == NTILE - 1)),
                    waits=wl,
                    inc=(s_t if (t == NTILE - 1 and h == H0 - 1) else None))
        ctx.emit("vector", lambda e: e.tensor_tensor(kv_acc[:, :], kv_acc[:, :], kv_ps[:, :], ALU.add),
                 waits=[s_t.at()])
        kv_evac_guard[0] = ctx.sems["s_v"].c

        for t in range(NTILE):
            for k in range(8):
                twl = [(s_v, s_v.c)]
                if k == 0 and gate_guard[0] > 0:
                    twl.append((s_s, gate_guard[0]))
                ctx.emit("tensor", lambda e, t=t, k=k, xnt32=xnt32: e.transpose(
                    ps_misc[:, 0:128], xnt32[:, t, k * 128:(k + 1) * 128],
                    c["identF"][:, :]), waits=twl, inc=s_t)
                ctx.emit("vector", lambda e, k=k: e.tensor_copy(
                    xnf32[:, k, :], ps_misc[:, 0:128]), waits=[s_t.at()])
            wtr32 = [s_v.at()]
            for k in range(8):
                gwl = list(wtr32) if k == 0 else []
                if k == 0 and gate_guard[0] > 0:
                    gwl.append((s_s, gate_guard[0]))
                ctx.emit("tensor", lambda e, t=t, k=k: e.matmul(
                    ps_misc[:, 384 + 16 * t:384 + 16 * t + 14], xnf32[:, k, :],
                    c["router_sb"][:, k, :], start=(k == 0), stop=(k == 7)),
                    waits=gwl, inc=(s_t if k == 7 else None))
            gw = [s_t.at()]
            ctx.emit("vector", lambda e, t=t: e.tensor_reduce(
                gstat[:, 0:1], ps_misc[:, 384 + 16 * t:384 + 16 * t + 14], AX.X, ALU.max,
                negate=True), waits=gw)
            ctx.emit("scalar", lambda e, t=t: e.activation(
                gexp[:, 0:14], ps_misc[:, 384 + 16 * t:384 + 16 * t + 14], AF.Exp,
                bias=gstat[:, 0:1], accum_out=gstat[:, 1:2]), waits=[s_v.at()])
            gate_guard[0] = s_s.c
            ws = [s_s.at()]
            ctx.emit("vector", lambda e: e.tensor_reduce(gstat[:, 2:3], gexp[:, 0:14], AX.X, ALU.max), waits=ws)
            ctx.emit("vector", lambda e: e.tensor_scalar(gtmp[:, 0:14], gexp[:, 0:14], gstat[:, 2:3], None, ALU.is_lt))
            ctx.emit("vector", lambda e: e.tensor_tensor(gtmp[:, 0:14], gtmp[:, 0:14], gexp[:, 0:14], ALU.mult))
            ctx.emit("vector", lambda e: e.tensor_reduce(gstat[:, 2:3], gtmp[:, 0:14], AX.X, ALU.max))
            ctx.emit("vector", lambda e: e.tensor_scalar(gkeep[:, 0:14], gexp[:, 0:14], gstat[:, 2:3], None, ALU.is_ge))
            ctx.emit("vector", lambda e: e.tensor_tensor(gkeep[:, 0:14], gkeep[:, 0:14], gexp[:, 0:14], ALU.mult))
            ctx.emit("vector", lambda e: e.tensor_reduce(gstat[:, 3:4], gkeep[:, 0:14], AX.X, ALU.add))
            ctx.emit("vector", lambda e: e.scalar_tensor_tensor(
                gstat[:, 3:4], gstat[:, 1:2], MOE_EPS, gstat[:, 3:4], ALU.mult, ALU.add))
            ctx.emit("vector", lambda e: e.reciprocal(gstat[:, 3:4], gstat[:, 3:4]))
            ctx.emit("vector", lambda e, ch=ch, t=t: e.tensor_scalar(
                w_tok[:, ch * 4 + t, :], gkeep[:, 0:14], gstat[:, 3:4], None, ALU.mult))
        xnt_guard[0] = s_t.c
        feat_guard[0] = s_t.c

    ctx.barrier()
    es.close()


def phase_l0_b(ctx):
    c = ctx.consts
    s_lda, s_st = ctx.sems["s_lda"], ctx.sems["s_st"]
    s_v, s_s, s_t = ctx.sems["s_v"], ctx.sems["s_s"], ctx.sems["s_t"]
    es = contextlib.ExitStack()
    kv_b = ctx.sb(es, "b_kv_b", [128, 512], BF16)
    q_sb = ctx.sb(es, "b_q_sb", [128, 8, 512], BF16)
    u_sb = ctx.sb(es, "b_u_sb", [128, 8, 512], BF16)
    att_sb = ctx.sb(es, "b_att_sb", [128, 8, 512], BF16)
    sq_sb = ctx.sb(es, "b_sq_sb", [128, 8, 512], BF16)
    fsb = ctx.sb(es, "b_fsb", [16, 512], F32)
    fsb_b = ctx.sb(es, "b_fsb_b", [16, 512], BF16)
    tmp = ctx.sb(es, "b_tmp", [128, 512], F32)
    attg = ctx.sb(es, "b_attg", [128, 8, 512], BF16)
    s_stage = ctx.sb(es, "b_s_stage", [128, 8, 512], BF16)
    ps_att = [ctx.ps(es, "b_ps_attA", [128, 512], F32), ctx.ps(es, "b_ps_attB", [128, 512], F32)]
    ps_wo = [ctx.ps(es, "b_ps_woA", [128, 512], F32), ctx.ps(es, "b_ps_woB", [128, 512], F32)]
    ps_norm = ctx.ps(es, "b_ps_norm", [16, 512], F32)
    ps_fb = ctx.ps(es, "b_ps_fb", [128, 512], F32)

    ctx.emit("vector", lambda e: e.tensor_copy(kv_b[:, :], c["kv_acc"][:, :]))
    wkvb = [s_v.at()]
    att_bank_guard = [0, 0]
    wo_bank_guard = [0, 0]
    norm_guard = [0]
    attg_guard = [0]
    s_store_guard = [0]

    for ch in range(NCHUNK):
        for (dst, nm) in ((q_sb, "q_feat"), (u_sb, "u_feat")):
            ctx.emit("sync", lambda e, dst=dst, nm=nm, ch=ch: e.dma_start(
                dst[:, :, :], ctx.dram[nm].ap()[:, :, ch * 512:(ch + 1) * 512]),
                waits=[s_t.at(), s_v.at(), s_s.at()], inc=s_lda, inc_n=16)
        wld = [s_lda.at()]
        for hp in range(8):
            bank = hp % 2
            for hh in range(2):
                h = 2 * hp + hh
                po, fo = (h % 2) * 64, (h // 2)
                wl = list(wld) + list(wkvb) if (hp == 0 and hh == 0) else []
                if att_bank_guard[bank] > 0 and hh == 0:
                    wl.append((s_s, att_bank_guard[bank]))
                ctx.emit("tensor", lambda e, h=h, po=po, fo=fo, bank=bank: e.matmul(
                    ps_att[bank][po:po + 64, :], kv_b[po:po + 64, fo * 64:fo * 64 + 64],
                    q_sb[po:po + 64, fo, :], start=True, stop=True),
                    waits=wl, inc=(s_t if hh == 1 else None))
            ctx.emit("scalar", lambda e, hp=hp, bank=bank: e.activation(
                att_sb[:, hp, :], ps_att[bank][:, :], AF.Identity), waits=[s_t.at()])
            ctx.emit("scalar", lambda e, hp=hp, bank=bank: e.activation(
                sq_sb[:, hp, :], ps_att[bank][:, :], AF.Square, accum_out=None))
            att_bank_guard[bank] = s_s.c
        wsq = [s_s.at()]
        for ec in range(8):
            wl = list(wsq) if ec == 0 else []
            if ec == 0 and norm_guard[0] > 0:
                wl.append((s_v, norm_guard[0]))
            ctx.emit("tensor", lambda e, ec=ec: e.matmul(
                ps_norm[:, :], c["blk8"][:, ec, :], sq_sb[:, ec, :],
                start=(ec == 0), stop=(ec == 7)), waits=wl,
                inc=(s_t if ec == 7 else None))
        ctx.emit("vector", lambda e: e.tensor_single_scalar(fsb[:, :], ps_norm[:, :], 1e-24, ALU.max),
                 waits=[s_t.at()])
        norm_guard[0] = s_v.c
        ctx.emit("scalar", lambda e: e.activation(fsb[:, :], fsb[:, :], AF.Ln), waits=[s_v.at()])
        ctx.emit("scalar", lambda e: e.activation(fsb_b[:, :], fsb[:, :], AF.Exp, scale=-0.5))
        wfac = [s_s.at()]
        for ec in range(8):
            wl = list(wfac) if ec == 0 else [(s_v, s_v.c)]
            ctx.emit("tensor", lambda e, ec=ec: e.matmul(
                ps_fb[:, :], c["sel8"][:, ec, :], fsb_b[:, :], start=True, stop=True),
                waits=wl, inc=s_t)
            tw = [s_t.at()]
            if ec == 0 and attg_guard[0] > 0:
                tw.append((s_t, attg_guard[0]))
            ctx.emit("vector", lambda e, ec=ec: e.tensor_tensor(
                tmp[:, :], att_sb[:, ec, :], ps_fb[:, :], ALU.mult), waits=tw)
            ctx.emit("vector", lambda e, ec=ec: e.tensor_tensor(
                attg[:, ec, :], tmp[:, :], u_sb[:, ec, :], ALU.mult))
        wattg = [s_v.at()]
        for m in range(8):
            bank = m % 2
            wl = list(wattg) if m == 0 else []
            if wo_bank_guard[bank] > 0:
                wl.append((s_s, wo_bank_guard[bank]))
            for k in range(8):
                ctx.emit("tensor", lambda e, m=m, k=k, bank=bank: e.matmul(
                    ps_wo[bank][:, :], c["WoT"][:, k, m * 128:(m + 1) * 128], attg[:, k, :],
                    start=(k == 0), stop=(k == 7)),
                    waits=wl if k == 0 else [], inc=(s_t if k == 7 else None))
            swl = [s_t.at()]
            if m == 0 and s_store_guard[0] > 0:
                swl.append((s_st, s_store_guard[0]))
            ctx.emit("scalar", lambda e, m=m, bank=bank: e.activation(
                s_stage[:, m, :], ps_wo[bank][:, :], AF.Identity), waits=swl)
            wo_bank_guard[bank] = s_s.c
        attg_guard[0] = s_t.c
        ctx.emit("sync", lambda e, ch=ch: e.dma_start(
            ctx.dram["s_feat"].ap()[:, :, ch * 512:(ch + 1) * 512], s_stage[:, :, :]),
            waits=[s_s.at()], inc=s_st, inc_n=16)
        s_store_guard[0] = s_st.c

    ctx.barrier()
    es.close()
    ctx.scope["ab"].close()


def phase_l0_moe(ctx):
    c = ctx.consts
    s_lda, s_ldw, s_st = ctx.sems["s_lda"], ctx.sems["s_ldw"], ctx.sems["s_st"]
    s_v, s_s, s_t = ctx.sems["s_v"], ctx.sems["s_s"], ctx.sems["s_t"]
    es = contextlib.ExitStack()
    NQ, QT = 4, 1024
    wbuf = [ctx.sb(es, f"m_wbuf{i}", [128, 16384], BF16) for i in range(3)]
    xrb = ctx.sb(es, "m_xrb", [128, 8, QT], BF16)
    hb = ctx.sb(es, "m_hb", [128, 16, 512], BF16)
    g_acc = ctx.sb(es, "m_gacc", [128, 8, QT], F32)
    g_stage = hb[:, :, :].rearrange("p c n -> p (c n)").rearrange("p (c n) -> p c n", c=8)
    wb_sb = ctx.sb(es, "m_wb", [128, 2, 512], BF16)
    tmp2 = ctx.sb(es, "m_tmp2", [128, 512], F32)
    wT_sb = ctx.sb(es, "m_wT", [16, T], BF16)
    wflat = ctx.sb(es, "m_wflat", [1, E_ROUTED * QT], BF16)
    ps_h = [ctx.ps(es, f"m_ps_h{i}", [128, 512], F32) for i in range(3)]
    ps_o = [ctx.ps(es, f"m_ps_o{i}", [128, 512], F32) for i in range(2)]
    ps_wb = ctx.ps(es, "m_ps_wb", [128, 512], F32)

    def w1view(i):
        return wbuf[i][:, :].rearrange("p (c n) -> p c n", c=8)

    def w2view(i):
        return wbuf[i][:, :].rearrange("p (c n) -> p c n", c=16)

    for g in range(32):
        ctx.emit("tensor", lambda e, g=g: e.transpose(
            ps_wb[0:14, 0:128], c["w_tok"][:, g, 0:14], c["identF"][:, :]),
            waits=[(s_v, s_v.c)], inc=s_t)
        ctx.emit("vector", lambda e, g=g: e.tensor_copy(
            wT_sb[0:14, g * 128:(g + 1) * 128], ps_wb[0:14, 0:128]), waits=[s_t.at()])
    wT_done = s_v.at()

    buf_guard = [0, 0, 0]
    bank_guard3 = [0, 0, 0]
    out_bank_guard = [0, 0]
    hb_guard = [0]
    wb_guard = [0, 0]
    gst_guard = [0]
    w1_free = [0]

    def w1src(e):
        if e < E_ROUTED:
            return ctx.dram["L0_ffn_w1"].ap()[e]
        return ctx.dram["L0_ffn_w1_s"].ap()[e - E_ROUTED]

    def w2src(e):
        if e < E_ROUTED:
            return ctx.dram["L0_ffn_w2"].ap()[e]
        return ctx.dram["L0_ffn_w2_s"].ap()[e - E_ROUTED]

    NE = E_ROUTED + E_SHARED
    for q in range(NQ):
        ctx.emit("sync", lambda e, q=q: e.dma_start(
            xrb[:, :, :], ctx.dram["xr_feat"].ap()[:, :, q * QT:(q + 1) * QT]),
            waits=[s_t.at(), s_v.at(), s_s.at()], inc=s_lda, inc_n=16)
        wxr = [s_lda.at()]
        ctx.emit("vector", lambda e: e.memset(g_acc[:, :, :], 0.0), waits=[s_s.at()])
        for e_i in range(E_ROUTED):
            ctx.emit("gpsimd", lambda e, e_i=e_i, q=q: e.dma_start(
                wflat[0:1, e_i * QT:(e_i + 1) * QT], wT_sb[e_i:e_i + 1, q * QT:(q + 1) * QT]),
                waits=([wT_done, (s_t, s_t.c)] if e_i == 0 else []), inc=ctx.sems["s_wfl"], inc_n=16)
        wfl = [ctx.sems["s_wfl"].at()]
        for e_i in range(NE):
            b1i, b2i = (2 * e_i) % 3, (2 * e_i + 1) % 3
            sb1, sb2 = ctx.sems[f"s_wb{b1i}"], ctx.sems[f"s_wb{b2i}"]
            ctx.emit("gpsimd", lambda e, e_i=e_i, b1i=b1i: e.dma_start(
                w1view(b1i)[:, :, :], w1src(e_i).rearrange("(c p) h -> p c h", p=128)),
                waits=[(s_t, buf_guard[b1i])] if buf_guard[b1i] > 0 else [],
                inc=sb1, inc_n=16)
            w1ld = [sb1.at()]
            ctx.emit("gpsimd", lambda e, e_i=e_i, b2i=b2i: e.dma_start(
                w2view(b2i)[:, :, :], w2src(e_i).rearrange("(c p) h -> p c h", p=128)),
                waits=[(s_t, buf_guard[b2i])] if buf_guard[b2i] > 0 else [],
                inc=sb2, inc_n=16)
            w2ld = [sb2.at()]
            for k in range(8):
                ctx.emit("vector", lambda e, e_i=e_i, b1i=b1i, k=k: e.tensor_scalar(
                    w1view(b1i)[:, k, :], w1view(b1i)[:, k, :], c["rms_all"][:, e_i, k:k + 1],
                    None, ALU.mult), waits=w1ld if k == 0 else [])
            wsc = [s_v.at()]
            routed = e_i < E_ROUTED
            if routed:
                for n in range(2):
                    wbw = []
                    if n == 0:
                        wbw.append(ctx.sems["s_wfl"].at())
                    else:
                        wbw.append((s_v, s_v.c))
                    if wb_guard[n] > 0:
                        wbw.append((s_v, wb_guard[n]))
                    ctx.emit("tensor", lambda e, e_i=e_i, n=n: e.matmul(
                        ps_wb[:, :], c["ones_k1"][:, :],
                        wflat[0:1, e_i * QT + n * 512:e_i * QT + (n + 1) * 512],
                        start=True, stop=True), waits=wbw, inc=s_t)
                    ctx.emit("vector", lambda e, n=n: e.tensor_copy(wb_sb[:, n, :], ps_wb[:, :]),
                             waits=[s_t.at()])
            for n in range(2):
                for m in range(16):
                    bank = m % 3
                    wl = []
                    if m == 0:
                        wl += wsc + wxr
                        if hb_guard[0] > 0:
                            wl.append((s_t, hb_guard[0]))
                    if bank_guard3[bank] > 0:
                        wl.append((s_s, bank_guard3[bank]))
                    for k in range(8):
                        ctx.emit("tensor", lambda e, b1i=b1i, m=m, k=k, n=n, bank=bank: e.matmul(
                            ps_h[bank][:, :], w1view(b1i)[:, k, m * 128:(m + 1) * 128],
                            xrb[:, k, n * 512:(n + 1) * 512], start=(k == 0), stop=(k == 7)),
                            waits=wl if k == 0 else [], inc=(s_t if k == 7 else None))
                    hwl = [s_t.at()]
                    if e_i == 0 and n == 0 and m == 0 and gst_guard[0] > 0:
                        hwl.append((s_st, gst_guard[0]))
                    ctx.emit("scalar", lambda e, m=m, bank=bank: e.activation(
                        hb[:, m, :], ps_h[bank][:, :], AF.Silu), waits=hwl)
                    bank_guard3[bank] = s_s.c
                if n == 1:
                    w1_free[0] = s_t.c
                whb = [s_s.at()]
                for mo in range(8):
                    bank = mo % 2
                    wl = list(whb) + list(w2ld) if mo == 0 else []
                    if out_bank_guard[bank] > 0:
                        wl.append((s_v, out_bank_guard[bank]))
                    for kh in range(16):
                        ctx.emit("tensor", lambda e, b2i=b2i, mo=mo, kh=kh, bank=bank: e.matmul(
                            ps_o[bank][:, :], w2view(b2i)[:, kh, mo * 128:(mo + 1) * 128],
                            hb[:, kh, :], start=(kh == 0), stop=(kh == 15)),
                            waits=wl if kh == 0 else [], inc=(s_t if kh == 15 else None))
                    if routed:
                        ctx.emit("vector", lambda e, mo=mo, bank=bank, n=n: e.tensor_tensor(
                            tmp2[:, :], ps_o[bank][:, :], wb_sb[:, n, :], ALU.mult), waits=[s_t.at()])
                        ctx.emit("vector", lambda e, mo=mo, n=n: e.tensor_tensor(
                            g_acc[:, mo, n * 512:(n + 1) * 512], g_acc[:, mo, n * 512:(n + 1) * 512],
                            tmp2[:, :], ALU.add))
                    else:
                        ctx.emit("vector", lambda e, mo=mo, n=n, bank=bank: e.tensor_tensor(
                            g_acc[:, mo, n * 512:(n + 1) * 512], g_acc[:, mo, n * 512:(n + 1) * 512],
                            ps_o[bank][:, :], ALU.add), waits=[s_t.at()])
                    out_bank_guard[bank] = s_v.c
                if routed:
                    wb_guard[n] = s_v.c
                hb_guard[0] = s_t.c
            buf_guard[b1i] = w1_free[0]
            buf_guard[b2i] = s_t.c
        gswl = [s_v.at(), s_t.at()]
        if q > 0:
            gswl.append((s_st, gst_guard[0]))
        for mo in range(8):
            ctx.emit("scalar", lambda e, mo=mo: e.activation(
                g_stage[:, mo, :], g_acc[:, mo, :], AF.Silu), waits=gswl if mo == 0 else [])
        ctx.emit("sync", lambda e, q=q: e.dma_start(
            ctx.dram["g_feat"].ap()[:, :, q * QT:(q + 1) * QT], g_stage[:, :, :]),
            waits=[s_s.at()], inc=s_st, inc_n=16)
        gst_guard[0] = s_st.c

    ctx.barrier()
    es.close()


def phase_r0(ctx):
    c = ctx.consts
    s_lda, s_st = ctx.sems["s_lda"], ctx.sems["s_st"]
    s_v, s_s, s_t = ctx.sems["s_v"], ctx.sems["s_s"], ctx.sems["s_t"]
    es = contextlib.ExitStack()
    NQ, QT, NT8 = 4, 1024, 8
    x_tok = ctx.sb(es, "r_x_tok", [128, NT8, D], F32)
    s_sb = ctx.sb(es, "r_s", [128, 8, QT], BF16)
    g_sb = ctx.sb(es, "r_g", [128, 8, QT], BF16)
    delta_f = ctx.sb(es, "r_delta_f", [128, 8, QT], BF16)
    delta_t = ctx.sb(es, "r_delta_t", [128, NT8, D], BF16)
    xn1_tok = ctx.sb(es, "r_xn1_tok", [128, NT8, D], BF16)
    xn1_stage = ctx.sb(es, "r_xn1_stage", [128, 8, QT], BF16)
    sq_buf = ctx.sb(es, "r_sq", [128, D], F32)
    ssq = ctx.sb(es, "r_ssq", [128, NT8], F32)
    ssm = ctx.sb(es, "r_ssm", [128, NT8], F32)
    mean = ctx.sb(es, "r_mean", [128, NT8], F32)
    rstd = ctx.sb(es, "r_rstd", [128, NT8], F32)
    tstat = ctx.sb(es, "r_tstat", [128, NT8], F32)
    ps_tr = [ctx.ps(es, "r_ps_trA", [128, 128], BF16), ctx.ps(es, "r_ps_trB", [128, 128], BF16)]

    emit_tr = ctx.make_transposer(ps_tr, c["ident"])
    st_guard = [0]
    for q in range(NQ):
        base = q * QT
        ctx.emit("sync", lambda e, q=q: e.dma_start(
            x_tok[:, :, :], ctx.dram["L0_attn_x"].ap()[q * QT:(q + 1) * QT, :].rearrange(
                "(t p) d -> p t d", p=128)),
            waits=[s_t.at(), s_v.at(), s_s.at()] + ([(s_st, st_guard[0])] if st_guard[0] > 0 else []),
            inc=s_lda, inc_n=16)
        ctx.emit("sync", lambda e, q=q: e.dma_start(
            s_sb[:, :, :], ctx.dram["s_feat"].ap()[:, :, q * QT:(q + 1) * QT]), inc=s_lda, inc_n=16)
        ctx.emit("sync", lambda e, q=q: e.dma_start(
            g_sb[:, :, :], ctx.dram["g_feat"].ap()[:, :, q * QT:(q + 1) * QT]), inc=s_lda, inc_n=16)
        wld = [s_lda.at()]
        for kc in range(8):
            ctx.emit("vector", lambda e, kc=kc: e.tensor_tensor(
                delta_f[:, kc, :], s_sb[:, kc, :], g_sb[:, kc, :], ALU.mult),
                waits=wld if kc == 0 else [])
        wdf = [s_v.at()]
        for t in range(NT8):
            for kc in range(8):
                emit_tr(delta_f[:, kc, t * 128:(t + 1) * 128],
                        delta_t[:, t, kc * 128:(kc + 1) * 128],
                        tw=(wdf if (t == 0 and kc == 0) else []))
        for t in range(NT8):
            ctx.emit("vector", lambda e, t=t: e.tensor_tensor(
                x_tok[:, t, :], x_tok[:, t, :], delta_t[:, t, :], ALU.add))
        wx1 = [s_v.at()]
        ctx.emit("sync", lambda e, q=q: e.dma_start(
            ctx.dram["x1m"].ap()[q * QT:(q + 1) * QT, :].rearrange("(t p) d -> p t d", p=128),
            x_tok[:, :, :]), waits=wx1, inc=s_st, inc_n=16)
        for t in range(NT8):
            ctx.emit("scalar", lambda e, t=t: e.activation(
                sq_buf[:, :], x_tok[:, t, :], AF.Square, accum_out=ssq[:, t:t + 1]),
                waits=wx1 if t == 0 else [])
            ctx.emit("vector", lambda e, t=t: e.tensor_reduce(
                ssm[:, t:t + 1], x_tok[:, t, :], AX.X, ALU.add))
        ctx.emit("vector", lambda e: e.tensor_single_scalar(mean[:, :], ssm[:, :], 1.0 / D, ALU.mult),
                 waits=[s_s.at()])
        ctx.emit("vector", lambda e: e.tensor_tensor(tstat[:, :], mean[:, :], mean[:, :], ALU.mult))
        ctx.emit("vector", lambda e: e.scalar_tensor_tensor(tstat[:, :], ssq[:, :], 1.0 / D, tstat[:, :],
                                                            ALU.mult, ALU.subtract))
        ctx.emit("scalar", lambda e: e.activation(rstd[:, :], tstat[:, :], AF.Ln, bias=c["cLN"][:, :]),
                 waits=[s_v.at()])
        ctx.emit("scalar", lambda e: e.activation(rstd[:, :], rstd[:, :], AF.Exp, scale=-0.5))
        wstat = [s_s.at()]
        for t in range(NT8):
            ctx.emit("vector", lambda e, t=t: e.tensor_scalar(
                xn1_tok[:, t, :], x_tok[:, t, :], mean[:, t:t + 1], rstd[:, t:t + 1],
                ALU.subtract, ALU.mult), waits=wstat if t == 0 else [])
        for t in range(NT8):
            for kc in range(8):
                tw = [(s_v, s_v.c)] if (t == 0 and kc == 0) else []
                cw = []
                if t == 0 and kc == 0 and st_guard[0] > 0:
                    cw = [(s_st, st_guard[0])]
                emit_tr(xn1_tok[:, t, kc * 128:(kc + 1) * 128],
                        xn1_stage[:, kc, t * 128:(t + 1) * 128], tw=tw, cw=cw)
        ctx.emit("sync", lambda e, q=q: e.dma_start(
            ctx.dram["xn1_feat"].ap()[:, :, q * QT:(q + 1) * QT], xn1_stage[:, :, :]),
            waits=[s_v.at()], inc=s_st, inc_n=16)
        st_guard[0] = s_st.c

    ctx.barrier()
    es.close()




def setup_l1_masks(ctx):
    c = ctx.consts
    es = ctx.scope["de"]
    s_v, s_g = ctx.sems["s_v"], ctx.sems["s_g"]
    es_tmp = contextlib.ExitStack()
    iota_i = ctx.sb(es_tmp, "m1_iota_i", [128, 512], I32, side="right")
    iota_f = ctx.sb(es_tmp, "m1_iota_f", [128, 512], F32, side="right")
    jcol_f = ctx.sb(es_tmp, "m1_jcol_f", [128, 512], F32, side="right")
    tmpa = ctx.sb(es_tmp, "m1_tmpa", [128, 512], F32, side="right")
    tmpb = ctx.sb(es_tmp, "m1_tmpb", [128, 512], F32, side="right")
    m_std = ctx.sb(es, "m_std", [128, 512], BF16)
    m_c0 = ctx.sb(es, "m_c0", [128, 128], BF16)
    m_c1 = ctx.sb(es, "m_c1", [128, 256], BF16)
    m_c2 = ctx.sb(es, "m_c2", [128, 384], BF16)
    c["masks"] = (m_c0, m_c1, m_c2, m_std)
    m_f = ctx.sb(es_tmp, "m1_mf", [128, 512], F32, side="right")

    ctx.emit("gpsimd", lambda e: e.iota(iota_i[:, :], [[1, 512]], channel_multiplier=-1),
             waits=[s_v.at()], inc=s_g)
    ctx.emit("vector", lambda e: e.tensor_copy(iota_f[:, :], iota_i[:, :]), waits=[s_g.at()], inc=s_v)
    ctx.emit("gpsimd", lambda e: e.iota(iota_i[:, :], [[1, 512]], channel_multiplier=0),
             waits=[s_v.at()], inc=s_g)
    ctx.emit("vector", lambda e: e.tensor_copy(jcol_f[:, :], iota_i[:, :]), waits=[s_g.at()], inc=s_v)
    ctx.emit("vector", lambda e: e.tensor_single_scalar(tmpa[:, 0:384], iota_f[:, 0:384], 1.0, ALU.is_ge))
    ctx.emit("vector", lambda e: e.tensor_single_scalar(tmpb[:, 0:384], iota_f[:, 0:384], 256.0, ALU.is_le))
    ctx.emit("vector", lambda e: e.tensor_tensor(m_f[:, 0:384], tmpa[:, 0:384], tmpb[:, 0:384], ALU.mult))
    ctx.emit("vector", lambda e: e.tensor_single_scalar(m_f[:, 384:512], jcol_f[:, 384:512], 388.0, ALU.is_lt))
    ctx.emit("vector", lambda e: e.tensor_copy(m_std[:, :], m_f[:, :]), inc=s_v)
    ctx.emit("vector", lambda e: e.tensor_single_scalar(m_f[:, 0:128], iota_f[:, 0:128], 0.0, ALU.is_le))
    ctx.emit("vector", lambda e: e.tensor_copy(m_c0[:, :], m_f[:, 0:128]), inc=s_v)
    ctx.emit("vector", lambda e: e.tensor_single_scalar(m_f[:, 0:256], iota_f[:, 0:256], 128.0, ALU.is_le))
    ctx.emit("vector", lambda e: e.tensor_copy(m_c1[:, :], m_f[:, 0:256]), inc=s_v)
    ctx.emit("vector", lambda e: e.tensor_single_scalar(tmpa[:, 0:384], iota_f[:, 0:384], 256.0, ALU.is_le))
    ctx.emit("vector", lambda e: e.tensor_single_scalar(tmpb[:, 0:384], jcol_f[:, 0:384], 4.0, ALU.is_lt))
    ctx.emit("vector", lambda e: e.tensor_single_scalar(m_f[:, 0:384], iota_f[:, 0:384], 1.0, ALU.is_ge))
    ctx.emit("vector", lambda e: e.tensor_tensor(m_f[:, 0:384], m_f[:, 0:384], tmpb[:, 0:384], ALU.max))
    ctx.emit("vector", lambda e: e.tensor_tensor(m_f[:, 0:384], m_f[:, 0:384], tmpa[:, 0:384], ALU.mult))
    ctx.emit("vector", lambda e: e.tensor_copy(m_c2[:, :], m_f[:, 0:384]), inc=s_v)

    ctx.barrier()
    es_tmp.close()




def phase_l1_kv(ctx):
    c = ctx.consts
    s_lda = ctx.sems["s_lda"]
    s_v, s_s, s_t = ctx.sems["s_v"], ctx.sems["s_s"], ctx.sems["s_t"]
    es_de = ctx.scope["de"]
    K_all = ctx.sb(es_de, "e_K_all", [128, 2, T], BF16)
    V_all = ctx.sb(es_de, "e_V_all", [128, 32, 256], BF16)
    c["K_all"], c["V_all"] = K_all, V_all
    es = contextlib.ExitStack()
    xn1c = ctx.sb(es, "d_xn1c", [128, 8, 512], BF16)
    ps = [ctx.ps(es, f"d_ps{i}", [128, 512], F32) for i in range(2)]
    bank_guard = [0, 0]
    nb = 0
    for ch in range(NCHUNK):
        ctx.emit("sync", lambda e, ch=ch: e.dma_start(
            xn1c[:, :, :], ctx.dram["xn1_feat"].ap()[:, :, ch * 512:(ch + 1) * 512]),
            waits=[s_t.at()], inc=s_lda, inc_n=16)
        wld = [s_lda.at()]
        for mc in range(2):
            bank = nb % 2
            wl = list(wld) if mc == 0 else []
            if bank_guard[bank] > 0:
                wl.append((s_s, bank_guard[bank]))
            for k in range(8):
                ctx.emit("tensor", lambda e, mc=mc, k=k, bank=bank: e.matmul(
                    ps[bank][:, :], c["Wk1"][:, k, mc * 128:(mc + 1) * 128], xn1c[:, k, :],
                    start=(k == 0), stop=(k == 7)), waits=wl if k == 0 else [],
                    inc=(s_t if k == 7 else None))
            ctx.emit("scalar", lambda e, ch=ch, mc=mc, bank=bank: e.activation(
                K_all[:, mc, ch * 512:(ch + 1) * 512], ps[bank][:, :], AF.Identity),
                waits=[s_t.at()])
            bank_guard[bank] = s_s.c
            nb += 1
        for t in range(NTILE):
            bank = nb % 2
            wl = [(s_s, bank_guard[bank])] if bank_guard[bank] > 0 else []
            for k in range(8):
                ctx.emit("tensor", lambda e, t=t, k=k, bank=bank: e.matmul(
                    ps[bank][:, 0:256], xn1c[:, k, t * 128:(t + 1) * 128], c["Wv1"][:, k, 0:256],
                    start=(k == 0), stop=(k == 7)), waits=wl if k == 0 else [],
                    inc=(s_t if k == 7 else None))
            ctx.emit("scalar", lambda e, ch=ch, t=t, bank=bank: e.activation(
                V_all[:, ch * 4 + t, :], ps[bank][:, 0:256], AF.Identity), waits=[s_t.at()])
            bank_guard[bank] = s_s.c
            nb += 1
    ctx.barrier()
    es.close()


def phase_l1_attn(ctx):
    c = ctx.consts
    s_lda, s_st = ctx.sems["s_lda"], ctx.sems["s_st"]
    s_v, s_s, s_t = ctx.sems["s_v"], ctx.sems["s_s"], ctx.sems["s_t"]
    m_c0, m_c1, m_c2, m_std = c["masks"]
    K_all, V_all = c["K_all"], c["V_all"]
    es = contextlib.ExitStack()
    xn1c = ctx.sb(es, "e_xn1c", [128, 8, 512], BF16)
    Q_sb = ctx.sb(es, "e_Q", [128, 8, 512], BF16)
    e_sb = ctx.sb(es, "e_exp", [128, 2, 512], BF16)
    pn_sb = ctx.sb(es, "e_pn", [128, 2, 512], BF16)
    den = ctx.sb(es, "e_den", [128, 2], F32)
    rcp = ctx.sb(es, "e_rcp", [128, 2], F32)
    pT_sb = ctx.sb(es, "e_pT", [128, 2, 4, 128], BF16)
    Y_sb = ctx.sb(es, "e_Y", [128, 8, 128], BF16)
    s1_stage = ctx.sb(es, "e_s1", [128, 8, 512], BF16)
    ps_lg = [ctx.ps(es, "e_ps_lgA", [128, 512], F32), ctx.ps(es, "e_ps_lgB", [128, 512], F32)]
    ps_pt = [ctx.ps(es, "e_ps_ptA", [128, 512], BF16), ctx.ps(es, "e_ps_ptB", [128, 512], BF16)]
    ps_y = ctx.ps(es, "e_ps_y", [128, 128], F32)
    ps_wo = [ctx.ps(es, "e_ps_woA", [128, 512], F32), ctx.ps(es, "e_ps_woB", [128, 512], F32)]
    pt_guard = [0, 0]

    lg_guard = [None, None]
    pn_read_guard = [0, 0]
    exp_done = [0, 0]
    y_guard = [None]
    ywo_guard = [0]
    wo_guard = [None, None]
    s1_st_guard = [0]

    def head_info(hi):
        pr, pos = hi // 2, hi % 2
        grp, j = pr // 4, pr % 4
        h = grp * 8 + pos * 4 + j
        kvh = h // 4
        return dict(h=h, pr=pr, pos=pos, kvh=kvh, po=(kvh % 2) * 64,
                    slot=(h // 8) * 4 + (h % 4), kfo=kvh // 2)

    def tile_geom(g):
        i0 = g * 128
        if g == 0:
            return [(0, 0, 128)], m_c0, 128, [0]
        if g == 1:
            return [(0, 0, 256)], m_c1, 256, [0, 1]
        if g == 2:
            return [(0, 0, 384)], m_c2, 384, [0, 1, 2]
        return [(0, i0 - 256, 384), (384, 0, 128)], m_std, 512, [g - 2, g - 1, g, 0]

    for ch in range(NCHUNK):
        ctx.emit("sync", lambda e, ch=ch: e.dma_start(
            xn1c[:, :, :], ctx.dram["xn1_feat"].ap()[:, :, ch * 512:(ch + 1) * 512]),
            waits=[s_t.at(), s_v.at(), s_s.at()], inc=s_lda, inc_n=16)
        wld = [s_lda.at()]
        for m in range(8):
            bank = m % 2
            wl = list(wld) if m == 0 else []
            if lg_guard[bank] is not None:
                wl.append(lg_guard[bank])
            for k in range(8):
                ctx.emit("tensor", lambda e, m=m, k=k, bank=bank: e.matmul(
                    ps_lg[bank][:, :], c["Wq1"][:, k, m * 128:(m + 1) * 128], xn1c[:, k, :],
                    start=(k == 0), stop=(k == 7)), waits=wl if k == 0 else [],
                    inc=(s_t if k == 7 else None))
            ctx.emit("scalar", lambda e, m=m, bank=bank: e.activation(
                Q_sb[:, m, :], ps_lg[bank][:, :], AF.Identity, scale=0.125), waits=[s_t.at()])
            lg_guard[bank] = (s_s, s_s.c)
        qs_at = [s_s.at()]

        for t in range(NTILE):
            g = ch * 4 + t
            spans, mask, mw, scs = tile_geom(g)
            nsp = len(spans)
            lg_done = [0] * (N_HEADS1 + 1)

            def emit_logits(hi, first_tile):
                hd = head_info(hi)
                bank = hi % 2
                wl = []
                if lg_guard[bank] is not None:
                    wl.append(lg_guard[bank])
                if first_tile and hi < 2:
                    wl.extend(qs_at)
                for si, (co, s0, sw) in enumerate(spans):
                    ctx.emit("tensor", lambda e, t=t, hd=hd, co=co, s0=s0, sw=sw, bank=bank: e.matmul(
                        ps_lg[bank][:, co:co + sw],
                        Q_sb[hd["po"]:hd["po"] + 64, hd["slot"], t * 128:(t + 1) * 128],
                        K_all[hd["po"]:hd["po"] + 64, hd["kfo"], s0:s0 + sw],
                        start=True, stop=True),
                        waits=wl if si == 0 else [], inc=(s_t if si == nsp - 1 else None))
                lg_done[hi] = s_t.c

            emit_logits(0, t == 0)

            def emit_exp(hi):
                p, bank = hi % 2, hi % 2
                ctx.emit("scalar", lambda e, mw=mw, bank=bank, p=p: e.activation(
                    e_sb[:, p, 0:mw], ps_lg[bank][:, 0:mw], AF.Exp),
                    waits=[(s_t, lg_done[hi])])
                lg_guard[bank] = (s_s, s_s.c)
                exp_done[p] = s_s.c

            emit_exp(0)
            for hi in range(N_HEADS1):
                hd = head_info(hi)
                p = hi % 2
                ctx.emit("vector", lambda e, mask=mask, mw=mw, p=p: e.scalar_tensor_tensor(
                    e_sb[:, p, 0:mw], e_sb[:, p, 0:mw], 1.0, mask[:, 0:mw], ALU.mult, ALU.mult,
                    accum_out=den[:, p:p + 1]), waits=[(s_s, exp_done[p])])
                ctx.emit("vector", lambda e, p=p: e.reciprocal(rcp[:, p:p + 1], den[:, p:p + 1]))
                rcp_done = s_v.c
                if hi + 1 < N_HEADS1:
                    emit_logits(hi + 1, t == 0)
                    emit_exp(hi + 1)
                pw = [(s_v, rcp_done)]
                if pn_read_guard[p] > 0:
                    pw.append((s_t, pn_read_guard[p]))
                ctx.emit("scalar", lambda e, mw=mw, p=p: e.activation(
                    pn_sb[:, p, 0:mw], e_sb[:, p, 0:mw], AF.Identity, scale=rcp[:, p:p + 1]),
                    waits=pw)
                pn_ready = s_s.c
                for j in range(len(scs)):
                    twl = []
                    if j == 0:
                        twl.append((s_s, pn_ready))
                        if pt_guard[p] > 0:
                            twl.append((s_v, pt_guard[p]))
                    ctx.emit("tensor", lambda e, p=p, j=j: e.transpose(
                        ps_pt[p][:, j * 128:(j + 1) * 128], pn_sb[:, p, j * 128:(j + 1) * 128],
                        c["ident"][:, :]), waits=twl, inc=(s_t if j == len(scs) - 1 else None))
                ctx.emit("vector", lambda e, p=p, ns=len(scs): e.tensor_copy(
                    pT_sb[:, p, 0:ns, :], ps_pt[p][:, 0:ns * 128].rearrange(
                        "q (c n) -> q c n", n=128)), waits=[s_t.at()])
                pt_guard[p] = s_v.c
                pn_read_guard[p] = s_t.c
                wl2 = [(s_v, s_v.c)]
                if hd["pos"] == 0 and y_guard[0] is not None:
                    wl2.append(y_guard[0])
                for j, sc in enumerate(scs):
                    ctx.emit("tensor", lambda e, sc=sc, j=j, hd=hd, p=p,
                             last=(j == len(scs) - 1): e.matmul(
                        ps_y[hd["po"]:hd["po"] + 64, :],
                        V_all[:, sc, hd["kvh"] * 64:(hd["kvh"] + 1) * 64],
                        pT_sb[:, p, j, :], start=(j == 0), stop=last),
                        waits=wl2 if j == 0 else [], inc=(s_t if j == len(scs) - 1 else None))
                if hd["pos"] == 1:
                    ywl = [s_t.at()]
                    if ywo_guard[0] > 0 and hi == 1:
                        ywl.append((s_t, ywo_guard[0]))
                    ctx.emit("scalar", lambda e, pr=hd["pr"]: e.activation(
                        Y_sb[:, pr, :], ps_y[:, :], AF.Identity), waits=ywl)
                    y_guard[0] = (s_s, s_s.c)
            wl3 = [s_s.at()]
            for half in range(2):
                bank = half
                wlh = list(wl3) if half == 0 else []
                if wo_guard[bank] is not None:
                    wlh.append(wo_guard[bank])
                for mq in range(4):
                    mo = half * 4 + mq
                    for kh in range(8):
                        ctx.emit("tensor", lambda e, mo=mo, mq=mq, kh=kh, bank=bank: e.matmul(
                            ps_wo[bank][:, mq * 128:(mq + 1) * 128],
                            c["Wo1"][:, kh, mo * 128:(mo + 1) * 128], Y_sb[:, kh, :],
                            start=(kh == 0), stop=(kh == 7)),
                            waits=wlh if (mq == 0 and kh == 0) else [],
                            inc=(s_t if (mq == 3 and kh == 7) else None))
                ewl = [s_t.at()]
                if t == 0 and half == 0 and s1_st_guard[0] > 0:
                    ewl.append((s_st, s1_st_guard[0]))
                ctx.emit("scalar", lambda e, half=half, t=t, bank=bank: e.activation(
                    s1_stage[:, half * 4:half * 4 + 4, t * 128:(t + 1) * 128],
                    ps_wo[bank][:, :].rearrange("p (c n) -> p c n", c=4), AF.Identity),
                    waits=ewl)
                wo_guard[bank] = (s_s, s_s.c)
            ywo_guard[0] = s_t.c
        ctx.emit("sync", lambda e, ch=ch: e.dma_start(
            ctx.dram["s1_feat"].ap()[:, :, ch * 512:(ch + 1) * 512], s1_stage[:, :, :]),
            waits=[s_s.at()], inc=s_st, inc_n=16)
        s1_st_guard[0] = s_st.c
    ctx.barrier()
    es.close()
    ctx.scope["de"].close()


def phase_l1_ffn(ctx):
    c = ctx.consts
    s_lda, s_st = ctx.sems["s_lda"], ctx.sems["s_st"]
    s_v, s_s, s_t = ctx.sems["s_v"], ctx.sems["s_s"], ctx.sems["s_t"]
    s_g = ctx.sems["s_g"]
    sw0, sw1 = ctx.sems["s_wb0"], ctx.sems["s_wb1"]
    es = contextlib.ExitStack()
    W1b = ctx.sb(es, "f_W1b", [128, 8, HID1], BF16)
    W2b = ctx.sb(es, "f_W2b", [128, 32, D], BF16)
    xn1c = ctx.sb(es, "f_xn1c", [128, 8, 512], BF16)
    hbq = ctx.sb(es, "f_hbq", [128, 16, 512], BF16)
    g1_sb = ctx.sb(es, "f_g1", [128, 8, 512], F32)
    g1b = ctx.sb(es, "f_g1b", [128, 8, 512], BF16)
    hq = ctx.sb(es, "f_hq", [128, 2, 512], F32)
    vv = ctx.sb(es, "f_vv", [128, 2, 512], F32)
    va = ctx.sb(es, "f_va", [128, 2, 512], F32)
    sn = ctx.sb(es, "f_sn", [128, 2, 512], F32)
    ps_h = [ctx.ps(es, f"f_ps_h{i}", [128, 512], F32) for i in range(3)]
    ps_o = [ctx.ps(es, f"f_ps_o{i}", [128, 512], F32) for i in range(2)]

    ctx.emit("gpsimd", lambda e: e.dma_start(
        W1b[:, :, :], rearr_feat_w(ctx.dram["L1_ffn_W1"].ap())), inc=sw0, inc_n=16)
    ctx.emit("gpsimd", lambda e: e.dma_start(
        W2b[:, :, :], rearr_feat_w(ctx.dram["L1_ffn_W2"].ap())), inc=sw1, inc_n=16)
    w1w, w2w = [sw0.at()], [sw1.at()]

    hband_guard = [None, None, None]
    oband_guard = [None, None]
    g1_st_guard = [0]
    hq_guard = [None, None]
    fold_done = [0, 0]
    sin_done = [0, 0]
    hbq_guard = [0]
    for ch in range(NCHUNK):
        ctx.emit("sync", lambda e, ch=ch: e.dma_start(
            xn1c[:, :, :], ctx.dram["xn1_feat"].ap()[:, :, ch * 512:(ch + 1) * 512]),
            waits=[s_t.at(), s_v.at(), s_s.at()], inc=s_lda, inc_n=16)
        wld = [s_lda.at()]
        for half in range(2):
            def emit_sin(mm):
                p2 = (half * 16 + mm) % 2
                ctx.emit("scalar", lambda e, p2=p2: e.activation(
                    sn[:, p2, :], vv[:, p2, :], AF.Sin, scale=TWOPI), waits=[(s_v, fold_done[p2])])
                sin_done[p2] = s_s.c

            def emit_tail(mm):
                p2 = (half * 16 + mm) % 2
                mg2 = half * 16 + mm
                ctx.emit("vector", lambda e, mg2=mg2, p2=p2: e.tensor_scalar(
                    sn[:, p2, :], sn[:, p2, :], c["Cv"][:, mg2:mg2 + 1], None, ALU.mult),
                    waits=[(s_s, sin_done[p2])])
                ctx.emit("vector", lambda e, p2=p2: e.tensor_tensor(
                    sn[:, p2, :], hq[:, p2, :], sn[:, p2, :], ALU.add))
                hwl = []
                if mm == 0 and hbq_guard[0] > 0:
                    hwl.append((s_t, hbq_guard[0]))
                ctx.emit("vector", lambda e, mm=mm, p2=p2: e.tensor_single_scalar(
                    hbq[:, mm, :], sn[:, p2, :], 0.0, ALU.max), waits=hwl)
                hq_guard[p2] = (s_v, s_v.c)

            for m in range(16):
                mg = half * 16 + m
                p = mg % 2
                bank = m % 3
                wl = list(wld) + w1w if (half == 0 and m == 0) else []
                if hband_guard[bank] is not None:
                    wl.append(hband_guard[bank])
                for k in range(8):
                    ctx.emit("tensor", lambda e, m=m, mg=mg, k=k, bank=bank: e.matmul(
                        ps_h[bank][:, :], W1b[:, k, mg * 128:(mg + 1) * 128], xn1c[:, k, :],
                        start=(k == 0), stop=(k == 7)), waits=wl if k == 0 else [],
                        inc=(s_t if k == 7 else None))
                hqw = [s_t.at()]
                if hq_guard[p] is not None:
                    hqw.append(hq_guard[p])
                ctx.emit("scalar", lambda e, mg=mg, bank=bank, p=p: e.activation(
                    hq[:, p, :], ps_h[bank][:, :], AF.Identity, bias=c["b1v"][:, mg:mg + 1]),
                    waits=hqw)
                ctx.emit("scalar", lambda e, mg=mg, bank=bank, p=p: e.activation(
                    vv[:, p, :], ps_h[bank][:, :], AF.Identity, scale=0.1 * INV2PI,
                    bias=c["qbias"][:, mg:mg + 1]))
                hband_guard[bank] = (s_s, s_s.c)
                ctx.emit("vector", lambda e, p=p: e.tensor_single_scalar(
                    va[:, p, :], vv[:, p, :], 0.5, ALU.is_gt), waits=[s_s.at()])
                ctx.emit("vector", lambda e, p=p: e.scalar_tensor_tensor(
                    vv[:, p, :], va[:, p, :], -1.0, vv[:, p, :], ALU.mult, ALU.add))
                ctx.emit("vector", lambda e, p=p: e.tensor_single_scalar(
                    va[:, p, :], vv[:, p, :], -0.5, ALU.is_lt))
                ctx.emit("vector", lambda e, p=p: e.scalar_tensor_tensor(
                    vv[:, p, :], va[:, p, :], 1.0, vv[:, p, :], ALU.mult, ALU.add))
                fold_done[p] = s_v.c
                if m > 0:
                    emit_sin(m - 1)
                    emit_tail(m - 1)
            emit_sin(15)
            emit_tail(15)
            whb = [s_v.at()]
            for mo in range(8):
                bank = mo % 2
                wl = list(whb) + w2w if mo == 0 else []
                if oband_guard[bank] is not None:
                    wl.append(oband_guard[bank])
                for kh in range(16):
                    khg = half * 16 + kh
                    ctx.emit("tensor", lambda e, mo=mo, kh=kh, khg=khg, bank=bank: e.matmul(
                        ps_o[bank][:, :], W2b[:, khg, mo * 128:(mo + 1) * 128], hbq[:, kh, :],
                        start=(kh == 0), stop=(kh == 15)), waits=wl if kh == 0 else [],
                        inc=(s_t if kh == 15 else None))
                if half == 0:
                    ctx.emit("scalar", lambda e, mo=mo, bank=bank: e.activation(
                        g1_sb[:, mo, :], ps_o[bank][:, :], AF.Identity,
                        bias=c["b2v"][:, mo:mo + 1]), waits=[s_t.at()])
                    oband_guard[bank] = (s_s, s_s.c)
                else:
                    ctx.emit("vector", lambda e, mo=mo, bank=bank: e.tensor_tensor(
                        g1_sb[:, mo, :], g1_sb[:, mo, :], ps_o[bank][:, :], ALU.add),
                        waits=[s_t.at()])
                    oband_guard[bank] = (s_v, s_v.c)
            hbq_guard[0] = s_t.c
        gwl = [s_v.at()]
        if g1_st_guard[0] > 0:
            gwl.append((s_st, g1_st_guard[0]))
        for mo in range(8):
            ctx.emit("scalar", lambda e, mo=mo: e.activation(
                g1b[:, mo, :], g1_sb[:, mo, :], AF.Silu), waits=gwl if mo == 0 else [])
        ctx.emit("sync", lambda e, ch=ch: e.dma_start(
            ctx.dram["g1_feat"].ap()[:, :, ch * 512:(ch + 1) * 512], g1b[:, :, :]),
            waits=[s_s.at()], inc=s_st, inc_n=16)
        g1_st_guard[0] = s_st.c
    ctx.barrier()
    es.close()


def phase_final(ctx):
    c = ctx.consts
    s_lda, s_st = ctx.sems["s_lda"], ctx.sems["s_st"]
    s_v, s_s, s_t = ctx.sems["s_v"], ctx.sems["s_s"], ctx.sems["s_t"]
    es = contextlib.ExitStack()
    NQ, QT, NT8 = 4, 1024, 8
    x_tok = ctx.sb(es, "o_x_tok", [128, NT8, D], F32)
    s_sb = ctx.sb(es, "o_s", [128, 8, QT], BF16)
    g_sb = ctx.sb(es, "o_g", [128, 8, QT], BF16)
    delta_f = ctx.sb(es, "o_delta_f", [128, 8, QT], BF16)
    delta_t = ctx.sb(es, "o_delta_t", [128, NT8, D], BF16)
    ps_tr = [ctx.ps(es, "o_ps_trA", [128, 128], BF16), ctx.ps(es, "o_ps_trB", [128, 128], BF16)]
    emit_tr = ctx.make_transposer(ps_tr, c["ident"])
    st_guard = [0]
    for q in range(NQ):
        ctx.emit("sync", lambda e, q=q: e.dma_start(
            x_tok[:, :, :], ctx.dram["x1m"].ap()[q * QT:(q + 1) * QT, :].rearrange(
                "(t p) d -> p t d", p=128)),
            waits=[s_t.at(), s_v.at()] + ([(s_st, st_guard[0])] if st_guard[0] > 0 else []),
            inc=s_lda, inc_n=16)
        ctx.emit("sync", lambda e, q=q: e.dma_start(
            s_sb[:, :, :], ctx.dram["s1_feat"].ap()[:, :, q * QT:(q + 1) * QT]), inc=s_lda, inc_n=16)
        ctx.emit("sync", lambda e, q=q: e.dma_start(
            g_sb[:, :, :], ctx.dram["g1_feat"].ap()[:, :, q * QT:(q + 1) * QT]), inc=s_lda, inc_n=16)
        wld = [s_lda.at()]
        for kc in range(8):
            ctx.emit("vector", lambda e, kc=kc: e.tensor_tensor(
                delta_f[:, kc, :], s_sb[:, kc, :], g_sb[:, kc, :], ALU.mult),
                waits=wld if kc == 0 else [])
        wdf = [s_v.at()]
        for t in range(NT8):
            for kc in range(8):
                emit_tr(delta_f[:, kc, t * 128:(t + 1) * 128],
                        delta_t[:, t, kc * 128:(kc + 1) * 128],
                        tw=(wdf if (t == 0 and kc == 0) else []))
        for t in range(NT8):
            ctx.emit("vector", lambda e, t=t: e.tensor_tensor(
                x_tok[:, t, :], x_tok[:, t, :], delta_t[:, t, :], ALU.add))
        ctx.emit("sync", lambda e, q=q: e.dma_start(
            ctx.dram["out"].ap()[q * QT:(q + 1) * QT, :].rearrange("(t p) d -> p t d", p=128),
            x_tok[:, :, :]), waits=[s_v.at()], inc=s_st, inc_n=16)
        st_guard[0] = s_st.c
    ctx.barrier()
    es.close()


def phase_dbg_a(ctx):
    s_v, s_st = ctx.sems["s_v"], ctx.sems["s_st"]
    es = contextlib.ExitStack()
    ctx.emit("sync", lambda e: e.dma_start(ctx.dram["kv_dbg"].ap(), ctx.consts["kv_acc"][:, :]),
             inc=s_st, inc_n=16)
    ctx.emit("sync", lambda e: e.dma_start(ctx.dram["wtok_dbg"].ap(), ctx.consts["w_tok"][:, :, :]),
             inc=s_st, inc_n=16)
    ctx.barrier()
    es.close()


DRAM_SPECS = {
    "kv_dbg": ([128, 512], F32),
    "xr_feat": ([128, 8, T], BF16),
    "q_feat": ([128, 8, T], BF16),
    "u_feat": ([128, 8, T], BF16),
    "s_feat": ([128, 8, T], BF16),
    "x1m": ([T, D], F32),
    "g_feat": ([128, 8, T], BF16),
    "xn1_feat": ([128, 8, T], BF16),
    "s1_feat": ([128, 8, T], BF16),
    "g1_feat": ([128, 8, T], BF16),
    "wtok_dbg": ([128, 32, 14], F32),
}

INPUT_SPECS = {
    "L0_attn_x": ([T, D], F32),
    "L0_attn_Wq": ([D, D], F32), "L0_attn_Wk": ([D, D], F32), "L0_attn_Wv": ([D, D], F32),
    "L0_attn_Wu": ([D, D], F32), "L0_attn_Wo": ([D, D], F32),
    "L0_ffn_router_w": ([D, E_ROUTED], F32),
    "L0_ffn_rms_w": ([E_ROUTED, D], F32),
    "L0_ffn_w1": ([E_ROUTED, D, HID0], F32), "L0_ffn_w2": ([E_ROUTED, HID0, D], F32),
    "L0_ffn_rms_w_s": ([E_SHARED, D], F32),
    "L0_ffn_w1_s": ([E_SHARED, D, HID0], F32), "L0_ffn_w2_s": ([E_SHARED, HID0, D], F32),
    "L1_attn_Wq": ([D, D], F32), "L1_attn_Wk": ([D, 256], F32), "L1_attn_Wv": ([D, 256], F32),
    "L1_attn_Wo": ([D, D], F32),
    "L1_ffn_W1": ([D, HID1], F32), "L1_ffn_b1": ([HID1], F32),
    "L1_ffn_quantum_phases": ([HID1], F32),
    "L1_ffn_quantum_weights_real": ([HID1], F32),
    "L1_ffn_quantum_weights_imag": ([HID1], F32),
    "L1_ffn_W2": ([HID1, D], F32), "L1_ffn_b2": ([D], F32),
}


def build_program(phases, expose=(), scratch_inputs=()):
    nc = bass.Bass("TRN2", target_bir_lowering=False)
    ctx = Ctx(nc)
    for name, (shape, dt) in INPUT_SPECS.items():
        ctx.dram[name] = nc.dram_tensor(name, shape, dt, kind="ExternalInput")
    ctx.dram["out"] = nc.dram_tensor("out", [T, D], F32, kind="ExternalOutput")
    for name, (shape, dt) in DRAM_SPECS.items():
        kind = ("ExternalInput" if name in scratch_inputs else
                ("ExternalOutput" if name in expose else "Internal"))
        ctx.dram[name] = nc.dram_tensor(name, shape, dt, kind=kind)
    with ctx.es_global:
        setup_consts(ctx)
        for sc in ("ab", "ac", "de"):
            ctx.scope[sc] = ctx.es_global.enter_context(contextlib.ExitStack())
        for ph in phases:
            ph(ctx)
        ctx.replay()
    return ctx


ALL_PHASES = None


def kernel(**inputs):
    x_full = np.ascontiguousarray(np.asarray(inputs["L0_attn_x"]))
    ctx = build_program([setup_l0_weights, phase_l0_a, phase_l0_b, phase_l0_moe, phase_r0,
                         setup_l1_weights, setup_l1_masks, phase_l1_kv, phase_l1_attn,
                         phase_l1_ffn, phase_final])
    weights = {k: np.ascontiguousarray(np.asarray(v)) for k, v in inputs.items()
               if k != "L0_attn_x"}
    in_maps = []
    for core in range(8):
        m = dict(weights)
        m["L0_attn_x"] = np.ascontiguousarray(x_full[core])
        in_maps.append(m)
    res = run_bass_kernel_spmd(ctx.nc, in_maps, core_ids=list(range(8)))
    return np.stack([r["out"] for r in res.results], axis=0)


# version 06
# speedup vs baseline: 1.0558x; 1.0558x over previous
import contextlib
import math
import numpy as np

import concourse.bass as bass
import concourse.mybir as mybir
from concourse.bass_utils import run_bass_kernel_spmd

F32 = mybir.dt.float32
BF16 = mybir.dt.bfloat16
I32 = mybir.dt.int32
AF = mybir.ActivationFunctionType
ALU = mybir.AluOpType
AX = mybir.AxisListType

ENGINES = ("sync", "gpsimd", "vector", "scalar", "tensor")

T = 4096
D = 1024
NCHUNK = 8
NTILE = 4
H0 = 16
HD = 64
E_ROUTED = 14
E_SHARED = 2
HID0 = 2048
RMS_EPS = 1.1920929e-07
LN_EPS = 1e-5
MOE_EPS = 1e-9
N_HEADS1 = 16
N_KV1 = 4
SINK = 4
WIN = 256
HID1 = 4096
INV2PI = 1.0 / (2.0 * math.pi)
TWOPI = 2.0 * math.pi


class Sem:
    def __init__(self, nc, es, name):
        self.h = es.enter_context(nc.semaphore(name))
        self.c = 0

    def at(self):
        return (self, self.c)


class Ctx:
    def __init__(self, nc):
        self.nc = nc
        self.steps = []
        self.es_global = contextlib.ExitStack()
        self.dram = {}
        self.nbar = 0
        self.sems = {}
        self.consts = {}
        self.scope = {}

    def sem(self, name):
        s = Sem(self.nc, self.es_global, name)
        self.sems[name] = s
        return s

    def sb(self, es, name, shape, dtype, side=None):
        return es.enter_context(self.nc.sbuf_tensor(name, shape, dtype, side=side))

    def ps(self, es, name, shape, dtype=F32, side=None):
        return es.enter_context(self.nc.psum_tensor(name, shape, dtype, side=side))

    CHAINED = {"vector": "s_v", "scalar": "s_s", "gpsimd": "s_g"}

    def emit(self, eng, fn, waits=(), inc=None, inc_n=1, chain=True):
        w = [(s.h, v) for (s, v) in waits if v > 0]
        extra = []
        if inc_n >= 16:
            chain = False
        if fn is not None and eng in self.CHAINED and chain:
            se = self.sems[self.CHAINED[eng]]
            if inc is se:
                inc = None
            assert inc is None, f"chained {eng} op cannot carry another inc"
            if se.c > 0:
                w.append((se.h, se.c))
            extra.append(se)
            se.c += 1
        self.steps.append((eng, fn, w, inc.h if inc is not None else None, inc_n, extra))
        if inc is not None:
            inc.c += inc_n

    def make_transposer(self, ps_pair, ident):
        state = {"i": 0, "guard": [0, 0]}
        s_v, s_t = self.sems["s_v"], self.sems["s_t"]

        def emit_tr(src_ap, dst_ap, tw=(), cw=()):
            par = state["i"] % 2
            state["i"] += 1
            twl = list(tw)
            if state["guard"][par] > 0:
                twl.append((s_v, state["guard"][par]))
            self.emit("tensor", lambda e, par=par, src_ap=src_ap: e.transpose(
                ps_pair[par][:, :], src_ap, ident[:, :]), waits=twl, inc=s_t)
            cwl = list(cw) + [s_t.at()]
            self.emit("vector", lambda e, par=par, dst_ap=dst_ap: e.tensor_copy(
                dst_ap, ps_pair[par][:, :]), waits=cwl)
            state["guard"][par] = s_v.c
        return emit_tr

    def barrier(self):
        bar = self.sems["bar"]
        dma_waits = [(self.sems[n], self.sems[n].c)
                     for n in ("s_ldw", "s_lda", "s_st", "s_lds", "s_wb0", "s_wb1",
                               "s_wb2", "s_wfl") if n in self.sems]
        self.nbar += 1
        tgt = 5 * self.nbar
        bps, bsb, bsb2 = self.consts["barbufs"]
        self.emit("sync", lambda e: e.nop(), waits=dma_waits, inc=bar)
        self.emit("gpsimd", lambda e: e.memset(bsb[0:1, 0:1], 0),
                  waits=dma_waits + [self.sems["s_g"].at()], inc=bar, chain=False)
        self.emit("vector", lambda e: e.memset(bsb[0:1, 1:2], 0),
                  waits=[self.sems["s_v"].at()], inc=bar, chain=False)
        self.emit("scalar", lambda e: e.copy(bsb[0:1, 2:3], bsb[0:1, 3:4]),
                  waits=[self.sems["s_s"].at(), self.sems["s_g"].at()], inc=bar, chain=False)
        self.emit("tensor", lambda e: e.matmul(bps[0:1, 0:1], bsb2[0:1, 0:1], bsb2[0:1, 0:1],
                                               start=True, stop=True),
                  waits=[self.sems["s_g"].at(), self.sems["s_v"].at(),
                         self.sems["s_s"].at()], inc=bar)
        for eng in ENGINES:
            self.emit(eng, None, waits=[(bar, tgt)])

    def replay(self):
        with self.nc.Block() as block:
            for eng_name in ENGINES:
                def make_body(eng_name=eng_name):
                    def body(eng):
                        for (e, fn, w, inch, incn, extra) in self.steps:
                            if e != eng_name:
                                continue
                            for (h, v) in w:
                                eng.wait_ge(h, v)
                            if fn is not None:
                                inst = fn(eng)
                                if inch is not None:
                                    assert inst is not None, "inc on step with no inst"
                                    inst.then_inc(inch, incn)
                                for s in extra:
                                    assert inst is not None
                                    inst.then_inc(s.h, 1)
                    return body
                getattr(block, eng_name)(make_body())


def rearr_tok(ap, ntile=NTILE):
    return ap.rearrange("(t p) d -> p t d", p=128)


def rearr_feat_w(ap):
    return ap.rearrange("(c p) n -> p c n", p=128)



def setup_consts(ctx):
    nc = ctx.nc
    es = ctx.es_global
    c = ctx.consts
    bps = ctx.ps(es, "bar_ps", [1, 1], F32)
    bsb = ctx.sb(es, "bar_sb", [1, 8], F32)
    bsb2 = ctx.sb(es, "bar_sb2", [1, 1], BF16)
    c["barbufs"] = (bps, bsb, bsb2)

    s_ldw = ctx.sem("s_ldw")
    s_lda = ctx.sem("s_lda")
    s_st = ctx.sem("s_st")
    s_lds = ctx.sem("s_lds")
    for i in range(3):
        ctx.sem(f"s_wb{i}")
    ctx.sem("s_wfl")
    ctx.sem("bar")
    s_v = ctx.sem("s_v")
    s_s = ctx.sem("s_s")
    s_t = ctx.sem("s_t")
    s_g = ctx.sem("s_g")

    ctx.emit("gpsimd", lambda e: e.memset(bsb[0:1, :], 0.0))
    ctx.emit("gpsimd", lambda e: e.memset(bsb2[0:1, :], 0.0))

    es_tmp = contextlib.ExitStack()
    ident = ctx.sb(es, "ident", [128, 128], BF16)
    iota_i = ctx.sb(es_tmp, "iota_i", [128, 512], I32, side="right")
    iota_f = ctx.sb(es_tmp, "iota_f", [128, 512], F32, side="right")
    jcol_f = ctx.sb(es_tmp, "jcol_f", [128, 512], F32, side="right")
    ones_k1 = ctx.sb(es, "ones_k1", [1, 128], BF16)
    c["ident"] = ident
    c["ones_k1"] = ones_k1

    identF = ctx.sb(es, "identF", [128, 128], F32)
    c["identF"] = identF
    ctx.emit("gpsimd", lambda e: e.iota(iota_i[:, 0:128], [[1, 128]], channel_multiplier=-1), inc=s_g)
    ctx.emit("vector", lambda e: e.tensor_single_scalar(ident[:, :], iota_i[:, 0:128], 0, ALU.is_equal),
             waits=[s_g.at()], inc=s_v)
    ctx.emit("vector", lambda e: e.tensor_copy(identF[:, :], ident[:, :]))
    ctx.emit("gpsimd", lambda e: e.memset(ones_k1[:, :], 1.0), inc=s_g)

    blk8 = ctx.sb(es, "blk8", [128, 8, 16], BF16)
    sel8 = ctx.sb(es, "sel8", [16, 8, 128], BF16)
    blkI = ctx.sb(es_tmp, "blkI", [128, 8, 16], BF16, side="right")
    c["blk8"] = blk8
    c["sel8"] = sel8
    ctx.emit("gpsimd", lambda e: e.memset(blk8[:, :, :], 0.0), inc=s_g)
    ctx.emit("gpsimd", lambda e: e.memset(blkI[:, :, :], 0.0), inc=s_g)
    wz = [s_g.at()]
    for ec in range(8):
        ctx.emit("gpsimd", lambda e, ec=ec: e.memset(blk8[0:64, ec, 2 * ec:2 * ec + 1], 1.0 / 64),
                 waits=wz if ec == 0 else [])
        ctx.emit("gpsimd", lambda e, ec=ec: e.memset(blk8[64:128, ec, 2 * ec + 1:2 * ec + 2], 1.0 / 64))
        ctx.emit("gpsimd", lambda e, ec=ec: e.memset(blkI[0:64, ec, 2 * ec:2 * ec + 1], 1.0))
        ctx.emit("gpsimd", lambda e, ec=ec: e.memset(blkI[64:128, ec, 2 * ec + 1:2 * ec + 2], 1.0), inc=s_g)
    ps_tr16 = ctx.ps(es_tmp, "ps_tr16", [16, 8, 128], BF16)
    for ec in range(8):
        ctx.emit("tensor", lambda e, ec=ec: e.transpose(ps_tr16[:, ec, :], blkI[:, ec, 0:16], ident[:, :]),
                 waits=[s_g.at(), s_v.at()], inc=s_t)
    ctx.emit("vector", lambda e: e.tensor_copy(sel8[:, :, :], ps_tr16[:, :, :]), waits=[s_t.at()], inc=s_v)

    router_sb = ctx.sb(es, "router_sb", [128, 8, 14], F32)
    c["router_sb"] = router_sb
    ctx.emit("gpsimd", lambda e: e.dma_start(router_sb[:, :, :],
                                             rearr_feat_w(ctx.dram["L0_ffn_router_w"].ap())), inc=s_ldw, inc_n=16)

    w_tok = ctx.sb(es, "w_tok", [128, 32, 14], F32)
    c["w_tok"] = w_tok
    cLN = ctx.sb(es, "cLN", [128, 1], F32)
    cRMS = ctx.sb(es, "cRMS", [128, 1], F32)
    c["cLN"], c["cRMS"] = cLN, cRMS
    ctx.emit("gpsimd", lambda e: e.memset(cLN[:, :], LN_EPS))
    ctx.emit("gpsimd", lambda e: e.memset(cRMS[:, :], RMS_EPS), inc=s_g)

    rms_all = ctx.sb(es, "rms_all", [128, 16, 8], F32)
    c["rms_all"] = rms_all
    for e_i in range(E_ROUTED):
        src = ctx.dram["L0_ffn_rms_w"].ap()[e_i:e_i + 1, :].rearrange("q (c p) -> q p c", p=128)
        ctx.emit("gpsimd", lambda e, e_i=e_i, src=src: e.dma_start(
            rms_all[:, e_i, :], src, allow_slow_non_contiguous=True), inc=s_ldw, inc_n=16)
    for e_i in range(E_SHARED):
        src = ctx.dram["L0_ffn_rms_w_s"].ap()[e_i:e_i + 1, :].rearrange("q (c p) -> q p c", p=128)
        ctx.emit("gpsimd", lambda e, e_i=e_i, src=src: e.dma_start(
            rms_all[:, E_ROUTED + e_i, :], src, allow_slow_non_contiguous=True), inc=s_ldw, inc_n=16)

    b1v = ctx.sb(es, "b1v", [128, 32], F32)
    thv = ctx.sb(es_tmp, "thv", [128, 32], F32, side="right")
    wrv = ctx.sb(es_tmp, "wrv", [128, 32], F32, side="right")
    wiv = ctx.sb(es_tmp, "wiv", [128, 32], F32, side="right")
    b2v = ctx.sb(es, "b2v", [128, 8], F32)
    qbias = ctx.sb(es, "qbias", [128, 32], F32)
    Cv = ctx.sb(es, "Cv", [128, 32], F32)
    c["b1v"] = b1v
    c["qbias"] = qbias
    c["Cv"] = Cv
    c["b2v"] = b2v
    for (nm, dst, n) in [("L1_ffn_b1", b1v, 32), ("L1_ffn_quantum_phases", thv, 32),
                         ("L1_ffn_quantum_weights_real", wrv, 32),
                         ("L1_ffn_quantum_weights_imag", wiv, 32), ("L1_ffn_b2", b2v, 8)]:
        for m in range(n):
            ctx.emit("scalar", lambda e, nm=nm, dst=dst, m=m: e.dma_start(
                dst[:, m:m + 1], ctx.dram[nm].ap()[m * 128:(m + 1) * 128].rearrange("(p q) -> p q", q=1)),
                inc=s_lds, inc_n=16)
    t1 = ctx.sb(es_tmp, "qc_t1", [128, 32], F32, side="right")
    t2 = ctx.sb(es_tmp, "qc_t2", [128, 32], F32, side="right")
    t3i = ctx.sb(es_tmp, "qc_t3i", [128, 32], I32, side="right")
    sth = ctx.sb(es_tmp, "qc_sth", [128, 32], F32, side="right")
    cth = ctx.sb(es_tmp, "qc_cth", [128, 32], F32, side="right")
    w0 = [(s_lds, s_lds.c)]

    def fold_full(buf):
        ctx.emit("vector", lambda e: e.tensor_copy(t3i[:, :], buf[:, :]))
        ctx.emit("vector", lambda e: e.tensor_copy(t2[:, :], t3i[:, :]))
        ctx.emit("vector", lambda e: e.tensor_tensor(buf[:, :], buf[:, :], t2[:, :], ALU.subtract))
        ctx.emit("vector", lambda e: e.tensor_single_scalar(t2[:, :], buf[:, :], 0.5, ALU.is_gt))
        ctx.emit("vector", lambda e: e.scalar_tensor_tensor(buf[:, :], t2[:, :], -1.0, buf[:, :], ALU.mult, ALU.add))
        ctx.emit("vector", lambda e: e.tensor_single_scalar(t2[:, :], buf[:, :], -0.5, ALU.is_lt))
        ctx.emit("vector", lambda e: e.scalar_tensor_tensor(buf[:, :], t2[:, :], 1.0, buf[:, :], ALU.mult, ALU.add))

    ctx.emit("vector", lambda e: e.scalar_tensor_tensor(qbias[:, :], b1v[:, :], 0.1, thv[:, :], ALU.mult, ALU.add), waits=w0)
    ctx.emit("vector", lambda e: e.tensor_single_scalar(qbias[:, :], qbias[:, :], INV2PI, ALU.mult))
    fold_full(qbias)
    ctx.emit("vector", lambda e: e.tensor_single_scalar(t1[:, :], thv[:, :], INV2PI, ALU.mult))
    fold_full(t1)
    ctx.emit("scalar", lambda e: e.activation(sth[:, :], t1[:, :], AF.Sin, scale=TWOPI), waits=[s_v.at()])
    ctx.emit("vector", lambda e: e.tensor_scalar(t1[:, :], thv[:, :], INV2PI, 0.25, ALU.mult, ALU.add),
             waits=[s_s.at()])
    fold_full(t1)
    ctx.emit("scalar", lambda e: e.activation(cth[:, :], t1[:, :], AF.Sin, scale=TWOPI), waits=[s_v.at()])
    ctx.emit("vector", lambda e: e.tensor_tensor(t1[:, :], cth[:, :], wrv[:, :], ALU.mult), waits=[s_s.at()])
    ctx.emit("vector", lambda e: e.tensor_tensor(t2[:, :], sth[:, :], wiv[:, :], ALU.mult))
    ctx.emit("vector", lambda e: e.tensor_tensor(t1[:, :], t1[:, :], t2[:, :], ALU.add))
    ctx.emit("vector", lambda e: e.scalar_tensor_tensor(Cv[:, :], t1[:, :], 0.1, wiv[:, :], ALU.mult, ALU.mult))

    ctx.barrier()
    es_tmp.close()


def setup_l0_weights(ctx):
    c = ctx.consts
    s_ldw = ctx.sems["s_ldw"]
    s_v, s_t = ctx.sems["s_v"], ctx.sems["s_t"]
    ident = c["ident"]
    es_tmp = contextlib.ExitStack()
    es = ctx.scope["ab"]
    wnat = ctx.sb(es_tmp, "wnat_stage", [128, 8, 1024], BF16, side="right")
    ps_trw = [ctx.ps(es_tmp, "ps_trwA", [128, 128], BF16, side="right"),
              ctx.ps(es_tmp, "ps_trwB", [128, 128], BF16, side="right")]
    names = [("L0_attn_Wq", "WqT"), ("L0_attn_Wk", "WkT"), ("L0_attn_Wv", "WvT"),
             ("L0_attn_Wu", "WuT"), ("L0_attn_Wo", "WoT")]
    ntr = 0
    ncp = 0
    for (src_name, dst_name) in names:
        dst = ctx.sb(es, dst_name, [128, 8, 1024], BF16)
        c[dst_name] = dst
        ctx.emit("gpsimd", lambda e, s=src_name: e.dma_start(
            wnat[:, :, :], rearr_feat_w(ctx.dram[s].ap())),
            waits=[s_t.at()], inc=s_ldw, inc_n=16)
        for ce in range(8):
            for kd in range(8):
                par = ntr % 2
                twl = []
                if ce == 0 and kd == 0:
                    twl.append((s_ldw, s_ldw.c))
                    twl.append((s_v, s_v.c))
                if ntr >= 2:
                    twl.append((s_v, ncp - 1))
                ctx.emit("tensor", lambda e, ce=ce, kd=kd, par=par: e.transpose(
                    ps_trw[par][:, :], wnat[:, ce, kd * 128:(kd + 1) * 128], ident[:, :]),
                    waits=twl, inc=s_t)
                ntr += 1
                ctx.emit("vector", lambda e, dst=dst, ce=ce, kd=kd, par=par: e.tensor_copy(
                    dst[:, kd, ce * 128:(ce + 1) * 128], ps_trw[par][:, :]),
                    waits=[(s_t, s_t.c)])
                ncp = s_v.c


    ctx.barrier()
    es_tmp.close()


def setup_l1_weights(ctx):
    c = ctx.consts
    es = ctx.scope["de"]
    s_ldw = ctx.sems["s_ldw"]
    for (nm, dst_name, ecols) in [("L1_attn_Wk", "Wk1", 256), ("L1_attn_Wv", "Wv1", 256)]:
        dst = ctx.sb(es, dst_name, [128, 8, ecols], BF16)
        c[dst_name] = dst
        ctx.emit("gpsimd", lambda e, nm=nm, dst=dst: e.dma_start(
            dst[:, :, :], rearr_feat_w(ctx.dram[nm].ap())), inc=s_ldw, inc_n=16)
    wq1 = ctx.sb(es, "Wq1", [128, 8, 1024], BF16)
    wo1 = ctx.sb(es, "Wo1", [128, 8, 1024], BF16)
    c["Wq1"], c["Wo1"] = wq1, wo1
    for s in range(8):
        grp, j = s // 4, s % 4
        for pos in range(2):
            h = grp * 8 + pos * 4 + j
            ctx.emit("gpsimd", lambda e, s=s, pos=pos, h=h: e.dma_start(
                wq1[:, :, s * 128 + pos * 64:s * 128 + pos * 64 + 64],
                ctx.dram["L1_attn_Wq"].ap()[:, h * 64:(h + 1) * 64].rearrange(
                    "(c p) e -> p c e", p=128)), inc=s_ldw, inc_n=16)
            ctx.emit("gpsimd", lambda e, s=s, pos=pos, h=h: e.dma_start(
                wo1[pos * 64:pos * 64 + 64, s, :],
                ctx.dram["L1_attn_Wo"].ap()[h * 64:(h + 1) * 64, :]), inc=s_ldw, inc_n=16)






def phase_l0_a(ctx):
    nc = ctx.nc
    es_ab = ctx.scope["ab"]
    c = ctx.consts
    s_lda, s_ldw, s_st = ctx.sems["s_lda"], ctx.sems["s_ldw"], ctx.sems["s_st"]
    s_v, s_s, s_t = ctx.sems["s_v"], ctx.sems["s_s"], ctx.sems["s_t"]

    kv_ps = ctx.ps(es_ab, "kv_ps", [128, 512], F32)
    kv_acc = ctx.sb(es_ab, "kv_acc", [128, 512], F32)
    ctx.consts["kv_ps"] = kv_ps
    ctx.consts["kv_acc"] = kv_acc
    es = contextlib.ExitStack()
    x_tok2 = ctx.sb(es, "a_x_tok", [128, 2, NTILE, D], F32)
    stats = ctx.sb(es, "a_stats", [128, 2, 6, NTILE], F32)
    xnt321 = ctx.sb(es, "a_xnt32", [128, 1, NTILE, D], F32)
    sq_buf = xnt321[:, 0, 0, :]
    xnf32 = ctx.sb(es, "a_xnf32", [128, 8, 128], F32)
    xnb_tok2 = ctx.sb(es, "a_xnb_tok", [128, 2, NTILE, D], BF16)
    rms4f = ctx.sb(es, "a_rms4f", [4, 128], F32)
    rmsr_b = ctx.sb(es, "a_rmsr", [1, 512], BF16)
    xnb_feat = ctx.sb(es, "a_xnb_feat", [128, 8, 512], BF16)
    xrb_feat = ctx.sb(es, "a_xrb_feat", [128, 8, 512], BF16)
    k_tok = ctx.sb(es, "a_k_tok", [128, NTILE, D], BF16)
    v_tok = ctx.sb(es, "a_v_tok", [128, NTILE, D], BF16)
    q_stage = ctx.sb(es, "a_q_stage", [128, 8, 512], BF16)
    u_stage = ctx.sb(es, "a_u_stage", [128, 8, 512], BF16)
    gexp = ctx.sb(es, "a_gexp", [128, 16], F32)
    gtmp = ctx.sb(es, "a_gtmp", [128, 16], F32)
    gkeep = ctx.sb(es, "a_gkeep", [128, 16], F32)
    gstat = ctx.sb(es, "a_gstat", [128, 4], F32)
    w_tok = ctx.consts["w_tok"]

    ps_tr = [ctx.ps(es, "a_ps_trA", [128, 128], BF16),
             ctx.ps(es, "a_ps_trB", [128, 128], BF16)]
    ps_misc = ctx.ps(es, "a_ps_misc", [128, 512], F32)
    ps_proj = [ctx.ps(es, f"a_ps_proj{i}", [128, 512], F32) for i in range(3)]

    ident = c["ident"]
    emit_tr = ctx.make_transposer(ps_tr, ident)
    ntr = 0
    nproj = 0
    bank_guard = [None, None, None]
    qu_store_guard = [0]
    xr_store_guard = [0]
    gate_guard = [0]
    kv_evac_guard = [0]
    xnt_guard = [0]
    feat_guard = [0]
    rms_mm_guard = [0]
    ctx.emit("vector", lambda e: e.memset(kv_acc[:, :], 0.0))

    x_guard = [[], []]
    lda_chunk = [0] * NCHUNK
    for ch in range(NCHUNK):
        par = ch % 2
        x_tok = x_tok2[:, par]
        xnt32 = xnt321[:, 0]
        xnb_tok = xnb_tok2[:, par]
        ssq, ssm = stats[:, par, 0], stats[:, par, 1]
        mean, rstd = stats[:, par, 2], stats[:, par, 3]
        rmsf, tstat = stats[:, par, 4], stats[:, par, 5]
        xsrc = ctx.dram["L0_attn_x"].ap()[ch * 512:(ch + 1) * 512, :]
        ctx.emit("sync", lambda e, xsrc=xsrc, par=par: e.dma_start(
            x_tok2[:, par, :, :], rearr_tok(xsrc)),
            waits=x_guard[par], inc=s_lda, inc_n=16)
        lda_chunk[ch] = s_lda.c
        wx = [s_lda.at()]

        for t in range(NTILE):
            sqw = list(wx) if t == 0 else []
            if t == 0 and xnt_guard[0] > 0:
                sqw.append((s_t, xnt_guard[0]))
            ctx.emit("scalar", lambda e, t=t, x_tok=x_tok, ssq=ssq: e.activation(
                sq_buf, x_tok[:, t, :], AF.Square, accum_out=ssq[:, t:t + 1]),
                waits=sqw, inc=s_s)
            ctx.emit("vector", lambda e, t=t, x_tok=x_tok, ssm=ssm: e.tensor_reduce(
                ssm[:, t:t + 1], x_tok[:, t, :], AX.X, ALU.add), waits=wx if t == 0 else [], inc=s_v)
        ctx.emit("vector", lambda e, mean=mean, ssm=ssm: e.tensor_single_scalar(
            mean[:, :], ssm[:, :], 1.0 / D, ALU.mult), waits=[s_s.at(), s_v.at()])
        ctx.emit("vector", lambda e, tstat=tstat, mean=mean: e.tensor_tensor(
            tstat[:, :], mean[:, :], mean[:, :], ALU.mult))
        ctx.emit("vector", lambda e, tstat=tstat, ssq=ssq: e.scalar_tensor_tensor(
            tstat[:, :], ssq[:, :], 1.0 / D, tstat[:, :], ALU.mult, ALU.subtract), inc=s_v)
        ctx.emit("scalar", lambda e, rstd=rstd, tstat=tstat: e.activation(
            rstd[:, :], tstat[:, :], AF.Ln, bias=c["cLN"][:, :]), waits=[s_v.at()], inc=s_s)
        ctx.emit("scalar", lambda e, rstd=rstd: e.activation(
            rstd[:, :], rstd[:, :], AF.Exp, scale=-0.5), inc=s_s)
        ctx.emit("vector", lambda e, rmsf=rmsf, rstd=rstd: e.tensor_tensor(
            rmsf[:, :], rstd[:, :], rstd[:, :], ALU.mult), waits=[s_s.at()])
        ctx.emit("vector", lambda e, rmsf=rmsf, tstat=tstat: e.tensor_tensor(
            rmsf[:, :], rmsf[:, :], tstat[:, :], ALU.mult), inc=s_v)
        ctx.emit("scalar", lambda e, rmsf=rmsf: e.activation(
            rmsf[:, :], rmsf[:, :], AF.Ln, bias=c["cRMS"][:, :]), waits=[s_v.at()], inc=s_s)
        ctx.emit("scalar", lambda e, rmsf=rmsf: e.activation(
            rmsf[:, :], rmsf[:, :], AF.Exp, scale=-0.5), inc=s_s)
        wstat = [s_s.at()]

        for t in range(NTILE):
            wl = list(wstat) if t == 0 else []
            if t == 0 and xnt_guard[0] > 0:
                wl.append((s_t, xnt_guard[0]))
            ctx.emit("vector", lambda e, t=t, xnt32=xnt32, x_tok=x_tok, mean=mean, rstd=rstd: e.tensor_scalar(
                xnt32[:, t, :], x_tok[:, t, :], mean[:, t:t + 1], rstd[:, t:t + 1],
                ALU.subtract, ALU.mult), waits=wl)
            ctx.emit("vector", lambda e, t=t, xnb_tok=xnb_tok, xnt32=xnt32: e.tensor_copy(
                xnb_tok[:, t, :], xnt32[:, t, :]))
        wxn = [s_v.at()]
        x_guard[par] = [(s_v, s_v.c), (s_s, s_s.c)]
        for t in range(NTILE):
            for k in range(8):
                tw = wxn if (t == 0 and k == 0) else []
                cw = []
                if t == 0 and k == 0 and feat_guard[0] > 0:
                    cw = [(s_t, feat_guard[0])]
                emit_tr(xnb_tok[:, t, k * 128:(k + 1) * 128],
                        xnb_feat[:, k, t * 128:(t + 1) * 128], tw=tw, cw=cw)
        wfeat = [s_v.at()]

        ctx.emit("tensor", lambda e, rmsf=rmsf: e.transpose(
            ps_misc[0:4, 0:128], rmsf[:, 0:4], c["identF"][:, :]),
            waits=([(s_s, gate_guard[0])] if gate_guard[0] > 0 else []) + [(s_v, s_v.c)],
            inc=s_t)
        ctx.emit("vector", lambda e: e.tensor_copy(rms4f[:, :], ps_misc[0:4, 0:128]),
                 waits=[s_t.at(), (s_ldw, s_ldw.c)])
        for tt in range(NTILE):
            ctx.emit("gpsimd", lambda e, tt=tt: e.dma_start(
                rmsr_b[0:1, tt * 128:(tt + 1) * 128], rms4f[tt:tt + 1, :]),
                waits=([(s_v, s_v.c), (s_t, rms_mm_guard[0])] if tt == 0 else []),
                inc=s_ldw, inc_n=16)
        bankx = nproj % 3
        nproj += 1
        bwl = [(s_ldw, s_ldw.c)]
        if bank_guard[bankx]:
            bwl.append(bank_guard[bankx])
        ctx.emit("tensor", lambda e, bankx=bankx: e.matmul(
            ps_proj[bankx][:, :], c["ones_k1"][:, :], rmsr_b[0:1, :], start=True, stop=True),
            waits=bwl, inc=s_t)
        rms_mm_guard[0] = s_t.c
        xrw = [s_t.at()]
        for k in range(8):
            cwx = list(xrw) if k == 0 else []
            if k == 0 and xr_store_guard[0] > 0:
                cwx.append((s_st, xr_store_guard[0]))
            ctx.emit("vector", lambda e, k=k, bankx=bankx: e.tensor_tensor(
                xrb_feat[:, k, :], xnb_feat[:, k, :], ps_proj[bankx][:, :], ALU.mult),
                waits=cwx)
        bank_guard[bankx] = (s_v, s_v.c)

        ctx.emit("sync", lambda e, ch=ch: e.dma_start(
            ctx.dram["xr_feat"].ap()[:, :, ch * 512:(ch + 1) * 512], xrb_feat[:, :, :]),
            waits=[s_v.at()], inc=s_st, inc_n=16)

        for (wt, stage, act, scale) in ((c["WqT"], q_stage, AF.Relu, 0.125),
                                        (c["WuT"], u_stage, AF.Silu, 1.0)):
            for m in range(8):
                bank = nproj % 3
                wlist = list(wfeat) if m == 0 else []
                if bank_guard[bank]:
                    wlist.append(bank_guard[bank])
                if m == 0 and qu_store_guard[0] > 0:
                    pass
                for k in range(8):
                    ctx.emit("tensor", lambda e, wt=wt, m=m, k=k, bank=bank: e.matmul(
                        ps_proj[bank][:, :], wt[:, k, m * 128:(m + 1) * 128], xnb_feat[:, k, :],
                        start=(k == 0), stop=(k == 7)),
                        waits=wlist if k == 0 else [], inc=(s_t if k == 7 else None))
                nproj += 1
                ctx.emit("scalar", lambda e, stage=stage, m=m, act=act, scale=scale, bank=bank: e.activation(
                    stage[:, m, :], ps_proj[bank][:, :], act, scale=scale),
                    waits=[s_t.at()] + ([(s_st, qu_store_guard[0])] if (m == 0 and qu_store_guard[0] > 0) else []),
                    inc=s_s)
                bank_guard[bank] = (s_s, s_s.c)
        ctx.emit("sync", lambda e, ch=ch: e.dma_start(
            ctx.dram["q_feat"].ap()[:, :, ch * 512:(ch + 1) * 512], q_stage[:, :, :]),
            waits=[s_s.at()], inc=s_st, inc_n=16)
        ctx.emit("sync", lambda e, ch=ch: e.dma_start(
            ctx.dram["u_feat"].ap()[:, :, ch * 512:(ch + 1) * 512], u_stage[:, :, :]),
            waits=[s_s.at()], inc=s_st, inc_n=16)
        qu_store_guard[0] = s_st.c
        xr_store_guard[0] = s_st.c

        for (wt, dst, act, scale) in ((c["WkT"], k_tok, AF.Relu, 0.125),
                                      (c["WvT"], v_tok, AF.Identity, 1.0)):
            for t in range(NTILE):
                for n in range(2):
                    bank = nproj % 3
                    wlist = [bank_guard[bank]] if bank_guard[bank] else []
                    for k in range(8):
                        ctx.emit("tensor", lambda e, wt=wt, t=t, n=n, k=k, bank=bank: e.matmul(
                            ps_proj[bank][:, :], xnb_feat[:, k, t * 128:(t + 1) * 128],
                            wt[:, k, n * 512:(n + 1) * 512], start=(k == 0), stop=(k == 7)),
                            waits=wlist if k == 0 else [], inc=(s_t if k == 7 else None))
                    nproj += 1
                    ctx.emit("scalar", lambda e, dst=dst, t=t, n=n, act=act, scale=scale, bank=bank: e.activation(
                        dst[:, t, n * 512:(n + 1) * 512], ps_proj[bank][:, :], act, scale=scale),
                        waits=[s_t.at()], inc=s_s)
                    bank_guard[bank] = (s_s, s_s.c)
        wkv = [s_s.at()]

        for h in range(H0):
            po = (h % 2) * 64
            fo = (h // 2) * 64
            for t in range(NTILE):
                wl = list(wkv) if (h == 0 and t == 0) else []
                if h == 0 and t == 0 and kv_evac_guard[0] > 0:
                    wl.append((s_v, kv_evac_guard[0]))
                ctx.emit("tensor", lambda e, t=t, h=h, po=po, fo=fo: e.matmul(
                    kv_ps[po:po + 64, fo:fo + 64], k_tok[:, t, h * 64:(h + 1) * 64],
                    v_tok[:, t, h * 64:(h + 1) * 64], start=(t == 0), stop=(t == NTILE - 1)),
                    waits=wl,
                    inc=(s_t if (t == NTILE - 1 and h == H0 - 1) else None))
        ctx.emit("vector", lambda e: e.tensor_tensor(kv_acc[:, :], kv_acc[:, :], kv_ps[:, :], ALU.add),
                 waits=[s_t.at()])
        kv_evac_guard[0] = ctx.sems["s_v"].c

        for t in range(NTILE):
            for k in range(8):
                twl = [(s_v, s_v.c)]
                if k == 0 and gate_guard[0] > 0:
                    twl.append((s_s, gate_guard[0]))
                ctx.emit("tensor", lambda e, t=t, k=k, xnt32=xnt32: e.transpose(
                    ps_misc[:, 0:128], xnt32[:, t, k * 128:(k + 1) * 128],
                    c["identF"][:, :]), waits=twl, inc=s_t)
                ctx.emit("vector", lambda e, k=k: e.tensor_copy(
                    xnf32[:, k, :], ps_misc[:, 0:128]), waits=[s_t.at()])
            wtr32 = [s_v.at()]
            for k in range(8):
                gwl = list(wtr32) if k == 0 else []
                if k == 0 and gate_guard[0] > 0:
                    gwl.append((s_s, gate_guard[0]))
                ctx.emit("tensor", lambda e, t=t, k=k: e.matmul(
                    ps_misc[:, 384 + 16 * t:384 + 16 * t + 14], xnf32[:, k, :],
                    c["router_sb"][:, k, :], start=(k == 0), stop=(k == 7)),
                    waits=gwl, inc=(s_t if k == 7 else None))
            gw = [s_t.at()]
            ctx.emit("vector", lambda e, t=t: e.tensor_reduce(
                gstat[:, 0:1], ps_misc[:, 384 + 16 * t:384 + 16 * t + 14], AX.X, ALU.max,
                negate=True), waits=gw)
            ctx.emit("scalar", lambda e, t=t: e.activation(
                gexp[:, 0:14], ps_misc[:, 384 + 16 * t:384 + 16 * t + 14], AF.Exp,
                bias=gstat[:, 0:1], accum_out=gstat[:, 1:2]), waits=[s_v.at()])
            gate_guard[0] = s_s.c
            ws = [s_s.at()]
            ctx.emit("vector", lambda e: e.tensor_reduce(gstat[:, 2:3], gexp[:, 0:14], AX.X, ALU.max), waits=ws)
            ctx.emit("vector", lambda e: e.tensor_scalar(gtmp[:, 0:14], gexp[:, 0:14], gstat[:, 2:3], None, ALU.is_lt))
            ctx.emit("vector", lambda e: e.tensor_tensor(gtmp[:, 0:14], gtmp[:, 0:14], gexp[:, 0:14], ALU.mult))
            ctx.emit("vector", lambda e: e.tensor_reduce(gstat[:, 2:3], gtmp[:, 0:14], AX.X, ALU.max))
            ctx.emit("vector", lambda e: e.tensor_scalar(gkeep[:, 0:14], gexp[:, 0:14], gstat[:, 2:3], None, ALU.is_ge))
            ctx.emit("vector", lambda e: e.tensor_tensor(gkeep[:, 0:14], gkeep[:, 0:14], gexp[:, 0:14], ALU.mult))
            ctx.emit("vector", lambda e: e.tensor_reduce(gstat[:, 3:4], gkeep[:, 0:14], AX.X, ALU.add))
            ctx.emit("vector", lambda e: e.scalar_tensor_tensor(
                gstat[:, 3:4], gstat[:, 1:2], MOE_EPS, gstat[:, 3:4], ALU.mult, ALU.add))
            ctx.emit("vector", lambda e: e.reciprocal(gstat[:, 3:4], gstat[:, 3:4]))
            ctx.emit("vector", lambda e, ch=ch, t=t: e.tensor_scalar(
                w_tok[:, ch * 4 + t, :], gkeep[:, 0:14], gstat[:, 3:4], None, ALU.mult))
        xnt_guard[0] = s_t.c
        feat_guard[0] = s_t.c

    ctx.barrier()
    es.close()


def phase_l0_b(ctx):
    c = ctx.consts
    s_lda, s_st = ctx.sems["s_lda"], ctx.sems["s_st"]
    s_v, s_s, s_t = ctx.sems["s_v"], ctx.sems["s_s"], ctx.sems["s_t"]
    es = contextlib.ExitStack()
    kv_b = ctx.sb(es, "b_kv_b", [128, 512], BF16)
    q_sb = ctx.sb(es, "b_q_sb", [128, 8, 512], BF16)
    u_sb = ctx.sb(es, "b_u_sb", [128, 8, 512], BF16)
    att_sb = ctx.sb(es, "b_att_sb", [128, 8, 512], BF16)
    sq_sb = ctx.sb(es, "b_sq_sb", [128, 8, 512], BF16)
    fsb = ctx.sb(es, "b_fsb", [16, 512], F32)
    fsb_b = ctx.sb(es, "b_fsb_b", [16, 512], BF16)
    tmp = ctx.sb(es, "b_tmp", [128, 512], F32)
    attg = ctx.sb(es, "b_attg", [128, 8, 512], BF16)
    s_stage = ctx.sb(es, "b_s_stage", [128, 8, 512], BF16)
    ps_att = [ctx.ps(es, "b_ps_attA", [128, 512], F32), ctx.ps(es, "b_ps_attB", [128, 512], F32)]
    ps_wo = [ctx.ps(es, "b_ps_woA", [128, 512], F32), ctx.ps(es, "b_ps_woB", [128, 512], F32)]
    ps_norm = ctx.ps(es, "b_ps_norm", [16, 512], F32)
    ps_fb = ctx.ps(es, "b_ps_fb", [128, 512], F32)

    ctx.emit("vector", lambda e: e.tensor_copy(kv_b[:, :], c["kv_acc"][:, :]))
    wkvb = [s_v.at()]
    att_bank_guard = [0, 0]
    wo_bank_guard = [0, 0]
    norm_guard = [0]
    attg_guard = [0]
    s_store_guard = [0]

    for ch in range(NCHUNK):
        for (dst, nm) in ((q_sb, "q_feat"), (u_sb, "u_feat")):
            ctx.emit("sync", lambda e, dst=dst, nm=nm, ch=ch: e.dma_start(
                dst[:, :, :], ctx.dram[nm].ap()[:, :, ch * 512:(ch + 1) * 512]),
                waits=[s_t.at(), s_v.at(), s_s.at()], inc=s_lda, inc_n=16)
        wld = [s_lda.at()]
        for hp in range(8):
            bank = hp % 2
            for hh in range(2):
                h = 2 * hp + hh
                po, fo = (h % 2) * 64, (h // 2)
                wl = list(wld) + list(wkvb) if (hp == 0 and hh == 0) else []
                if att_bank_guard[bank] > 0 and hh == 0:
                    wl.append((s_s, att_bank_guard[bank]))
                ctx.emit("tensor", lambda e, h=h, po=po, fo=fo, bank=bank: e.matmul(
                    ps_att[bank][po:po + 64, :], kv_b[po:po + 64, fo * 64:fo * 64 + 64],
                    q_sb[po:po + 64, fo, :], start=True, stop=True),
                    waits=wl, inc=(s_t if hh == 1 else None))
            ctx.emit("scalar", lambda e, hp=hp, bank=bank: e.activation(
                att_sb[:, hp, :], ps_att[bank][:, :], AF.Identity), waits=[s_t.at()])
            ctx.emit("scalar", lambda e, hp=hp, bank=bank: e.activation(
                sq_sb[:, hp, :], ps_att[bank][:, :], AF.Square, accum_out=None))
            att_bank_guard[bank] = s_s.c
        wsq = [s_s.at()]
        for ec in range(8):
            wl = list(wsq) if ec == 0 else []
            if ec == 0 and norm_guard[0] > 0:
                wl.append((s_v, norm_guard[0]))
            ctx.emit("tensor", lambda e, ec=ec: e.matmul(
                ps_norm[:, :], c["blk8"][:, ec, :], sq_sb[:, ec, :],
                start=(ec == 0), stop=(ec == 7)), waits=wl,
                inc=(s_t if ec == 7 else None))
        ctx.emit("vector", lambda e: e.tensor_single_scalar(fsb[:, :], ps_norm[:, :], 1e-24, ALU.max),
                 waits=[s_t.at()])
        norm_guard[0] = s_v.c
        ctx.emit("scalar", lambda e: e.activation(fsb[:, :], fsb[:, :], AF.Ln), waits=[s_v.at()])
        ctx.emit("scalar", lambda e: e.activation(fsb_b[:, :], fsb[:, :], AF.Exp, scale=-0.5))
        wfac = [s_s.at()]
        for ec in range(8):
            wl = list(wfac) if ec == 0 else [(s_v, s_v.c)]
            ctx.emit("tensor", lambda e, ec=ec: e.matmul(
                ps_fb[:, :], c["sel8"][:, ec, :], fsb_b[:, :], start=True, stop=True),
                waits=wl, inc=s_t)
            tw = [s_t.at()]
            if ec == 0 and attg_guard[0] > 0:
                tw.append((s_t, attg_guard[0]))
            ctx.emit("vector", lambda e, ec=ec: e.tensor_tensor(
                tmp[:, :], att_sb[:, ec, :], ps_fb[:, :], ALU.mult), waits=tw)
            ctx.emit("vector", lambda e, ec=ec: e.tensor_tensor(
                attg[:, ec, :], tmp[:, :], u_sb[:, ec, :], ALU.mult))
        wattg = [s_v.at()]
        for m in range(8):
            bank = m % 2
            wl = list(wattg) if m == 0 else []
            if wo_bank_guard[bank] > 0:
                wl.append((s_s, wo_bank_guard[bank]))
            for k in range(8):
                ctx.emit("tensor", lambda e, m=m, k=k, bank=bank: e.matmul(
                    ps_wo[bank][:, :], c["WoT"][:, k, m * 128:(m + 1) * 128], attg[:, k, :],
                    start=(k == 0), stop=(k == 7)),
                    waits=wl if k == 0 else [], inc=(s_t if k == 7 else None))
            swl = [s_t.at()]
            if m == 0 and s_store_guard[0] > 0:
                swl.append((s_st, s_store_guard[0]))
            ctx.emit("scalar", lambda e, m=m, bank=bank: e.activation(
                s_stage[:, m, :], ps_wo[bank][:, :], AF.Identity), waits=swl)
            wo_bank_guard[bank] = s_s.c
        attg_guard[0] = s_t.c
        ctx.emit("sync", lambda e, ch=ch: e.dma_start(
            ctx.dram["s_feat"].ap()[:, :, ch * 512:(ch + 1) * 512], s_stage[:, :, :]),
            waits=[s_s.at()], inc=s_st, inc_n=16)
        s_store_guard[0] = s_st.c

    ctx.barrier()
    es.close()
    ctx.scope["ab"].close()


def phase_l0_moe(ctx):
    c = ctx.consts
    s_lda, s_ldw, s_st = ctx.sems["s_lda"], ctx.sems["s_ldw"], ctx.sems["s_st"]
    s_v, s_s, s_t = ctx.sems["s_v"], ctx.sems["s_s"], ctx.sems["s_t"]
    es = contextlib.ExitStack()
    NQ, QT = 4, 1024
    wbuf = [ctx.sb(es, f"m_wbuf{i}", [128, 16384], BF16) for i in range(3)]
    xrb = ctx.sb(es, "m_xrb", [128, 8, QT], BF16)
    hb = ctx.sb(es, "m_hb", [128, 16, 512], BF16)
    g_acc = ctx.sb(es, "m_gacc", [128, 8, QT], F32)
    g_stage = hb[:, :, :].rearrange("p c n -> p (c n)").rearrange("p (c n) -> p c n", c=8)
    wb_sb = ctx.sb(es, "m_wb", [128, 2, 512], BF16)
    tmp2 = ctx.sb(es, "m_tmp2", [128, 512], F32)
    wT_sb = ctx.sb(es, "m_wT", [16, T], BF16)
    wflat = ctx.sb(es, "m_wflat", [1, E_ROUTED * QT], BF16)
    ps_h = [ctx.ps(es, f"m_ps_h{i}", [128, 512], F32) for i in range(3)]
    ps_o = [ctx.ps(es, f"m_ps_o{i}", [128, 512], F32) for i in range(2)]
    ps_wb = ctx.ps(es, "m_ps_wb", [128, 512], F32)

    def w1view(i):
        return wbuf[i][:, :].rearrange("p (c n) -> p c n", c=8)

    def w2view(i):
        return wbuf[i][:, :].rearrange("p (c n) -> p c n", c=16)

    for g in range(32):
        ctx.emit("tensor", lambda e, g=g: e.transpose(
            ps_wb[0:14, 0:128], c["w_tok"][:, g, 0:14], c["identF"][:, :]),
            waits=[(s_v, s_v.c)], inc=s_t)
        ctx.emit("vector", lambda e, g=g: e.tensor_copy(
            wT_sb[0:14, g * 128:(g + 1) * 128], ps_wb[0:14, 0:128]), waits=[s_t.at()])
    wT_done = s_v.at()

    buf_guard = [0, 0, 0]
    bank_guard3 = [0, 0, 0]
    out_bank_guard = [0, 0]
    hb_guard = [0]
    wb_guard = [0, 0]
    gst_guard = [0]
    w1_free = [0]

    def w1src(e):
        if e < E_ROUTED:
            return ctx.dram["L0_ffn_w1"].ap()[e]
        return ctx.dram["L0_ffn_w1_s"].ap()[e - E_ROUTED]

    def w2src(e):
        if e < E_ROUTED:
            return ctx.dram["L0_ffn_w2"].ap()[e]
        return ctx.dram["L0_ffn_w2_s"].ap()[e - E_ROUTED]

    NE = E_ROUTED + E_SHARED
    for q in range(NQ):
        ctx.emit("sync", lambda e, q=q: e.dma_start(
            xrb[:, :, :], ctx.dram["xr_feat"].ap()[:, :, q * QT:(q + 1) * QT]),
            waits=[s_t.at(), s_v.at(), s_s.at()], inc=s_lda, inc_n=16)
        wxr = [s_lda.at()]
        ctx.emit("vector", lambda e: e.memset(g_acc[:, :, :], 0.0), waits=[s_s.at()])
        for e_i in range(E_ROUTED):
            ctx.emit("gpsimd", lambda e, e_i=e_i, q=q: e.dma_start(
                wflat[0:1, e_i * QT:(e_i + 1) * QT], wT_sb[e_i:e_i + 1, q * QT:(q + 1) * QT]),
                waits=([wT_done, (s_t, s_t.c)] if e_i == 0 else []), inc=ctx.sems["s_wfl"], inc_n=16)
        wfl = [ctx.sems["s_wfl"].at()]
        for e_i in range(NE):
            b1i, b2i = (2 * e_i) % 3, (2 * e_i + 1) % 3
            sb1, sb2 = ctx.sems[f"s_wb{b1i}"], ctx.sems[f"s_wb{b2i}"]
            ctx.emit("gpsimd", lambda e, e_i=e_i, b1i=b1i: e.dma_start(
                w1view(b1i)[:, :, :], w1src(e_i).rearrange("(c p) h -> p c h", p=128)),
                waits=[(s_t, buf_guard[b1i])] if buf_guard[b1i] > 0 else [],
                inc=sb1, inc_n=16)
            w1ld = [sb1.at()]
            ctx.emit("gpsimd", lambda e, e_i=e_i, b2i=b2i: e.dma_start(
                w2view(b2i)[:, :, :], w2src(e_i).rearrange("(c p) h -> p c h", p=128)),
                waits=[(s_t, buf_guard[b2i])] if buf_guard[b2i] > 0 else [],
                inc=sb2, inc_n=16)
            w2ld = [sb2.at()]
            for k in range(8):
                ctx.emit("vector", lambda e, e_i=e_i, b1i=b1i, k=k: e.tensor_scalar(
                    w1view(b1i)[:, k, :], w1view(b1i)[:, k, :], c["rms_all"][:, e_i, k:k + 1],
                    None, ALU.mult), waits=w1ld if k == 0 else [])
            wsc = [s_v.at()]
            routed = e_i < E_ROUTED
            if routed:
                for n in range(2):
                    wbw = []
                    if n == 0:
                        wbw.append(ctx.sems["s_wfl"].at())
                    else:
                        wbw.append((s_v, s_v.c))
                    if wb_guard[n] > 0:
                        wbw.append((s_v, wb_guard[n]))
                    ctx.emit("tensor", lambda e, e_i=e_i, n=n: e.matmul(
                        ps_wb[:, :], c["ones_k1"][:, :],
                        wflat[0:1, e_i * QT + n * 512:e_i * QT + (n + 1) * 512],
                        start=True, stop=True), waits=wbw, inc=s_t)
                    ctx.emit("vector", lambda e, n=n: e.tensor_copy(wb_sb[:, n, :], ps_wb[:, :]),
                             waits=[s_t.at()])
            for n in range(2):
                for m in range(16):
                    bank = m % 3
                    wl = []
                    if m == 0:
                        wl += wsc + wxr
                        if hb_guard[0] > 0:
                            wl.append((s_t, hb_guard[0]))
                    if bank_guard3[bank] > 0:
                        wl.append((s_s, bank_guard3[bank]))
                    for k in range(8):
                        ctx.emit("tensor", lambda e, b1i=b1i, m=m, k=k, n=n, bank=bank: e.matmul(
                            ps_h[bank][:, :], w1view(b1i)[:, k, m * 128:(m + 1) * 128],
                            xrb[:, k, n * 512:(n + 1) * 512], start=(k == 0), stop=(k == 7)),
                            waits=wl if k == 0 else [], inc=(s_t if k == 7 else None))
                    hwl = [s_t.at()]
                    if e_i == 0 and n == 0 and m == 0 and gst_guard[0] > 0:
                        hwl.append((s_st, gst_guard[0]))
                    ctx.emit("scalar", lambda e, m=m, bank=bank: e.activation(
                        hb[:, m, :], ps_h[bank][:, :], AF.Silu), waits=hwl)
                    bank_guard3[bank] = s_s.c
                if n == 1:
                    w1_free[0] = s_t.c
                whb = [s_s.at()]
                for mo in range(8):
                    bank = mo % 2
                    wl = list(whb) + list(w2ld) if mo == 0 else []
                    if out_bank_guard[bank] > 0:
                        wl.append((s_v, out_bank_guard[bank]))
                    for kh in range(16):
                        ctx.emit("tensor", lambda e, b2i=b2i, mo=mo, kh=kh, bank=bank: e.matmul(
                            ps_o[bank][:, :], w2view(b2i)[:, kh, mo * 128:(mo + 1) * 128],
                            hb[:, kh, :], start=(kh == 0), stop=(kh == 15)),
                            waits=wl if kh == 0 else [], inc=(s_t if kh == 15 else None))
                    if routed:
                        ctx.emit("vector", lambda e, mo=mo, bank=bank, n=n: e.tensor_tensor(
                            tmp2[:, :], ps_o[bank][:, :], wb_sb[:, n, :], ALU.mult), waits=[s_t.at()])
                        ctx.emit("vector", lambda e, mo=mo, n=n: e.tensor_tensor(
                            g_acc[:, mo, n * 512:(n + 1) * 512], g_acc[:, mo, n * 512:(n + 1) * 512],
                            tmp2[:, :], ALU.add))
                    else:
                        ctx.emit("vector", lambda e, mo=mo, n=n, bank=bank: e.tensor_tensor(
                            g_acc[:, mo, n * 512:(n + 1) * 512], g_acc[:, mo, n * 512:(n + 1) * 512],
                            ps_o[bank][:, :], ALU.add), waits=[s_t.at()])
                    out_bank_guard[bank] = s_v.c
                if routed:
                    wb_guard[n] = s_v.c
                hb_guard[0] = s_t.c
            buf_guard[b1i] = w1_free[0]
            buf_guard[b2i] = s_t.c
        gswl = [s_v.at(), s_t.at()]
        if q > 0:
            gswl.append((s_st, gst_guard[0]))
        for mo in range(8):
            ctx.emit("scalar", lambda e, mo=mo: e.activation(
                g_stage[:, mo, :], g_acc[:, mo, :], AF.Silu), waits=gswl if mo == 0 else [])
        ctx.emit("sync", lambda e, q=q: e.dma_start(
            ctx.dram["g_feat"].ap()[:, :, q * QT:(q + 1) * QT], g_stage[:, :, :]),
            waits=[s_s.at()], inc=s_st, inc_n=16)
        gst_guard[0] = s_st.c

    ctx.barrier()
    es.close()


def phase_r0(ctx):
    c = ctx.consts
    s_lda, s_st = ctx.sems["s_lda"], ctx.sems["s_st"]
    s_v, s_s, s_t = ctx.sems["s_v"], ctx.sems["s_s"], ctx.sems["s_t"]
    es = contextlib.ExitStack()
    NQ, QT, NT8 = 4, 1024, 8
    x_tok = ctx.sb(es, "r_x_tok", [128, NT8, D], F32)
    s_sb = ctx.sb(es, "r_s", [128, 8, QT], BF16)
    g_sb = ctx.sb(es, "r_g", [128, 8, QT], BF16)
    delta_f = ctx.sb(es, "r_delta_f", [128, 8, QT], BF16)
    delta_t = ctx.sb(es, "r_delta_t", [128, NT8, D], BF16)
    xn1_tok = ctx.sb(es, "r_xn1_tok", [128, NT8, D], BF16)
    xn1_stage = ctx.sb(es, "r_xn1_stage", [128, 8, QT], BF16)
    sq_buf = ctx.sb(es, "r_sq", [128, D], F32)
    ssq = ctx.sb(es, "r_ssq", [128, NT8], F32)
    ssm = ctx.sb(es, "r_ssm", [128, NT8], F32)
    mean = ctx.sb(es, "r_mean", [128, NT8], F32)
    rstd = ctx.sb(es, "r_rstd", [128, NT8], F32)
    tstat = ctx.sb(es, "r_tstat", [128, NT8], F32)
    ps_tr = [ctx.ps(es, "r_ps_trA", [128, 128], BF16), ctx.ps(es, "r_ps_trB", [128, 128], BF16)]

    emit_tr = ctx.make_transposer(ps_tr, c["ident"])
    st_guard = [0]
    for q in range(NQ):
        base = q * QT
        ctx.emit("sync", lambda e, q=q: e.dma_start(
            x_tok[:, :, :], ctx.dram["L0_attn_x"].ap()[q * QT:(q + 1) * QT, :].rearrange(
                "(t p) d -> p t d", p=128)),
            waits=[s_t.at(), s_v.at(), s_s.at()] + ([(s_st, st_guard[0])] if st_guard[0] > 0 else []),
            inc=s_lda, inc_n=16)
        ctx.emit("sync", lambda e, q=q: e.dma_start(
            s_sb[:, :, :], ctx.dram["s_feat"].ap()[:, :, q * QT:(q + 1) * QT]), inc=s_lda, inc_n=16)
        ctx.emit("sync", lambda e, q=q: e.dma_start(
            g_sb[:, :, :], ctx.dram["g_feat"].ap()[:, :, q * QT:(q + 1) * QT]), inc=s_lda, inc_n=16)
        wld = [s_lda.at()]
        for kc in range(8):
            ctx.emit("vector", lambda e, kc=kc: e.tensor_tensor(
                delta_f[:, kc, :], s_sb[:, kc, :], g_sb[:, kc, :], ALU.mult),
                waits=wld if kc == 0 else [])
        wdf = [s_v.at()]
        for t in range(NT8):
            for kc in range(8):
                emit_tr(delta_f[:, kc, t * 128:(t + 1) * 128],
                        delta_t[:, t, kc * 128:(kc + 1) * 128],
                        tw=(wdf if (t == 0 and kc == 0) else []))
        for t in range(NT8):
            ctx.emit("vector", lambda e, t=t: e.tensor_tensor(
                x_tok[:, t, :], x_tok[:, t, :], delta_t[:, t, :], ALU.add))
        wx1 = [s_v.at()]
        ctx.emit("sync", lambda e, q=q: e.dma_start(
            ctx.dram["x1m"].ap()[q * QT:(q + 1) * QT, :].rearrange("(t p) d -> p t d", p=128),
            x_tok[:, :, :]), waits=wx1, inc=s_st, inc_n=16)
        for t in range(NT8):
            ctx.emit("scalar", lambda e, t=t: e.activation(
                sq_buf[:, :], x_tok[:, t, :], AF.Square, accum_out=ssq[:, t:t + 1]),
                waits=wx1 if t == 0 else [])
            ctx.emit("vector", lambda e, t=t: e.tensor_reduce(
                ssm[:, t:t + 1], x_tok[:, t, :], AX.X, ALU.add))
        ctx.emit("vector", lambda e: e.tensor_single_scalar(mean[:, :], ssm[:, :], 1.0 / D, ALU.mult),
                 waits=[s_s.at()])
        ctx.emit("vector", lambda e: e.tensor_tensor(tstat[:, :], mean[:, :], mean[:, :], ALU.mult))
        ctx.emit("vector", lambda e: e.scalar_tensor_tensor(tstat[:, :], ssq[:, :], 1.0 / D, tstat[:, :],
                                                            ALU.mult, ALU.subtract))
        ctx.emit("scalar", lambda e: e.activation(rstd[:, :], tstat[:, :], AF.Ln, bias=c["cLN"][:, :]),
                 waits=[s_v.at()])
        ctx.emit("scalar", lambda e: e.activation(rstd[:, :], rstd[:, :], AF.Exp, scale=-0.5))
        wstat = [s_s.at()]
        for t in range(NT8):
            ctx.emit("vector", lambda e, t=t: e.tensor_scalar(
                xn1_tok[:, t, :], x_tok[:, t, :], mean[:, t:t + 1], rstd[:, t:t + 1],
                ALU.subtract, ALU.mult), waits=wstat if t == 0 else [])
        for t in range(NT8):
            for kc in range(8):
                tw = [(s_v, s_v.c)] if (t == 0 and kc == 0) else []
                cw = []
                if t == 0 and kc == 0 and st_guard[0] > 0:
                    cw = [(s_st, st_guard[0])]
                emit_tr(xn1_tok[:, t, kc * 128:(kc + 1) * 128],
                        xn1_stage[:, kc, t * 128:(t + 1) * 128], tw=tw, cw=cw)
        ctx.emit("sync", lambda e, q=q: e.dma_start(
            ctx.dram["xn1_feat"].ap()[:, :, q * QT:(q + 1) * QT], xn1_stage[:, :, :]),
            waits=[s_v.at()], inc=s_st, inc_n=16)
        st_guard[0] = s_st.c

    ctx.barrier()
    es.close()




def setup_l1_masks(ctx):
    c = ctx.consts
    es = ctx.scope["de"]
    s_v, s_g = ctx.sems["s_v"], ctx.sems["s_g"]
    es_tmp = contextlib.ExitStack()
    iota_i = ctx.sb(es_tmp, "m1_iota_i", [128, 512], I32, side="right")
    iota_f = ctx.sb(es_tmp, "m1_iota_f", [128, 512], F32, side="right")
    jcol_f = ctx.sb(es_tmp, "m1_jcol_f", [128, 512], F32, side="right")
    tmpa = ctx.sb(es_tmp, "m1_tmpa", [128, 512], F32, side="right")
    tmpb = ctx.sb(es_tmp, "m1_tmpb", [128, 512], F32, side="right")
    m_std = ctx.sb(es, "m_std", [128, 512], BF16)
    m_c0 = ctx.sb(es, "m_c0", [128, 128], BF16)
    m_c1 = ctx.sb(es, "m_c1", [128, 256], BF16)
    m_c2 = ctx.sb(es, "m_c2", [128, 384], BF16)
    c["masks"] = (m_c0, m_c1, m_c2, m_std)
    m_f = ctx.sb(es_tmp, "m1_mf", [128, 512], F32, side="right")

    ctx.emit("gpsimd", lambda e: e.iota(iota_i[:, :], [[1, 512]], channel_multiplier=-1),
             waits=[s_v.at()], inc=s_g)
    ctx.emit("vector", lambda e: e.tensor_copy(iota_f[:, :], iota_i[:, :]), waits=[s_g.at()], inc=s_v)
    ctx.emit("gpsimd", lambda e: e.iota(iota_i[:, :], [[1, 512]], channel_multiplier=0),
             waits=[s_v.at()], inc=s_g)
    ctx.emit("vector", lambda e: e.tensor_copy(jcol_f[:, :], iota_i[:, :]), waits=[s_g.at()], inc=s_v)
    ctx.emit("vector", lambda e: e.tensor_single_scalar(tmpa[:, 0:384], iota_f[:, 0:384], 1.0, ALU.is_ge))
    ctx.emit("vector", lambda e: e.tensor_single_scalar(tmpb[:, 0:384], iota_f[:, 0:384], 256.0, ALU.is_le))
    ctx.emit("vector", lambda e: e.tensor_tensor(m_f[:, 0:384], tmpa[:, 0:384], tmpb[:, 0:384], ALU.mult))
    ctx.emit("vector", lambda e: e.tensor_single_scalar(m_f[:, 384:512], jcol_f[:, 384:512], 388.0, ALU.is_lt))
    ctx.emit("vector", lambda e: e.tensor_copy(m_std[:, :], m_f[:, :]), inc=s_v)
    ctx.emit("vector", lambda e: e.tensor_single_scalar(m_f[:, 0:128], iota_f[:, 0:128], 0.0, ALU.is_le))
    ctx.emit("vector", lambda e: e.tensor_copy(m_c0[:, :], m_f[:, 0:128]), inc=s_v)
    ctx.emit("vector", lambda e: e.tensor_single_scalar(m_f[:, 0:256], iota_f[:, 0:256], 128.0, ALU.is_le))
    ctx.emit("vector", lambda e: e.tensor_copy(m_c1[:, :], m_f[:, 0:256]), inc=s_v)
    ctx.emit("vector", lambda e: e.tensor_single_scalar(tmpa[:, 0:384], iota_f[:, 0:384], 256.0, ALU.is_le))
    ctx.emit("vector", lambda e: e.tensor_single_scalar(tmpb[:, 0:384], jcol_f[:, 0:384], 4.0, ALU.is_lt))
    ctx.emit("vector", lambda e: e.tensor_single_scalar(m_f[:, 0:384], iota_f[:, 0:384], 1.0, ALU.is_ge))
    ctx.emit("vector", lambda e: e.tensor_tensor(m_f[:, 0:384], m_f[:, 0:384], tmpb[:, 0:384], ALU.max))
    ctx.emit("vector", lambda e: e.tensor_tensor(m_f[:, 0:384], m_f[:, 0:384], tmpa[:, 0:384], ALU.mult))
    ctx.emit("vector", lambda e: e.tensor_copy(m_c2[:, :], m_f[:, 0:384]), inc=s_v)

    ctx.barrier()
    es_tmp.close()




def phase_l1_kv(ctx):
    c = ctx.consts
    s_lda = ctx.sems["s_lda"]
    s_v, s_s, s_t = ctx.sems["s_v"], ctx.sems["s_s"], ctx.sems["s_t"]
    es_de = ctx.scope["de"]
    K_all = ctx.sb(es_de, "e_K_all", [128, 2, T], BF16)
    V_all = ctx.sb(es_de, "e_V_all", [128, 32, 256], BF16)
    c["K_all"], c["V_all"] = K_all, V_all
    es = contextlib.ExitStack()
    xn1c = ctx.sb(es, "d_xn1c", [128, 8, 512], BF16)
    ps = [ctx.ps(es, f"d_ps{i}", [128, 512], F32) for i in range(2)]
    bank_guard = [0, 0]
    nb = 0
    for ch in range(NCHUNK):
        ctx.emit("sync", lambda e, ch=ch: e.dma_start(
            xn1c[:, :, :], ctx.dram["xn1_feat"].ap()[:, :, ch * 512:(ch + 1) * 512]),
            waits=[s_t.at()], inc=s_lda, inc_n=16)
        wld = [s_lda.at()]
        for mc in range(2):
            bank = nb % 2
            wl = list(wld) if mc == 0 else []
            if bank_guard[bank] > 0:
                wl.append((s_s, bank_guard[bank]))
            for k in range(8):
                ctx.emit("tensor", lambda e, mc=mc, k=k, bank=bank: e.matmul(
                    ps[bank][:, :], c["Wk1"][:, k, mc * 128:(mc + 1) * 128], xn1c[:, k, :],
                    start=(k == 0), stop=(k == 7)), waits=wl if k == 0 else [],
                    inc=(s_t if k == 7 else None))
            ctx.emit("scalar", lambda e, ch=ch, mc=mc, bank=bank: e.activation(
                K_all[:, mc, ch * 512:(ch + 1) * 512], ps[bank][:, :], AF.Identity),
                waits=[s_t.at()])
            bank_guard[bank] = s_s.c
            nb += 1
        for t in range(NTILE):
            bank = nb % 2
            wl = [(s_s, bank_guard[bank])] if bank_guard[bank] > 0 else []
            for k in range(8):
                ctx.emit("tensor", lambda e, t=t, k=k, bank=bank: e.matmul(
                    ps[bank][:, 0:256], xn1c[:, k, t * 128:(t + 1) * 128], c["Wv1"][:, k, 0:256],
                    start=(k == 0), stop=(k == 7)), waits=wl if k == 0 else [],
                    inc=(s_t if k == 7 else None))
            ctx.emit("scalar", lambda e, ch=ch, t=t, bank=bank: e.activation(
                V_all[:, ch * 4 + t, :], ps[bank][:, 0:256], AF.Identity), waits=[s_t.at()])
            bank_guard[bank] = s_s.c
            nb += 1
    ctx.barrier()
    es.close()


def phase_l1_attn(ctx):
    c = ctx.consts
    s_lda, s_st = ctx.sems["s_lda"], ctx.sems["s_st"]
    s_v, s_s, s_t = ctx.sems["s_v"], ctx.sems["s_s"], ctx.sems["s_t"]
    m_c0, m_c1, m_c2, m_std = c["masks"]
    K_all, V_all = c["K_all"], c["V_all"]
    es = contextlib.ExitStack()
    xn1c = ctx.sb(es, "e_xn1c", [128, 8, 512], BF16)
    Q_sb = ctx.sb(es, "e_Q", [128, 8, 512], BF16)
    e_sb = ctx.sb(es, "e_exp", [128, 2, 512], BF16)
    pn_sb = ctx.sb(es, "e_pn", [128, 2, 512], BF16)
    den = ctx.sb(es, "e_den", [128, 2], F32)
    rcp = ctx.sb(es, "e_rcp", [128, 2], F32)
    pT_sb = ctx.sb(es, "e_pT", [128, 2, 4, 128], BF16)
    Y_sb = ctx.sb(es, "e_Y", [128, 8, 128], BF16)
    s1_stage = ctx.sb(es, "e_s1", [128, 8, 512], BF16)
    ps_lg = [ctx.ps(es, f"e_ps_lg{i}", [128, 512], F32) for i in range(4)]
    ps_pt = [ctx.ps(es, "e_ps_ptA", [128, 512], BF16)]
    ps_y = ctx.ps(es, "e_ps_y", [128, 128], F32)
    ps_wo = [ctx.ps(es, "e_ps_wo", [128, 512], F32)]
    pt_guard = [0]

    lg_guard = [None, None, None, None]
    pn_read_guard = [0, 0]
    exp_done = [0, 0]
    y_guard = [None]
    ywo_guard = [0]
    wo_guard = [None]
    s1_st_guard = [0]

    def head_info(hi):
        pr, pos = hi // 2, hi % 2
        grp, j = pr // 4, pr % 4
        h = grp * 8 + pos * 4 + j
        kvh = h // 4
        return dict(h=h, pr=pr, pos=pos, kvh=kvh, po=(kvh % 2) * 64,
                    slot=(h // 8) * 4 + (h % 4), kfo=kvh // 2)

    def tile_geom(g):
        i0 = g * 128
        if g == 0:
            return [(0, 0, 128)], m_c0, 128, [0]
        if g == 1:
            return [(0, 0, 256)], m_c1, 256, [0, 1]
        if g == 2:
            return [(0, 0, 384)], m_c2, 384, [0, 1, 2]
        return [(0, i0 - 256, 384), (384, 0, 128)], m_std, 512, [g - 2, g - 1, g, 0]

    for ch in range(NCHUNK):
        ctx.emit("sync", lambda e, ch=ch: e.dma_start(
            xn1c[:, :, :], ctx.dram["xn1_feat"].ap()[:, :, ch * 512:(ch + 1) * 512]),
            waits=[s_t.at(), s_v.at(), s_s.at()], inc=s_lda, inc_n=16)
        wld = [s_lda.at()]
        for m in range(8):
            bank = m % 4
            wl = list(wld) if m == 0 else []
            if lg_guard[bank] is not None:
                wl.append(lg_guard[bank])
            for k in range(8):
                ctx.emit("tensor", lambda e, m=m, k=k, bank=bank: e.matmul(
                    ps_lg[bank][:, :], c["Wq1"][:, k, m * 128:(m + 1) * 128], xn1c[:, k, :],
                    start=(k == 0), stop=(k == 7)), waits=wl if k == 0 else [],
                    inc=(s_t if k == 7 else None))
            ctx.emit("scalar", lambda e, m=m, bank=bank: e.activation(
                Q_sb[:, m, :], ps_lg[bank][:, :], AF.Identity, scale=0.125), waits=[s_t.at()])
            lg_guard[bank] = (s_s, s_s.c)
        qs_at = [s_s.at()]

        for t in range(NTILE):
            g = ch * 4 + t
            spans, mask, mw, scs = tile_geom(g)
            nsp = len(spans)
            lg_done = [0] * (N_HEADS1 + 1)

            def emit_logits(hi, first_tile):
                hd = head_info(hi)
                bank = hi % 4
                wl = []
                if lg_guard[bank] is not None:
                    wl.append(lg_guard[bank])
                if first_tile and hi < 2:
                    wl.extend(qs_at)
                for si, (co, s0, sw) in enumerate(spans):
                    ctx.emit("tensor", lambda e, t=t, hd=hd, co=co, s0=s0, sw=sw, bank=bank: e.matmul(
                        ps_lg[bank][:, co:co + sw],
                        Q_sb[hd["po"]:hd["po"] + 64, hd["slot"], t * 128:(t + 1) * 128],
                        K_all[hd["po"]:hd["po"] + 64, hd["kfo"], s0:s0 + sw],
                        start=True, stop=True),
                        waits=wl if si == 0 else [], inc=(s_t if si == nsp - 1 else None))
                lg_done[hi] = s_t.c

            emit_logits(0, t == 0)

            def emit_exp(hi):
                p, bank = hi % 2, hi % 4
                ctx.emit("scalar", lambda e, mw=mw, bank=bank, p=p: e.activation(
                    e_sb[:, p, 0:mw], ps_lg[bank][:, 0:mw], AF.Exp),
                    waits=[(s_t, lg_done[hi])])
                lg_guard[bank] = (s_s, s_s.c)
                exp_done[p] = s_s.c

            emit_exp(0)
            for hi in range(N_HEADS1):
                hd = head_info(hi)
                p = hi % 2
                ctx.emit("vector", lambda e, mask=mask, mw=mw, p=p: e.scalar_tensor_tensor(
                    e_sb[:, p, 0:mw], e_sb[:, p, 0:mw], 1.0, mask[:, 0:mw], ALU.mult, ALU.mult,
                    accum_out=den[:, p:p + 1]), waits=[(s_s, exp_done[p])])
                ctx.emit("vector", lambda e, p=p: e.reciprocal(rcp[:, p:p + 1], den[:, p:p + 1]))
                rcp_done = s_v.c
                if hi + 1 < N_HEADS1:
                    emit_logits(hi + 1, t == 0)
                    emit_exp(hi + 1)
                pw = [(s_v, rcp_done)]
                if pn_read_guard[p] > 0:
                    pw.append((s_t, pn_read_guard[p]))
                ctx.emit("scalar", lambda e, mw=mw, p=p: e.activation(
                    pn_sb[:, p, 0:mw], e_sb[:, p, 0:mw], AF.Identity, scale=rcp[:, p:p + 1]),
                    waits=pw)
                pn_ready = s_s.c
                for j in range(len(scs)):
                    twl = []
                    if j == 0:
                        twl.append((s_s, pn_ready))
                        if pt_guard[0] > 0:
                            twl.append((s_v, pt_guard[0]))
                    ctx.emit("tensor", lambda e, j=j, p=p: e.transpose(
                        ps_pt[0][:, j * 128:(j + 1) * 128], pn_sb[:, p, j * 128:(j + 1) * 128],
                        c["ident"][:, :]), waits=twl, inc=(s_t if j == len(scs) - 1 else None))
                ctx.emit("vector", lambda e, p=p, ns=len(scs): e.tensor_copy(
                    pT_sb[:, p, 0:ns, :], ps_pt[0][:, 0:ns * 128].rearrange(
                        "q (c n) -> q c n", n=128)), waits=[s_t.at()])
                pt_guard[0] = s_v.c
                pn_read_guard[p] = s_t.c
                wl2 = [(s_v, s_v.c)]
                if hd["pos"] == 0 and y_guard[0] is not None:
                    wl2.append(y_guard[0])
                for j, sc in enumerate(scs):
                    ctx.emit("tensor", lambda e, sc=sc, j=j, hd=hd, p=p,
                             last=(j == len(scs) - 1): e.matmul(
                        ps_y[hd["po"]:hd["po"] + 64, :],
                        V_all[:, sc, hd["kvh"] * 64:(hd["kvh"] + 1) * 64],
                        pT_sb[:, p, j, :], start=(j == 0), stop=last),
                        waits=wl2 if j == 0 else [], inc=(s_t if j == len(scs) - 1 else None))
                if hd["pos"] == 1:
                    ywl = [s_t.at()]
                    if ywo_guard[0] > 0 and hi == 1:
                        ywl.append((s_t, ywo_guard[0]))
                    ctx.emit("scalar", lambda e, pr=hd["pr"]: e.activation(
                        Y_sb[:, pr, :], ps_y[:, :], AF.Identity), waits=ywl)
                    y_guard[0] = (s_s, s_s.c)
            wl3 = [s_s.at()]
            for half in range(2):
                wlh = list(wl3) if half == 0 else []
                if wo_guard[0] is not None:
                    wlh.append(wo_guard[0])
                for mq in range(4):
                    mo = half * 4 + mq
                    for kh in range(8):
                        ctx.emit("tensor", lambda e, mo=mo, mq=mq, kh=kh: e.matmul(
                            ps_wo[0][:, mq * 128:(mq + 1) * 128],
                            c["Wo1"][:, kh, mo * 128:(mo + 1) * 128], Y_sb[:, kh, :],
                            start=(kh == 0), stop=(kh == 7)),
                            waits=wlh if (mq == 0 and kh == 0) else [],
                            inc=(s_t if (mq == 3 and kh == 7) else None))
                ewl = [s_t.at()]
                if t == 0 and half == 0 and s1_st_guard[0] > 0:
                    ewl.append((s_st, s1_st_guard[0]))
                ctx.emit("scalar", lambda e, half=half, t=t: e.activation(
                    s1_stage[:, half * 4:half * 4 + 4, t * 128:(t + 1) * 128],
                    ps_wo[0][:, :].rearrange("p (c n) -> p c n", c=4), AF.Identity),
                    waits=ewl)
                wo_guard[0] = (s_s, s_s.c)
            ywo_guard[0] = s_t.c
        ctx.emit("sync", lambda e, ch=ch: e.dma_start(
            ctx.dram["s1_feat"].ap()[:, :, ch * 512:(ch + 1) * 512], s1_stage[:, :, :]),
            waits=[s_s.at()], inc=s_st, inc_n=16)
        s1_st_guard[0] = s_st.c
    ctx.barrier()
    es.close()
    ctx.scope["de"].close()


def phase_l1_ffn(ctx):
    c = ctx.consts
    s_lda, s_st = ctx.sems["s_lda"], ctx.sems["s_st"]
    s_v, s_s, s_t = ctx.sems["s_v"], ctx.sems["s_s"], ctx.sems["s_t"]
    s_g = ctx.sems["s_g"]
    sw0, sw1 = ctx.sems["s_wb0"], ctx.sems["s_wb1"]
    es = contextlib.ExitStack()
    W1b = ctx.sb(es, "f_W1b", [128, 8, HID1], BF16)
    W2b = ctx.sb(es, "f_W2b", [128, 32, D], BF16)
    xn1c = ctx.sb(es, "f_xn1c", [128, 8, 512], BF16)
    hbq = ctx.sb(es, "f_hbq", [128, 16, 512], BF16)
    g1_sb = ctx.sb(es, "f_g1", [128, 8, 512], F32)
    g1b = ctx.sb(es, "f_g1b", [128, 8, 512], BF16)
    hq = ctx.sb(es, "f_hq", [128, 2, 512], F32)
    vv = ctx.sb(es, "f_vv", [128, 2, 512], F32)
    va = ctx.sb(es, "f_va", [128, 2, 512], F32)
    sn = ctx.sb(es, "f_sn", [128, 2, 512], F32)
    ps_h = [ctx.ps(es, f"f_ps_h{i}", [128, 512], F32) for i in range(3)]
    ps_o = [ctx.ps(es, f"f_ps_o{i}", [128, 512], F32) for i in range(2)]

    ctx.emit("gpsimd", lambda e: e.dma_start(
        W1b[:, :, :], rearr_feat_w(ctx.dram["L1_ffn_W1"].ap())), inc=sw0, inc_n=16)
    ctx.emit("gpsimd", lambda e: e.dma_start(
        W2b[:, :, :], rearr_feat_w(ctx.dram["L1_ffn_W2"].ap())), inc=sw1, inc_n=16)
    w1w, w2w = [sw0.at()], [sw1.at()]

    hband_guard = [None, None, None]
    oband_guard = [None, None]
    g1_st_guard = [0]
    hq_guard = [None, None]
    fold_done = [0, 0]
    sin_done = [0, 0]
    hbq_guard = [0]
    for ch in range(NCHUNK):
        ctx.emit("sync", lambda e, ch=ch: e.dma_start(
            xn1c[:, :, :], ctx.dram["xn1_feat"].ap()[:, :, ch * 512:(ch + 1) * 512]),
            waits=[s_t.at(), s_v.at(), s_s.at()], inc=s_lda, inc_n=16)
        wld = [s_lda.at()]
        for half in range(2):
            def emit_sin(mm):
                p2 = (half * 16 + mm) % 2
                ctx.emit("scalar", lambda e, p2=p2: e.activation(
                    sn[:, p2, :], vv[:, p2, :], AF.Sin, scale=TWOPI), waits=[(s_v, fold_done[p2])])
                sin_done[p2] = s_s.c

            def emit_tail(mm):
                p2 = (half * 16 + mm) % 2
                mg2 = half * 16 + mm
                ctx.emit("scalar", lambda e, mg2=mg2, p2=p2: e.activation(
                    sn[:, p2, :], sn[:, p2, :], AF.Identity, scale=c["Cv"][:, mg2:mg2 + 1]))
                ctx.emit("vector", lambda e, p2=p2: e.tensor_tensor(
                    sn[:, p2, :], hq[:, p2, :], sn[:, p2, :], ALU.add), waits=[s_s.at()])
                hwl = []
                if mm == 0 and hbq_guard[0] > 0:
                    hwl.append((s_t, hbq_guard[0]))
                ctx.emit("vector", lambda e, mm=mm, p2=p2: e.tensor_single_scalar(
                    hbq[:, mm, :], sn[:, p2, :], 0.0, ALU.max), waits=hwl)
                hq_guard[p2] = (s_v, s_v.c)

            for m in range(16):
                mg = half * 16 + m
                p = mg % 2
                bank = m % 3
                wl = list(wld) + w1w if (half == 0 and m == 0) else []
                if hband_guard[bank] is not None:
                    wl.append(hband_guard[bank])
                for k in range(8):
                    ctx.emit("tensor", lambda e, m=m, mg=mg, k=k, bank=bank: e.matmul(
                        ps_h[bank][:, :], W1b[:, k, mg * 128:(mg + 1) * 128], xn1c[:, k, :],
                        start=(k == 0), stop=(k == 7)), waits=wl if k == 0 else [],
                        inc=(s_t if k == 7 else None))
                hqw = [s_t.at()]
                if hq_guard[p] is not None:
                    hqw.append(hq_guard[p])
                ctx.emit("scalar", lambda e, mg=mg, bank=bank, p=p: e.activation(
                    hq[:, p, :], ps_h[bank][:, :], AF.Identity, bias=c["b1v"][:, mg:mg + 1]),
                    waits=hqw)
                ctx.emit("scalar", lambda e, mg=mg, bank=bank, p=p: e.activation(
                    vv[:, p, :], ps_h[bank][:, :], AF.Identity, scale=0.1 * INV2PI,
                    bias=c["qbias"][:, mg:mg + 1]))
                hband_guard[bank] = (s_s, s_s.c)
                ctx.emit("vector", lambda e, p=p: e.tensor_single_scalar(
                    va[:, p, :], vv[:, p, :], 0.5, ALU.is_gt), waits=[s_s.at()])
                ctx.emit("vector", lambda e, p=p: e.scalar_tensor_tensor(
                    vv[:, p, :], va[:, p, :], -1.0, vv[:, p, :], ALU.mult, ALU.add))
                ctx.emit("vector", lambda e, p=p: e.tensor_single_scalar(
                    va[:, p, :], vv[:, p, :], -0.5, ALU.is_lt))
                ctx.emit("vector", lambda e, p=p: e.scalar_tensor_tensor(
                    vv[:, p, :], va[:, p, :], 1.0, vv[:, p, :], ALU.mult, ALU.add))
                fold_done[p] = s_v.c
                if m > 0:
                    emit_sin(m - 1)
                    emit_tail(m - 1)
            emit_sin(15)
            emit_tail(15)
            whb = [s_v.at()]
            for mo in range(8):
                bank = mo % 2
                wl = list(whb) + w2w if mo == 0 else []
                if oband_guard[bank] is not None:
                    wl.append(oband_guard[bank])
                for kh in range(16):
                    khg = half * 16 + kh
                    ctx.emit("tensor", lambda e, mo=mo, kh=kh, khg=khg, bank=bank: e.matmul(
                        ps_o[bank][:, :], W2b[:, khg, mo * 128:(mo + 1) * 128], hbq[:, kh, :],
                        start=(kh == 0), stop=(kh == 15)), waits=wl if kh == 0 else [],
                        inc=(s_t if kh == 15 else None))
                if half == 0:
                    ctx.emit("scalar", lambda e, mo=mo, bank=bank: e.activation(
                        g1_sb[:, mo, :], ps_o[bank][:, :], AF.Identity,
                        bias=c["b2v"][:, mo:mo + 1]), waits=[s_t.at()])
                    oband_guard[bank] = (s_s, s_s.c)
                else:
                    ctx.emit("vector", lambda e, mo=mo, bank=bank: e.tensor_tensor(
                        g1_sb[:, mo, :], g1_sb[:, mo, :], ps_o[bank][:, :], ALU.add),
                        waits=[s_t.at()])
                    oband_guard[bank] = (s_v, s_v.c)
            hbq_guard[0] = s_t.c
        gwl = [s_v.at()]
        if g1_st_guard[0] > 0:
            gwl.append((s_st, g1_st_guard[0]))
        for mo in range(8):
            ctx.emit("scalar", lambda e, mo=mo: e.activation(
                g1b[:, mo, :], g1_sb[:, mo, :], AF.Silu), waits=gwl if mo == 0 else [])
        ctx.emit("sync", lambda e, ch=ch: e.dma_start(
            ctx.dram["g1_feat"].ap()[:, :, ch * 512:(ch + 1) * 512], g1b[:, :, :]),
            waits=[s_s.at()], inc=s_st, inc_n=16)
        g1_st_guard[0] = s_st.c
    ctx.barrier()
    es.close()


def phase_final(ctx):
    c = ctx.consts
    s_lda, s_st = ctx.sems["s_lda"], ctx.sems["s_st"]
    s_v, s_s, s_t = ctx.sems["s_v"], ctx.sems["s_s"], ctx.sems["s_t"]
    es = contextlib.ExitStack()
    NQ, QT, NT8 = 4, 1024, 8
    x_tok = ctx.sb(es, "o_x_tok", [128, NT8, D], F32)
    s_sb = ctx.sb(es, "o_s", [128, 8, QT], BF16)
    g_sb = ctx.sb(es, "o_g", [128, 8, QT], BF16)
    delta_f = ctx.sb(es, "o_delta_f", [128, 8, QT], BF16)
    delta_t = ctx.sb(es, "o_delta_t", [128, NT8, D], BF16)
    ps_tr = [ctx.ps(es, "o_ps_trA", [128, 128], BF16), ctx.ps(es, "o_ps_trB", [128, 128], BF16)]
    emit_tr = ctx.make_transposer(ps_tr, c["ident"])
    st_guard = [0]
    for q in range(NQ):
        ctx.emit("sync", lambda e, q=q: e.dma_start(
            x_tok[:, :, :], ctx.dram["x1m"].ap()[q * QT:(q + 1) * QT, :].rearrange(
                "(t p) d -> p t d", p=128)),
            waits=[s_t.at(), s_v.at()] + ([(s_st, st_guard[0])] if st_guard[0] > 0 else []),
            inc=s_lda, inc_n=16)
        ctx.emit("sync", lambda e, q=q: e.dma_start(
            s_sb[:, :, :], ctx.dram["s1_feat"].ap()[:, :, q * QT:(q + 1) * QT]), inc=s_lda, inc_n=16)
        ctx.emit("sync", lambda e, q=q: e.dma_start(
            g_sb[:, :, :], ctx.dram["g1_feat"].ap()[:, :, q * QT:(q + 1) * QT]), inc=s_lda, inc_n=16)
        wld = [s_lda.at()]
        for kc in range(8):
            ctx.emit("vector", lambda e, kc=kc: e.tensor_tensor(
                delta_f[:, kc, :], s_sb[:, kc, :], g_sb[:, kc, :], ALU.mult),
                waits=wld if kc == 0 else [])
        wdf = [s_v.at()]
        for t in range(NT8):
            for kc in range(8):
                emit_tr(delta_f[:, kc, t * 128:(t + 1) * 128],
                        delta_t[:, t, kc * 128:(kc + 1) * 128],
                        tw=(wdf if (t == 0 and kc == 0) else []))
        for t in range(NT8):
            ctx.emit("vector", lambda e, t=t: e.tensor_tensor(
                x_tok[:, t, :], x_tok[:, t, :], delta_t[:, t, :], ALU.add))
        ctx.emit("sync", lambda e, q=q: e.dma_start(
            ctx.dram["out"].ap()[q * QT:(q + 1) * QT, :].rearrange("(t p) d -> p t d", p=128),
            x_tok[:, :, :]), waits=[s_v.at()], inc=s_st, inc_n=16)
        st_guard[0] = s_st.c
    ctx.barrier()
    es.close()


def phase_dbg_a(ctx):
    s_v, s_st = ctx.sems["s_v"], ctx.sems["s_st"]
    es = contextlib.ExitStack()
    ctx.emit("sync", lambda e: e.dma_start(ctx.dram["kv_dbg"].ap(), ctx.consts["kv_acc"][:, :]),
             inc=s_st, inc_n=16)
    ctx.emit("sync", lambda e: e.dma_start(ctx.dram["wtok_dbg"].ap(), ctx.consts["w_tok"][:, :, :]),
             inc=s_st, inc_n=16)
    ctx.barrier()
    es.close()


DRAM_SPECS = {
    "kv_dbg": ([128, 512], F32),
    "xr_feat": ([128, 8, T], BF16),
    "q_feat": ([128, 8, T], BF16),
    "u_feat": ([128, 8, T], BF16),
    "s_feat": ([128, 8, T], BF16),
    "x1m": ([T, D], F32),
    "g_feat": ([128, 8, T], BF16),
    "xn1_feat": ([128, 8, T], BF16),
    "s1_feat": ([128, 8, T], BF16),
    "g1_feat": ([128, 8, T], BF16),
    "wtok_dbg": ([128, 32, 14], F32),
}

INPUT_SPECS = {
    "L0_attn_x": ([T, D], F32),
    "L0_attn_Wq": ([D, D], F32), "L0_attn_Wk": ([D, D], F32), "L0_attn_Wv": ([D, D], F32),
    "L0_attn_Wu": ([D, D], F32), "L0_attn_Wo": ([D, D], F32),
    "L0_ffn_router_w": ([D, E_ROUTED], F32),
    "L0_ffn_rms_w": ([E_ROUTED, D], F32),
    "L0_ffn_w1": ([E_ROUTED, D, HID0], F32), "L0_ffn_w2": ([E_ROUTED, HID0, D], F32),
    "L0_ffn_rms_w_s": ([E_SHARED, D], F32),
    "L0_ffn_w1_s": ([E_SHARED, D, HID0], F32), "L0_ffn_w2_s": ([E_SHARED, HID0, D], F32),
    "L1_attn_Wq": ([D, D], F32), "L1_attn_Wk": ([D, 256], F32), "L1_attn_Wv": ([D, 256], F32),
    "L1_attn_Wo": ([D, D], F32),
    "L1_ffn_W1": ([D, HID1], F32), "L1_ffn_b1": ([HID1], F32),
    "L1_ffn_quantum_phases": ([HID1], F32),
    "L1_ffn_quantum_weights_real": ([HID1], F32),
    "L1_ffn_quantum_weights_imag": ([HID1], F32),
    "L1_ffn_W2": ([HID1, D], F32), "L1_ffn_b2": ([D], F32),
}


def build_program(phases, expose=(), scratch_inputs=()):
    nc = bass.Bass("TRN2", target_bir_lowering=False)
    ctx = Ctx(nc)
    for name, (shape, dt) in INPUT_SPECS.items():
        ctx.dram[name] = nc.dram_tensor(name, shape, dt, kind="ExternalInput")
    ctx.dram["out"] = nc.dram_tensor("out", [T, D], F32, kind="ExternalOutput")
    for name, (shape, dt) in DRAM_SPECS.items():
        kind = ("ExternalInput" if name in scratch_inputs else
                ("ExternalOutput" if name in expose else "Internal"))
        ctx.dram[name] = nc.dram_tensor(name, shape, dt, kind=kind)
    with ctx.es_global:
        setup_consts(ctx)
        for sc in ("ab", "ac", "de"):
            ctx.scope[sc] = ctx.es_global.enter_context(contextlib.ExitStack())
        for ph in phases:
            ph(ctx)
        ctx.replay()
    return ctx


ALL_PHASES = None


def kernel(**inputs):
    x_full = np.ascontiguousarray(np.asarray(inputs["L0_attn_x"]))
    ctx = build_program([setup_l0_weights, phase_l0_a, phase_l0_b, phase_l0_moe, phase_r0,
                         setup_l1_weights, setup_l1_masks, phase_l1_kv, phase_l1_attn,
                         phase_l1_ffn, phase_final])
    weights = {k: np.ascontiguousarray(np.asarray(v)) for k, v in inputs.items()
               if k != "L0_attn_x"}
    in_maps = []
    for core in range(8):
        m = dict(weights)
        m["L0_attn_x"] = np.ascontiguousarray(x_full[core])
        in_maps.append(m)
    res = run_bass_kernel_spmd(ctx.nc, in_maps, core_ids=list(range(8)))
    return np.stack([r["out"] for r in res.results], axis=0)


# version 07
# speedup vs baseline: 1.1134x; 1.0545x over previous
import contextlib
import math
import numpy as np

import concourse.bass as bass
import concourse.mybir as mybir
from concourse.bass_utils import run_bass_kernel_spmd

F32 = mybir.dt.float32
BF16 = mybir.dt.bfloat16
I32 = mybir.dt.int32
AF = mybir.ActivationFunctionType
ALU = mybir.AluOpType
AX = mybir.AxisListType

ENGINES = ("sync", "gpsimd", "vector", "scalar", "tensor")

T = 4096
D = 1024
NCHUNK = 8
NTILE = 4
H0 = 16
HD = 64
E_ROUTED = 14
E_SHARED = 2
HID0 = 2048
RMS_EPS = 1.1920929e-07
LN_EPS = 1e-5
MOE_EPS = 1e-9
N_HEADS1 = 16
N_KV1 = 4
SINK = 4
WIN = 256
HID1 = 4096
INV2PI = 1.0 / (2.0 * math.pi)
TWOPI = 2.0 * math.pi


class Sem:
    def __init__(self, nc, es, name):
        self.h = es.enter_context(nc.semaphore(name))
        self.c = 0

    def at(self):
        return (self, self.c)


class Ctx:
    def __init__(self, nc):
        self.nc = nc
        self.steps = []
        self.es_global = contextlib.ExitStack()
        self.dram = {}
        self.nbar = 0
        self.sems = {}
        self.consts = {}
        self.scope = {}

    def sem(self, name):
        s = Sem(self.nc, self.es_global, name)
        self.sems[name] = s
        return s

    def sb(self, es, name, shape, dtype, side=None):
        return es.enter_context(self.nc.sbuf_tensor(name, shape, dtype, side=side))

    def ps(self, es, name, shape, dtype=F32, side=None):
        return es.enter_context(self.nc.psum_tensor(name, shape, dtype, side=side))

    CHAINED = {"vector": "s_v", "scalar": "s_s", "gpsimd": "s_g"}

    def emit(self, eng, fn, waits=(), inc=None, inc_n=1, chain=True):
        w = [(s.h, v) for (s, v) in waits if v > 0]
        extra = []
        if inc_n >= 16:
            chain = False
        if fn is not None and eng in self.CHAINED and chain:
            se = self.sems[self.CHAINED[eng]]
            if inc is se:
                inc = None
            assert inc is None, f"chained {eng} op cannot carry another inc"
            if se.c > 0:
                w.append((se.h, se.c))
            extra.append(se)
            se.c += 1
        self.steps.append((eng, fn, w, inc.h if inc is not None else None, inc_n, extra))
        if inc is not None:
            inc.c += inc_n

    def make_transposer(self, ps_pair, ident):
        state = {"i": 0, "guard": [0, 0]}
        s_v, s_t = self.sems["s_v"], self.sems["s_t"]

        def emit_tr(src_ap, dst_ap, tw=(), cw=()):
            par = state["i"] % 2
            state["i"] += 1
            twl = list(tw)
            if state["guard"][par] > 0:
                twl.append((s_v, state["guard"][par]))
            self.emit("tensor", lambda e, par=par, src_ap=src_ap: e.transpose(
                ps_pair[par][:, :], src_ap, ident[:, :]), waits=twl, inc=s_t)
            cwl = list(cw) + [s_t.at()]
            self.emit("vector", lambda e, par=par, dst_ap=dst_ap: e.tensor_copy(
                dst_ap, ps_pair[par][:, :]), waits=cwl)
            state["guard"][par] = s_v.c
        return emit_tr

    def barrier(self):
        bar = self.sems["bar"]
        dma_waits = [(self.sems[n], self.sems[n].c)
                     for n in ("s_ldw", "s_lda", "s_st", "s_lds", "s_wb0", "s_wb1",
                               "s_wb2", "s_wfl") if n in self.sems]
        self.nbar += 1
        tgt = 5 * self.nbar
        bps, bsb, bsb2 = self.consts["barbufs"]
        self.emit("sync", lambda e: e.nop(), waits=dma_waits, inc=bar)
        self.emit("gpsimd", lambda e: e.memset(bsb[0:1, 0:1], 0),
                  waits=dma_waits + [self.sems["s_g"].at()], inc=bar, chain=False)
        self.emit("vector", lambda e: e.memset(bsb[0:1, 1:2], 0),
                  waits=[self.sems["s_v"].at()], inc=bar, chain=False)
        self.emit("scalar", lambda e: e.copy(bsb[0:1, 2:3], bsb[0:1, 3:4]),
                  waits=[self.sems["s_s"].at(), self.sems["s_g"].at()], inc=bar, chain=False)
        self.emit("tensor", lambda e: e.matmul(bps[0:1, 0:1], bsb2[0:1, 0:1], bsb2[0:1, 0:1],
                                               start=True, stop=True),
                  waits=[self.sems["s_g"].at(), self.sems["s_v"].at(),
                         self.sems["s_s"].at()], inc=bar)
        for eng in ENGINES:
            self.emit(eng, None, waits=[(bar, tgt)])

    def replay(self):
        with self.nc.Block() as block:
            for eng_name in ENGINES:
                def make_body(eng_name=eng_name):
                    def body(eng):
                        for (e, fn, w, inch, incn, extra) in self.steps:
                            if e != eng_name:
                                continue
                            for (h, v) in w:
                                eng.wait_ge(h, v)
                            if fn is not None:
                                inst = fn(eng)
                                if inch is not None:
                                    assert inst is not None, "inc on step with no inst"
                                    inst.then_inc(inch, incn)
                                for s in extra:
                                    assert inst is not None
                                    inst.then_inc(s.h, 1)
                    return body
                getattr(block, eng_name)(make_body())


def rearr_tok(ap, ntile=NTILE):
    return ap.rearrange("(t p) d -> p t d", p=128)


def rearr_feat_w(ap):
    return ap.rearrange("(c p) n -> p c n", p=128)



def setup_consts(ctx):
    nc = ctx.nc
    es = ctx.es_global
    c = ctx.consts
    bps = ctx.ps(es, "bar_ps", [1, 1], F32)
    bsb = ctx.sb(es, "bar_sb", [1, 8], F32)
    bsb2 = ctx.sb(es, "bar_sb2", [1, 1], BF16)
    c["barbufs"] = (bps, bsb, bsb2)

    s_ldw = ctx.sem("s_ldw")
    s_lda = ctx.sem("s_lda")
    s_st = ctx.sem("s_st")
    s_lds = ctx.sem("s_lds")
    for i in range(3):
        ctx.sem(f"s_wb{i}")
    ctx.sem("s_wfl")
    ctx.sem("bar")
    s_v = ctx.sem("s_v")
    s_s = ctx.sem("s_s")
    s_t = ctx.sem("s_t")
    s_g = ctx.sem("s_g")

    ctx.emit("gpsimd", lambda e: e.memset(bsb[0:1, :], 0.0))
    ctx.emit("gpsimd", lambda e: e.memset(bsb2[0:1, :], 0.0))

    es_tmp = contextlib.ExitStack()
    ident = ctx.sb(es, "ident", [128, 128], BF16)
    iota_i = ctx.sb(es_tmp, "iota_i", [128, 512], I32, side="right")
    iota_f = ctx.sb(es_tmp, "iota_f", [128, 512], F32, side="right")
    jcol_f = ctx.sb(es_tmp, "jcol_f", [128, 512], F32, side="right")
    ones_k1 = ctx.sb(es, "ones_k1", [1, 128], BF16)
    c["ident"] = ident
    c["ones_k1"] = ones_k1

    identF = ctx.sb(es, "identF", [128, 128], F32)
    c["identF"] = identF
    ctx.emit("gpsimd", lambda e: e.iota(iota_i[:, 0:128], [[1, 128]], channel_multiplier=-1), inc=s_g)
    ctx.emit("vector", lambda e: e.tensor_single_scalar(ident[:, :], iota_i[:, 0:128], 0, ALU.is_equal),
             waits=[s_g.at()], inc=s_v)
    ctx.emit("vector", lambda e: e.tensor_copy(identF[:, :], ident[:, :]))
    ctx.emit("gpsimd", lambda e: e.memset(ones_k1[:, :], 1.0), inc=s_g)

    blk8 = ctx.sb(es, "blk8", [128, 8, 16], BF16)
    sel8 = ctx.sb(es, "sel8", [16, 8, 128], BF16)
    blkI = ctx.sb(es_tmp, "blkI", [128, 8, 16], BF16, side="right")
    c["blk8"] = blk8
    c["sel8"] = sel8
    ctx.emit("gpsimd", lambda e: e.memset(blk8[:, :, :], 0.0), inc=s_g)
    ctx.emit("gpsimd", lambda e: e.memset(blkI[:, :, :], 0.0), inc=s_g)
    wz = [s_g.at()]
    for ec in range(8):
        ctx.emit("gpsimd", lambda e, ec=ec: e.memset(blk8[0:64, ec, 2 * ec:2 * ec + 1], 1.0 / 64),
                 waits=wz if ec == 0 else [])
        ctx.emit("gpsimd", lambda e, ec=ec: e.memset(blk8[64:128, ec, 2 * ec + 1:2 * ec + 2], 1.0 / 64))
        ctx.emit("gpsimd", lambda e, ec=ec: e.memset(blkI[0:64, ec, 2 * ec:2 * ec + 1], 1.0))
        ctx.emit("gpsimd", lambda e, ec=ec: e.memset(blkI[64:128, ec, 2 * ec + 1:2 * ec + 2], 1.0), inc=s_g)
    ps_tr16 = ctx.ps(es_tmp, "ps_tr16", [16, 8, 128], BF16)
    for ec in range(8):
        ctx.emit("tensor", lambda e, ec=ec: e.transpose(ps_tr16[:, ec, :], blkI[:, ec, 0:16], ident[:, :]),
                 waits=[s_g.at(), s_v.at()], inc=s_t)
    ctx.emit("vector", lambda e: e.tensor_copy(sel8[:, :, :], ps_tr16[:, :, :]), waits=[s_t.at()], inc=s_v)

    router_sb = ctx.sb(es, "router_sb", [128, 8, 14], F32)
    c["router_sb"] = router_sb
    ctx.emit("gpsimd", lambda e: e.dma_start(router_sb[:, :, :],
                                             rearr_feat_w(ctx.dram["L0_ffn_router_w"].ap())), inc=s_ldw, inc_n=16)

    w_tok = ctx.sb(es, "w_tok", [128, 32, 14], F32)
    c["w_tok"] = w_tok
    cLN = ctx.sb(es, "cLN", [128, 1], F32)
    cRMS = ctx.sb(es, "cRMS", [128, 1], F32)
    c["cLN"], c["cRMS"] = cLN, cRMS
    ctx.emit("gpsimd", lambda e: e.memset(cLN[:, :], LN_EPS))
    ctx.emit("gpsimd", lambda e: e.memset(cRMS[:, :], RMS_EPS), inc=s_g)

    rms_all = ctx.sb(es, "rms_all", [128, 16, 8], F32)
    c["rms_all"] = rms_all
    for e_i in range(E_ROUTED):
        src = ctx.dram["L0_ffn_rms_w"].ap()[e_i:e_i + 1, :].rearrange("q (c p) -> q p c", p=128)
        ctx.emit("gpsimd", lambda e, e_i=e_i, src=src: e.dma_start(
            rms_all[:, e_i, :], src, allow_slow_non_contiguous=True), inc=s_ldw, inc_n=16)
    for e_i in range(E_SHARED):
        src = ctx.dram["L0_ffn_rms_w_s"].ap()[e_i:e_i + 1, :].rearrange("q (c p) -> q p c", p=128)
        ctx.emit("gpsimd", lambda e, e_i=e_i, src=src: e.dma_start(
            rms_all[:, E_ROUTED + e_i, :], src, allow_slow_non_contiguous=True), inc=s_ldw, inc_n=16)

    b1v = ctx.sb(es, "b1v", [128, 32], F32)
    thv = ctx.sb(es_tmp, "thv", [128, 32], F32, side="right")
    wrv = ctx.sb(es_tmp, "wrv", [128, 32], F32, side="right")
    wiv = ctx.sb(es_tmp, "wiv", [128, 32], F32, side="right")
    b2v = ctx.sb(es, "b2v", [128, 8], F32)
    qbias = ctx.sb(es, "qbias", [128, 32], F32)
    Cv = ctx.sb(es, "Cv", [128, 32], F32)
    c["b1v"] = b1v
    c["qbias"] = qbias
    c["Cv"] = Cv
    c["b2v"] = b2v
    for (nm, dst, n) in [("L1_ffn_b1", b1v, 32), ("L1_ffn_quantum_phases", thv, 32),
                         ("L1_ffn_quantum_weights_real", wrv, 32),
                         ("L1_ffn_quantum_weights_imag", wiv, 32), ("L1_ffn_b2", b2v, 8)]:
        for m in range(n):
            ctx.emit("scalar", lambda e, nm=nm, dst=dst, m=m: e.dma_start(
                dst[:, m:m + 1], ctx.dram[nm].ap()[m * 128:(m + 1) * 128].rearrange("(p q) -> p q", q=1)),
                inc=s_lds, inc_n=16)
    t1 = ctx.sb(es_tmp, "qc_t1", [128, 32], F32, side="right")
    t2 = ctx.sb(es_tmp, "qc_t2", [128, 32], F32, side="right")
    t3i = ctx.sb(es_tmp, "qc_t3i", [128, 32], I32, side="right")
    sth = ctx.sb(es_tmp, "qc_sth", [128, 32], F32, side="right")
    cth = ctx.sb(es_tmp, "qc_cth", [128, 32], F32, side="right")
    w0 = [(s_lds, s_lds.c)]

    def fold_full(buf):
        ctx.emit("vector", lambda e: e.tensor_copy(t3i[:, :], buf[:, :]))
        ctx.emit("vector", lambda e: e.tensor_copy(t2[:, :], t3i[:, :]))
        ctx.emit("vector", lambda e: e.tensor_tensor(buf[:, :], buf[:, :], t2[:, :], ALU.subtract))
        ctx.emit("vector", lambda e: e.tensor_single_scalar(t2[:, :], buf[:, :], 0.5, ALU.is_gt))
        ctx.emit("vector", lambda e: e.scalar_tensor_tensor(buf[:, :], t2[:, :], -1.0, buf[:, :], ALU.mult, ALU.add))
        ctx.emit("vector", lambda e: e.tensor_single_scalar(t2[:, :], buf[:, :], -0.5, ALU.is_lt))
        ctx.emit("vector", lambda e: e.scalar_tensor_tensor(buf[:, :], t2[:, :], 1.0, buf[:, :], ALU.mult, ALU.add))

    ctx.emit("vector", lambda e: e.scalar_tensor_tensor(qbias[:, :], b1v[:, :], 0.1, thv[:, :], ALU.mult, ALU.add), waits=w0)
    ctx.emit("vector", lambda e: e.tensor_single_scalar(qbias[:, :], qbias[:, :], INV2PI, ALU.mult))
    fold_full(qbias)
    ctx.emit("vector", lambda e: e.tensor_single_scalar(t1[:, :], thv[:, :], INV2PI, ALU.mult))
    fold_full(t1)
    ctx.emit("scalar", lambda e: e.activation(sth[:, :], t1[:, :], AF.Sin, scale=TWOPI), waits=[s_v.at()])
    ctx.emit("vector", lambda e: e.tensor_scalar(t1[:, :], thv[:, :], INV2PI, 0.25, ALU.mult, ALU.add),
             waits=[s_s.at()])
    fold_full(t1)
    ctx.emit("scalar", lambda e: e.activation(cth[:, :], t1[:, :], AF.Sin, scale=TWOPI), waits=[s_v.at()])
    ctx.emit("vector", lambda e: e.tensor_tensor(t1[:, :], cth[:, :], wrv[:, :], ALU.mult), waits=[s_s.at()])
    ctx.emit("vector", lambda e: e.tensor_tensor(t2[:, :], sth[:, :], wiv[:, :], ALU.mult))
    ctx.emit("vector", lambda e: e.tensor_tensor(t1[:, :], t1[:, :], t2[:, :], ALU.add))
    ctx.emit("vector", lambda e: e.scalar_tensor_tensor(Cv[:, :], t1[:, :], 0.1, wiv[:, :], ALU.mult, ALU.mult))

    ctx.barrier()
    es_tmp.close()


def setup_l0_weights(ctx):
    c = ctx.consts
    s_ldw = ctx.sems["s_ldw"]
    s_v, s_t = ctx.sems["s_v"], ctx.sems["s_t"]
    ident = c["ident"]
    es_tmp = contextlib.ExitStack()
    es = ctx.scope["ab"]
    wnat = ctx.sb(es_tmp, "wnat_stage", [128, 8, 1024], BF16, side="right")
    ps_trw = [ctx.ps(es_tmp, "ps_trwA", [128, 128], BF16, side="right"),
              ctx.ps(es_tmp, "ps_trwB", [128, 128], BF16, side="right")]
    names = [("L0_attn_Wq", "WqT"), ("L0_attn_Wk", "WkT"), ("L0_attn_Wv", "WvT"),
             ("L0_attn_Wu", "WuT"), ("L0_attn_Wo", "WoT")]
    ntr = 0
    ncp = 0
    for (src_name, dst_name) in names:
        dst = ctx.sb(es, dst_name, [128, 8, 1024], BF16)
        c[dst_name] = dst
        ctx.emit("gpsimd", lambda e, s=src_name: e.dma_start(
            wnat[:, :, :], rearr_feat_w(ctx.dram[s].ap())),
            waits=[s_t.at()], inc=s_ldw, inc_n=16)
        for ce in range(8):
            for kd in range(8):
                par = ntr % 2
                twl = []
                if ce == 0 and kd == 0:
                    twl.append((s_ldw, s_ldw.c))
                    twl.append((s_v, s_v.c))
                if ntr >= 2:
                    twl.append((s_v, ncp - 1))
                ctx.emit("tensor", lambda e, ce=ce, kd=kd, par=par: e.transpose(
                    ps_trw[par][:, :], wnat[:, ce, kd * 128:(kd + 1) * 128], ident[:, :]),
                    waits=twl, inc=s_t)
                ntr += 1
                ctx.emit("vector", lambda e, dst=dst, ce=ce, kd=kd, par=par: e.tensor_copy(
                    dst[:, kd, ce * 128:(ce + 1) * 128], ps_trw[par][:, :]),
                    waits=[(s_t, s_t.c)])
                ncp = s_v.c


    ctx.barrier()
    es_tmp.close()


def setup_l1_weights(ctx):
    c = ctx.consts
    es = ctx.scope["de"]
    s_ldw = ctx.sems["s_ldw"]
    for (nm, dst_name, ecols) in [("L1_attn_Wk", "Wk1", 256), ("L1_attn_Wv", "Wv1", 256)]:
        dst = ctx.sb(es, dst_name, [128, 8, ecols], BF16)
        c[dst_name] = dst
        ctx.emit("gpsimd", lambda e, nm=nm, dst=dst: e.dma_start(
            dst[:, :, :], rearr_feat_w(ctx.dram[nm].ap())), inc=s_ldw, inc_n=16)
    wq1 = ctx.sb(es, "Wq1", [128, 8, 1024], BF16)
    wo1 = ctx.sb(es, "Wo1", [128, 8, 1024], BF16)
    c["Wq1"], c["Wo1"] = wq1, wo1
    for s in range(8):
        grp, j = s // 4, s % 4
        for pos in range(2):
            h = grp * 8 + pos * 4 + j
            ctx.emit("gpsimd", lambda e, s=s, pos=pos, h=h: e.dma_start(
                wq1[:, :, s * 128 + pos * 64:s * 128 + pos * 64 + 64],
                ctx.dram["L1_attn_Wq"].ap()[:, h * 64:(h + 1) * 64].rearrange(
                    "(c p) e -> p c e", p=128)), inc=s_ldw, inc_n=16)
            ctx.emit("gpsimd", lambda e, s=s, pos=pos, h=h: e.dma_start(
                wo1[pos * 64:pos * 64 + 64, s, :],
                ctx.dram["L1_attn_Wo"].ap()[h * 64:(h + 1) * 64, :]), inc=s_ldw, inc_n=16)






def phase_l0_a(ctx):
    nc = ctx.nc
    es_ab = ctx.scope["ab"]
    c = ctx.consts
    s_lda, s_ldw, s_st = ctx.sems["s_lda"], ctx.sems["s_ldw"], ctx.sems["s_st"]
    s_v, s_s, s_t = ctx.sems["s_v"], ctx.sems["s_s"], ctx.sems["s_t"]

    kv_ps = ctx.ps(es_ab, "kv_ps", [128, 512], F32)
    kv_acc = ctx.sb(es_ab, "kv_acc", [128, 512], F32)
    ctx.consts["kv_ps"] = kv_ps
    ctx.consts["kv_acc"] = kv_acc
    es = contextlib.ExitStack()
    x_tok2 = ctx.sb(es, "a_x_tok", [128, 2, NTILE, D], F32)
    stats = ctx.sb(es, "a_stats", [128, 2, 6, NTILE], F32)
    xnt321 = ctx.sb(es, "a_xnt32", [128, 1, NTILE, D], F32)
    sq_buf = xnt321[:, 0, 0, :]
    xnf32 = ctx.sb(es, "a_xnf32", [128, 8, 128], F32)
    xnb_tok2 = ctx.sb(es, "a_xnb_tok", [128, 2, NTILE, D], BF16)
    rms4f = ctx.sb(es, "a_rms4f", [4, 128], F32)
    rmsr_b = ctx.sb(es, "a_rmsr", [1, 512], BF16)
    xnb_feat = ctx.sb(es, "a_xnb_feat", [128, 8, 512], BF16)
    xrb_feat = ctx.sb(es, "a_xrb_feat", [128, 8, 512], BF16)
    k_tok = ctx.sb(es, "a_k_tok", [128, NTILE, D], BF16)
    v_tok = ctx.sb(es, "a_v_tok", [128, NTILE, D], BF16)
    q_stage = ctx.sb(es, "a_q_stage", [128, 8, 512], BF16)
    u_stage = ctx.sb(es, "a_u_stage", [128, 8, 512], BF16)
    gexp = ctx.sb(es, "a_gexp", [128, 16], F32)
    gtmp = ctx.sb(es, "a_gtmp", [128, 16], F32)
    gkeep = ctx.sb(es, "a_gkeep", [128, 16], F32)
    gstat = ctx.sb(es, "a_gstat", [128, 4], F32)
    w_tok = ctx.consts["w_tok"]

    ps_tr = [ctx.ps(es, "a_ps_trA", [128, 128], BF16),
             ctx.ps(es, "a_ps_trB", [128, 128], BF16)]
    ps_misc = ctx.ps(es, "a_ps_misc", [128, 512], F32)
    ps_proj = [ctx.ps(es, f"a_ps_proj{i}", [128, 512], F32) for i in range(3)]

    ident = c["ident"]
    emit_tr = ctx.make_transposer(ps_tr, ident)
    ntr = 0
    nproj = 0
    bank_guard = [None, None, None]
    qu_store_guard = [0]
    xr_store_guard = [0]
    gate_guard = [0]
    kv_evac_guard = [0]
    xnt_guard = [0]
    feat_guard = [0]
    rms_mm_guard = [0]
    ctx.emit("vector", lambda e: e.memset(kv_acc[:, :], 0.0))

    x_guard = [[], []]
    lda_chunk = [0] * NCHUNK
    for ch in range(NCHUNK):
        par = ch % 2
        x_tok = x_tok2[:, par]
        xnt32 = xnt321[:, 0]
        xnb_tok = xnb_tok2[:, par]
        ssq, ssm = stats[:, par, 0], stats[:, par, 1]
        mean, rstd = stats[:, par, 2], stats[:, par, 3]
        rmsf, tstat = stats[:, par, 4], stats[:, par, 5]
        xsrc = ctx.dram["L0_attn_x"].ap()[ch * 512:(ch + 1) * 512, :]
        ctx.emit("sync", lambda e, xsrc=xsrc, par=par: e.dma_start(
            x_tok2[:, par, :, :], rearr_tok(xsrc)),
            waits=x_guard[par], inc=s_lda, inc_n=16)
        lda_chunk[ch] = s_lda.c
        wx = [s_lda.at()]

        for t in range(NTILE):
            sqw = list(wx) if t == 0 else []
            if t == 0 and xnt_guard[0] > 0:
                sqw.append((s_t, xnt_guard[0]))
            ctx.emit("scalar", lambda e, t=t, x_tok=x_tok, ssq=ssq: e.activation(
                sq_buf, x_tok[:, t, :], AF.Square, accum_out=ssq[:, t:t + 1]),
                waits=sqw, inc=s_s)
            ctx.emit("vector", lambda e, t=t, x_tok=x_tok, ssm=ssm: e.tensor_reduce(
                ssm[:, t:t + 1], x_tok[:, t, :], AX.X, ALU.add), waits=wx if t == 0 else [], inc=s_v)
        ctx.emit("vector", lambda e, mean=mean, ssm=ssm: e.tensor_single_scalar(
            mean[:, :], ssm[:, :], 1.0 / D, ALU.mult), waits=[s_s.at(), s_v.at()])
        ctx.emit("vector", lambda e, tstat=tstat, mean=mean: e.tensor_tensor(
            tstat[:, :], mean[:, :], mean[:, :], ALU.mult))
        ctx.emit("vector", lambda e, tstat=tstat, ssq=ssq: e.scalar_tensor_tensor(
            tstat[:, :], ssq[:, :], 1.0 / D, tstat[:, :], ALU.mult, ALU.subtract), inc=s_v)
        ctx.emit("scalar", lambda e, rstd=rstd, tstat=tstat: e.activation(
            rstd[:, :], tstat[:, :], AF.Ln, bias=c["cLN"][:, :]), waits=[s_v.at()], inc=s_s)
        ctx.emit("scalar", lambda e, rstd=rstd: e.activation(
            rstd[:, :], rstd[:, :], AF.Exp, scale=-0.5), inc=s_s)
        ctx.emit("vector", lambda e, rmsf=rmsf, rstd=rstd: e.tensor_tensor(
            rmsf[:, :], rstd[:, :], rstd[:, :], ALU.mult), waits=[s_s.at()])
        ctx.emit("vector", lambda e, rmsf=rmsf, tstat=tstat: e.tensor_tensor(
            rmsf[:, :], rmsf[:, :], tstat[:, :], ALU.mult), inc=s_v)
        ctx.emit("scalar", lambda e, rmsf=rmsf: e.activation(
            rmsf[:, :], rmsf[:, :], AF.Ln, bias=c["cRMS"][:, :]), waits=[s_v.at()], inc=s_s)
        ctx.emit("scalar", lambda e, rmsf=rmsf: e.activation(
            rmsf[:, :], rmsf[:, :], AF.Exp, scale=-0.5), inc=s_s)
        wstat = [s_s.at()]

        for t in range(NTILE):
            wl = list(wstat) if t == 0 else []
            if t == 0 and xnt_guard[0] > 0:
                wl.append((s_t, xnt_guard[0]))
            ctx.emit("vector", lambda e, t=t, xnt32=xnt32, x_tok=x_tok, mean=mean, rstd=rstd: e.tensor_scalar(
                xnt32[:, t, :], x_tok[:, t, :], mean[:, t:t + 1], rstd[:, t:t + 1],
                ALU.subtract, ALU.mult), waits=wl)
            ctx.emit("vector", lambda e, t=t, xnb_tok=xnb_tok, xnt32=xnt32: e.tensor_copy(
                xnb_tok[:, t, :], xnt32[:, t, :]))
        wxn = [s_v.at()]
        x_guard[par] = [(s_v, s_v.c), (s_s, s_s.c)]
        for t in range(NTILE):
            for k in range(8):
                tw = wxn if (t == 0 and k == 0) else []
                cw = []
                if t == 0 and k == 0 and feat_guard[0] > 0:
                    cw = [(s_t, feat_guard[0])]
                emit_tr(xnb_tok[:, t, k * 128:(k + 1) * 128],
                        xnb_feat[:, k, t * 128:(t + 1) * 128], tw=tw, cw=cw)
        wfeat = [s_v.at()]

        ctx.emit("tensor", lambda e, rmsf=rmsf: e.transpose(
            ps_misc[0:4, 0:128], rmsf[:, 0:4], c["identF"][:, :]),
            waits=([(s_s, gate_guard[0])] if gate_guard[0] > 0 else []) + [(s_v, s_v.c)],
            inc=s_t)
        ctx.emit("vector", lambda e: e.tensor_copy(rms4f[:, :], ps_misc[0:4, 0:128]),
                 waits=[s_t.at(), (s_ldw, s_ldw.c)])
        for tt in range(NTILE):
            ctx.emit("gpsimd", lambda e, tt=tt: e.dma_start(
                rmsr_b[0:1, tt * 128:(tt + 1) * 128], rms4f[tt:tt + 1, :]),
                waits=([(s_v, s_v.c), (s_t, rms_mm_guard[0])] if tt == 0 else []),
                inc=s_ldw, inc_n=16)
        bankx = nproj % 3
        nproj += 1
        bwl = [(s_ldw, s_ldw.c)]
        if bank_guard[bankx]:
            bwl.append(bank_guard[bankx])
        ctx.emit("tensor", lambda e, bankx=bankx: e.matmul(
            ps_proj[bankx][:, :], c["ones_k1"][:, :], rmsr_b[0:1, :], start=True, stop=True),
            waits=bwl, inc=s_t)
        rms_mm_guard[0] = s_t.c
        xrw = [s_t.at()]
        for k in range(8):
            cwx = list(xrw) if k == 0 else []
            if k == 0 and xr_store_guard[0] > 0:
                cwx.append((s_st, xr_store_guard[0]))
            ctx.emit("vector", lambda e, k=k, bankx=bankx: e.tensor_tensor(
                xrb_feat[:, k, :], xnb_feat[:, k, :], ps_proj[bankx][:, :], ALU.mult),
                waits=cwx)
        bank_guard[bankx] = (s_v, s_v.c)

        ctx.emit("sync", lambda e, ch=ch: e.dma_start(
            ctx.dram["xr_feat"].ap()[:, :, ch * 512:(ch + 1) * 512], xrb_feat[:, :, :]),
            waits=[s_v.at()], inc=s_st, inc_n=16)

        for (wt, stage, act, scale) in ((c["WqT"], q_stage, AF.Relu, 0.125),
                                        (c["WuT"], u_stage, AF.Silu, 1.0)):
            for m in range(8):
                bank = nproj % 3
                wlist = list(wfeat) if m == 0 else []
                if bank_guard[bank]:
                    wlist.append(bank_guard[bank])
                if m == 0 and qu_store_guard[0] > 0:
                    pass
                for k in range(8):
                    ctx.emit("tensor", lambda e, wt=wt, m=m, k=k, bank=bank: e.matmul(
                        ps_proj[bank][:, :], wt[:, k, m * 128:(m + 1) * 128], xnb_feat[:, k, :],
                        start=(k == 0), stop=(k == 7)),
                        waits=wlist if k == 0 else [], inc=(s_t if k == 7 else None))
                nproj += 1
                ctx.emit("scalar", lambda e, stage=stage, m=m, act=act, scale=scale, bank=bank: e.activation(
                    stage[:, m, :], ps_proj[bank][:, :], act, scale=scale),
                    waits=[s_t.at()] + ([(s_st, qu_store_guard[0])] if (m == 0 and qu_store_guard[0] > 0) else []),
                    inc=s_s)
                bank_guard[bank] = (s_s, s_s.c)
        ctx.emit("sync", lambda e, ch=ch: e.dma_start(
            ctx.dram["q_feat"].ap()[:, :, ch * 512:(ch + 1) * 512], q_stage[:, :, :]),
            waits=[s_s.at()], inc=s_st, inc_n=16)
        ctx.emit("sync", lambda e, ch=ch: e.dma_start(
            ctx.dram["u_feat"].ap()[:, :, ch * 512:(ch + 1) * 512], u_stage[:, :, :]),
            waits=[s_s.at()], inc=s_st, inc_n=16)
        qu_store_guard[0] = s_st.c
        xr_store_guard[0] = s_st.c

        for (wt, dst, act, scale) in ((c["WkT"], k_tok, AF.Relu, 0.125),
                                      (c["WvT"], v_tok, AF.Identity, 1.0)):
            for t in range(NTILE):
                for n in range(2):
                    bank = nproj % 3
                    wlist = [bank_guard[bank]] if bank_guard[bank] else []
                    for k in range(8):
                        ctx.emit("tensor", lambda e, wt=wt, t=t, n=n, k=k, bank=bank: e.matmul(
                            ps_proj[bank][:, :], xnb_feat[:, k, t * 128:(t + 1) * 128],
                            wt[:, k, n * 512:(n + 1) * 512], start=(k == 0), stop=(k == 7)),
                            waits=wlist if k == 0 else [], inc=(s_t if k == 7 else None))
                    nproj += 1
                    ctx.emit("scalar", lambda e, dst=dst, t=t, n=n, act=act, scale=scale, bank=bank: e.activation(
                        dst[:, t, n * 512:(n + 1) * 512], ps_proj[bank][:, :], act, scale=scale),
                        waits=[s_t.at()], inc=s_s)
                    bank_guard[bank] = (s_s, s_s.c)
        wkv = [s_s.at()]

        for h in range(H0):
            po = (h % 2) * 64
            fo = (h // 2) * 64
            for t in range(NTILE):
                wl = list(wkv) if (h == 0 and t == 0) else []
                if h == 0 and t == 0 and kv_evac_guard[0] > 0:
                    wl.append((s_v, kv_evac_guard[0]))
                ctx.emit("tensor", lambda e, t=t, h=h, po=po, fo=fo: e.matmul(
                    kv_ps[po:po + 64, fo:fo + 64], k_tok[:, t, h * 64:(h + 1) * 64],
                    v_tok[:, t, h * 64:(h + 1) * 64], start=(t == 0), stop=(t == NTILE - 1)),
                    waits=wl,
                    inc=(s_t if (t == NTILE - 1 and h == H0 - 1) else None))
        ctx.emit("vector", lambda e: e.tensor_tensor(kv_acc[:, :], kv_acc[:, :], kv_ps[:, :], ALU.add),
                 waits=[s_t.at()])
        kv_evac_guard[0] = ctx.sems["s_v"].c

        for t in range(NTILE):
            for k in range(8):
                twl = [(s_v, s_v.c)]
                if k == 0 and gate_guard[0] > 0:
                    twl.append((s_s, gate_guard[0]))
                ctx.emit("tensor", lambda e, t=t, k=k, xnt32=xnt32: e.transpose(
                    ps_misc[:, 0:128], xnt32[:, t, k * 128:(k + 1) * 128],
                    c["identF"][:, :]), waits=twl, inc=s_t)
                ctx.emit("vector", lambda e, k=k: e.tensor_copy(
                    xnf32[:, k, :], ps_misc[:, 0:128]), waits=[s_t.at()])
            wtr32 = [s_v.at()]
            for k in range(8):
                gwl = list(wtr32) if k == 0 else []
                if k == 0 and gate_guard[0] > 0:
                    gwl.append((s_s, gate_guard[0]))
                ctx.emit("tensor", lambda e, t=t, k=k: e.matmul(
                    ps_misc[:, 384 + 16 * t:384 + 16 * t + 14], xnf32[:, k, :],
                    c["router_sb"][:, k, :], start=(k == 0), stop=(k == 7)),
                    waits=gwl, inc=(s_t if k == 7 else None))
            gw = [s_t.at()]
            ctx.emit("vector", lambda e, t=t: e.tensor_reduce(
                gstat[:, 0:1], ps_misc[:, 384 + 16 * t:384 + 16 * t + 14], AX.X, ALU.max,
                negate=True), waits=gw)
            ctx.emit("scalar", lambda e, t=t: e.activation(
                gexp[:, 0:14], ps_misc[:, 384 + 16 * t:384 + 16 * t + 14], AF.Exp,
                bias=gstat[:, 0:1], accum_out=gstat[:, 1:2]), waits=[s_v.at()])
            gate_guard[0] = s_s.c
            ws = [s_s.at()]
            ctx.emit("vector", lambda e: e.tensor_reduce(gstat[:, 2:3], gexp[:, 0:14], AX.X, ALU.max), waits=ws)
            ctx.emit("vector", lambda e: e.tensor_scalar(gtmp[:, 0:14], gexp[:, 0:14], gstat[:, 2:3], None, ALU.is_lt))
            ctx.emit("vector", lambda e: e.tensor_tensor(gtmp[:, 0:14], gtmp[:, 0:14], gexp[:, 0:14], ALU.mult))
            ctx.emit("vector", lambda e: e.tensor_reduce(gstat[:, 2:3], gtmp[:, 0:14], AX.X, ALU.max))
            ctx.emit("vector", lambda e: e.tensor_scalar(gkeep[:, 0:14], gexp[:, 0:14], gstat[:, 2:3], None, ALU.is_ge))
            ctx.emit("vector", lambda e: e.tensor_tensor(gkeep[:, 0:14], gkeep[:, 0:14], gexp[:, 0:14], ALU.mult))
            ctx.emit("vector", lambda e: e.tensor_reduce(gstat[:, 3:4], gkeep[:, 0:14], AX.X, ALU.add))
            ctx.emit("vector", lambda e: e.scalar_tensor_tensor(
                gstat[:, 3:4], gstat[:, 1:2], MOE_EPS, gstat[:, 3:4], ALU.mult, ALU.add))
            ctx.emit("vector", lambda e: e.reciprocal(gstat[:, 3:4], gstat[:, 3:4]))
            ctx.emit("vector", lambda e, ch=ch, t=t: e.tensor_scalar(
                w_tok[:, ch * 4 + t, :], gkeep[:, 0:14], gstat[:, 3:4], None, ALU.mult))
        xnt_guard[0] = s_t.c
        feat_guard[0] = s_t.c

    ctx.barrier()
    es.close()


def phase_l0_b(ctx):
    c = ctx.consts
    s_lda, s_st = ctx.sems["s_lda"], ctx.sems["s_st"]
    s_v, s_s, s_t = ctx.sems["s_v"], ctx.sems["s_s"], ctx.sems["s_t"]
    es = contextlib.ExitStack()
    kv_b = ctx.sb(es, "b_kv_b", [128, 512], BF16)
    q_sb = ctx.sb(es, "b_q_sb", [128, 8, 512], BF16)
    u_sb = ctx.sb(es, "b_u_sb", [128, 8, 512], BF16)
    att_sb = ctx.sb(es, "b_att_sb", [128, 8, 512], BF16)
    sq_sb = ctx.sb(es, "b_sq_sb", [128, 8, 512], BF16)
    fsb = ctx.sb(es, "b_fsb", [16, 512], F32)
    fsb_b = ctx.sb(es, "b_fsb_b", [16, 512], BF16)
    tmp = ctx.sb(es, "b_tmp", [128, 512], F32)
    attg = ctx.sb(es, "b_attg", [128, 8, 512], BF16)
    s_stage = ctx.sb(es, "b_s_stage", [128, 8, 512], BF16)
    ps_att = [ctx.ps(es, "b_ps_attA", [128, 512], F32), ctx.ps(es, "b_ps_attB", [128, 512], F32)]
    ps_wo = [ctx.ps(es, "b_ps_woA", [128, 512], F32), ctx.ps(es, "b_ps_woB", [128, 512], F32)]
    ps_norm = ctx.ps(es, "b_ps_norm", [16, 512], F32)
    ps_fb = ctx.ps(es, "b_ps_fb", [128, 512], F32)

    ctx.emit("vector", lambda e: e.tensor_copy(kv_b[:, :], c["kv_acc"][:, :]))
    wkvb = [s_v.at()]
    att_bank_guard = [0, 0]
    wo_bank_guard = [0, 0]
    norm_guard = [0]
    attg_guard = [0]
    s_store_guard = [0]

    for ch in range(NCHUNK):
        for (dst, nm) in ((q_sb, "q_feat"), (u_sb, "u_feat")):
            ctx.emit("sync", lambda e, dst=dst, nm=nm, ch=ch: e.dma_start(
                dst[:, :, :], ctx.dram[nm].ap()[:, :, ch * 512:(ch + 1) * 512]),
                waits=[s_t.at(), s_v.at(), s_s.at()], inc=s_lda, inc_n=16)
        wld = [s_lda.at()]
        for hp in range(8):
            bank = hp % 2
            for hh in range(2):
                h = 2 * hp + hh
                po, fo = (h % 2) * 64, (h // 2)
                wl = list(wld) + list(wkvb) if (hp == 0 and hh == 0) else []
                if att_bank_guard[bank] > 0 and hh == 0:
                    wl.append((s_s, att_bank_guard[bank]))
                ctx.emit("tensor", lambda e, h=h, po=po, fo=fo, bank=bank: e.matmul(
                    ps_att[bank][po:po + 64, :], kv_b[po:po + 64, fo * 64:fo * 64 + 64],
                    q_sb[po:po + 64, fo, :], start=True, stop=True),
                    waits=wl, inc=(s_t if hh == 1 else None))
            ctx.emit("scalar", lambda e, hp=hp, bank=bank: e.activation(
                att_sb[:, hp, :], ps_att[bank][:, :], AF.Identity), waits=[s_t.at()])
            ctx.emit("scalar", lambda e, hp=hp, bank=bank: e.activation(
                sq_sb[:, hp, :], ps_att[bank][:, :], AF.Square, accum_out=None))
            att_bank_guard[bank] = s_s.c
        wsq = [s_s.at()]
        for ec in range(8):
            wl = list(wsq) if ec == 0 else []
            if ec == 0 and norm_guard[0] > 0:
                wl.append((s_v, norm_guard[0]))
            ctx.emit("tensor", lambda e, ec=ec: e.matmul(
                ps_norm[:, :], c["blk8"][:, ec, :], sq_sb[:, ec, :],
                start=(ec == 0), stop=(ec == 7)), waits=wl,
                inc=(s_t if ec == 7 else None))
        ctx.emit("vector", lambda e: e.tensor_single_scalar(fsb[:, :], ps_norm[:, :], 1e-24, ALU.max),
                 waits=[s_t.at()])
        norm_guard[0] = s_v.c
        ctx.emit("scalar", lambda e: e.activation(fsb[:, :], fsb[:, :], AF.Ln), waits=[s_v.at()])
        ctx.emit("scalar", lambda e: e.activation(fsb_b[:, :], fsb[:, :], AF.Exp, scale=-0.5))
        wfac = [s_s.at()]
        for ec in range(8):
            wl = list(wfac) if ec == 0 else [(s_v, s_v.c)]
            ctx.emit("tensor", lambda e, ec=ec: e.matmul(
                ps_fb[:, :], c["sel8"][:, ec, :], fsb_b[:, :], start=True, stop=True),
                waits=wl, inc=s_t)
            tw = [s_t.at()]
            if ec == 0 and attg_guard[0] > 0:
                tw.append((s_t, attg_guard[0]))
            ctx.emit("vector", lambda e, ec=ec: e.tensor_tensor(
                tmp[:, :], att_sb[:, ec, :], ps_fb[:, :], ALU.mult), waits=tw)
            ctx.emit("vector", lambda e, ec=ec: e.tensor_tensor(
                attg[:, ec, :], tmp[:, :], u_sb[:, ec, :], ALU.mult))
        wattg = [s_v.at()]
        for m in range(8):
            bank = m % 2
            wl = list(wattg) if m == 0 else []
            if wo_bank_guard[bank] > 0:
                wl.append((s_s, wo_bank_guard[bank]))
            for k in range(8):
                ctx.emit("tensor", lambda e, m=m, k=k, bank=bank: e.matmul(
                    ps_wo[bank][:, :], c["WoT"][:, k, m * 128:(m + 1) * 128], attg[:, k, :],
                    start=(k == 0), stop=(k == 7)),
                    waits=wl if k == 0 else [], inc=(s_t if k == 7 else None))
            swl = [s_t.at()]
            if m == 0 and s_store_guard[0] > 0:
                swl.append((s_st, s_store_guard[0]))
            ctx.emit("scalar", lambda e, m=m, bank=bank: e.activation(
                s_stage[:, m, :], ps_wo[bank][:, :], AF.Identity), waits=swl)
            wo_bank_guard[bank] = s_s.c
        attg_guard[0] = s_t.c
        ctx.emit("sync", lambda e, ch=ch: e.dma_start(
            ctx.dram["s_feat"].ap()[:, :, ch * 512:(ch + 1) * 512], s_stage[:, :, :]),
            waits=[s_s.at()], inc=s_st, inc_n=16)
        s_store_guard[0] = s_st.c

    ctx.barrier()
    es.close()
    ctx.scope["ab"].close()


def phase_l0_moe(ctx):
    c = ctx.consts
    s_lda, s_ldw, s_st = ctx.sems["s_lda"], ctx.sems["s_ldw"], ctx.sems["s_st"]
    s_v, s_s, s_t = ctx.sems["s_v"], ctx.sems["s_s"], ctx.sems["s_t"]
    es = contextlib.ExitStack()
    NQ, QT = 4, 1024
    wbuf = [ctx.sb(es, f"m_wbuf{i}", [128, 16384], BF16) for i in range(3)]
    xrb = ctx.sb(es, "m_xrb", [128, 8, QT], BF16)
    hb = ctx.sb(es, "m_hb", [128, 16, 512], BF16)
    g_acc = ctx.sb(es, "m_gacc", [128, 8, QT], F32)
    g_stage = hb[:, :, :].rearrange("p c n -> p (c n)").rearrange("p (c n) -> p c n", c=8)
    wb_sb = ctx.sb(es, "m_wb", [128, 2, 512], BF16)
    tmp2 = ctx.sb(es, "m_tmp2", [128, 512], F32)
    wT_sb = ctx.sb(es, "m_wT", [16, T], BF16)
    wflat = ctx.sb(es, "m_wflat", [1, E_ROUTED * QT], BF16)
    ps_h = [ctx.ps(es, f"m_ps_h{i}", [128, 512], F32) for i in range(3)]
    ps_o = [ctx.ps(es, f"m_ps_o{i}", [128, 512], F32) for i in range(2)]
    ps_wb = ctx.ps(es, "m_ps_wb", [128, 512], F32)

    def w1view(i):
        return wbuf[i][:, :].rearrange("p (c n) -> p c n", c=8)

    def w2view(i):
        return wbuf[i][:, :].rearrange("p (c n) -> p c n", c=16)

    for g in range(32):
        ctx.emit("tensor", lambda e, g=g: e.transpose(
            ps_wb[0:14, 0:128], c["w_tok"][:, g, 0:14], c["identF"][:, :]),
            waits=[(s_v, s_v.c)], inc=s_t)
        ctx.emit("vector", lambda e, g=g: e.tensor_copy(
            wT_sb[0:14, g * 128:(g + 1) * 128], ps_wb[0:14, 0:128]), waits=[s_t.at()])
    wT_done = s_v.at()

    buf_guard = [0, 0, 0]
    bank_guard3 = [0, 0, 0]
    out_bank_guard = [0, 0]
    hb_guard = [0]
    wb_guard = [0, 0]
    gst_guard = [0]
    w1_free = [0]

    def w1src(e):
        if e < E_ROUTED:
            return ctx.dram["L0_ffn_w1"].ap()[e]
        return ctx.dram["L0_ffn_w1_s"].ap()[e - E_ROUTED]

    def w2src(e):
        if e < E_ROUTED:
            return ctx.dram["L0_ffn_w2"].ap()[e]
        return ctx.dram["L0_ffn_w2_s"].ap()[e - E_ROUTED]

    NE = E_ROUTED + E_SHARED
    for q in range(NQ):
        ctx.emit("sync", lambda e, q=q: e.dma_start(
            xrb[:, :, :], ctx.dram["xr_feat"].ap()[:, :, q * QT:(q + 1) * QT]),
            waits=[s_t.at(), s_v.at(), s_s.at()], inc=s_lda, inc_n=16)
        wxr = [s_lda.at()]
        ctx.emit("vector", lambda e: e.memset(g_acc[:, :, :], 0.0), waits=[s_s.at()])
        for e_i in range(E_ROUTED):
            ctx.emit("gpsimd", lambda e, e_i=e_i, q=q: e.dma_start(
                wflat[0:1, e_i * QT:(e_i + 1) * QT], wT_sb[e_i:e_i + 1, q * QT:(q + 1) * QT]),
                waits=([wT_done, (s_t, s_t.c)] if e_i == 0 else []), inc=ctx.sems["s_wfl"], inc_n=16)
        wfl = [ctx.sems["s_wfl"].at()]
        for e_i in range(NE):
            b1i, b2i = (2 * e_i) % 3, (2 * e_i + 1) % 3
            sb1, sb2 = ctx.sems[f"s_wb{b1i}"], ctx.sems[f"s_wb{b2i}"]
            ctx.emit("gpsimd", lambda e, e_i=e_i, b1i=b1i: e.dma_start(
                w1view(b1i)[:, :, :], w1src(e_i).rearrange("(c p) h -> p c h", p=128)),
                waits=[(s_t, buf_guard[b1i])] if buf_guard[b1i] > 0 else [],
                inc=sb1, inc_n=16)
            w1ld = [sb1.at()]
            ctx.emit("gpsimd", lambda e, e_i=e_i, b2i=b2i: e.dma_start(
                w2view(b2i)[:, :, :], w2src(e_i).rearrange("(c p) h -> p c h", p=128)),
                waits=[(s_t, buf_guard[b2i])] if buf_guard[b2i] > 0 else [],
                inc=sb2, inc_n=16)
            w2ld = [sb2.at()]
            for k in range(8):
                ctx.emit("vector", lambda e, e_i=e_i, b1i=b1i, k=k: e.tensor_scalar(
                    w1view(b1i)[:, k, :], w1view(b1i)[:, k, :], c["rms_all"][:, e_i, k:k + 1],
                    None, ALU.mult), waits=w1ld if k == 0 else [])
            wsc = [s_v.at()]
            routed = e_i < E_ROUTED
            if routed:
                for n in range(2):
                    wbw = []
                    if n == 0:
                        wbw.append(ctx.sems["s_wfl"].at())
                    else:
                        wbw.append((s_v, s_v.c))
                    if wb_guard[n] > 0:
                        wbw.append((s_v, wb_guard[n]))
                    ctx.emit("tensor", lambda e, e_i=e_i, n=n: e.matmul(
                        ps_wb[:, :], c["ones_k1"][:, :],
                        wflat[0:1, e_i * QT + n * 512:e_i * QT + (n + 1) * 512],
                        start=True, stop=True), waits=wbw, inc=s_t)
                    ctx.emit("vector", lambda e, n=n: e.tensor_copy(wb_sb[:, n, :], ps_wb[:, :]),
                             waits=[s_t.at()])
            for n in range(2):
                for m in range(16):
                    bank = m % 3
                    wl = []
                    if m == 0:
                        wl += wsc + wxr
                        if hb_guard[0] > 0:
                            wl.append((s_t, hb_guard[0]))
                    if bank_guard3[bank] > 0:
                        wl.append((s_s, bank_guard3[bank]))
                    for k in range(8):
                        ctx.emit("tensor", lambda e, b1i=b1i, m=m, k=k, n=n, bank=bank: e.matmul(
                            ps_h[bank][:, :], w1view(b1i)[:, k, m * 128:(m + 1) * 128],
                            xrb[:, k, n * 512:(n + 1) * 512], start=(k == 0), stop=(k == 7)),
                            waits=wl if k == 0 else [], inc=(s_t if k == 7 else None))
                    hwl = [s_t.at()]
                    if e_i == 0 and n == 0 and m == 0 and gst_guard[0] > 0:
                        hwl.append((s_st, gst_guard[0]))
                    ctx.emit("scalar", lambda e, m=m, bank=bank: e.activation(
                        hb[:, m, :], ps_h[bank][:, :], AF.Silu), waits=hwl)
                    bank_guard3[bank] = s_s.c
                if n == 1:
                    w1_free[0] = s_t.c
                whb = [s_s.at()]
                for mo in range(8):
                    bank = mo % 2
                    wl = list(whb) + list(w2ld) if mo == 0 else []
                    if out_bank_guard[bank] > 0:
                        wl.append((s_v, out_bank_guard[bank]))
                    for kh in range(16):
                        ctx.emit("tensor", lambda e, b2i=b2i, mo=mo, kh=kh, bank=bank: e.matmul(
                            ps_o[bank][:, :], w2view(b2i)[:, kh, mo * 128:(mo + 1) * 128],
                            hb[:, kh, :], start=(kh == 0), stop=(kh == 15)),
                            waits=wl if kh == 0 else [], inc=(s_t if kh == 15 else None))
                    if routed:
                        ctx.emit("vector", lambda e, mo=mo, bank=bank, n=n: e.tensor_tensor(
                            tmp2[:, :], ps_o[bank][:, :], wb_sb[:, n, :], ALU.mult), waits=[s_t.at()])
                        ctx.emit("vector", lambda e, mo=mo, n=n: e.tensor_tensor(
                            g_acc[:, mo, n * 512:(n + 1) * 512], g_acc[:, mo, n * 512:(n + 1) * 512],
                            tmp2[:, :], ALU.add))
                    else:
                        ctx.emit("vector", lambda e, mo=mo, n=n, bank=bank: e.tensor_tensor(
                            g_acc[:, mo, n * 512:(n + 1) * 512], g_acc[:, mo, n * 512:(n + 1) * 512],
                            ps_o[bank][:, :], ALU.add), waits=[s_t.at()])
                    out_bank_guard[bank] = s_v.c
                if routed:
                    wb_guard[n] = s_v.c
                hb_guard[0] = s_t.c
            buf_guard[b1i] = w1_free[0]
            buf_guard[b2i] = s_t.c
        gswl = [s_v.at(), s_t.at()]
        if q > 0:
            gswl.append((s_st, gst_guard[0]))
        for mo in range(8):
            ctx.emit("scalar", lambda e, mo=mo: e.activation(
                g_stage[:, mo, :], g_acc[:, mo, :], AF.Silu), waits=gswl if mo == 0 else [])
        ctx.emit("sync", lambda e, q=q: e.dma_start(
            ctx.dram["g_feat"].ap()[:, :, q * QT:(q + 1) * QT], g_stage[:, :, :]),
            waits=[s_s.at()], inc=s_st, inc_n=16)
        gst_guard[0] = s_st.c

    ctx.barrier()
    es.close()


def phase_r0(ctx):
    c = ctx.consts
    s_lda, s_st = ctx.sems["s_lda"], ctx.sems["s_st"]
    s_v, s_s, s_t = ctx.sems["s_v"], ctx.sems["s_s"], ctx.sems["s_t"]
    es = contextlib.ExitStack()
    NQ, QT, NT8 = 4, 1024, 8
    x_tok = ctx.sb(es, "r_x_tok", [128, NT8, D], F32)
    s_sb = ctx.sb(es, "r_s", [128, 8, QT], BF16)
    g_sb = ctx.sb(es, "r_g", [128, 8, QT], BF16)
    delta_f = ctx.sb(es, "r_delta_f", [128, 8, QT], BF16)
    delta_t = ctx.sb(es, "r_delta_t", [128, NT8, D], BF16)
    xn1_tok = ctx.sb(es, "r_xn1_tok", [128, NT8, D], BF16)
    xn1_stage = ctx.sb(es, "r_xn1_stage", [128, 8, QT], BF16)
    sq_buf = ctx.sb(es, "r_sq", [128, D], F32)
    ssq = ctx.sb(es, "r_ssq", [128, NT8], F32)
    ssm = ctx.sb(es, "r_ssm", [128, NT8], F32)
    mean = ctx.sb(es, "r_mean", [128, NT8], F32)
    rstd = ctx.sb(es, "r_rstd", [128, NT8], F32)
    tstat = ctx.sb(es, "r_tstat", [128, NT8], F32)
    ps_tr = [ctx.ps(es, "r_ps_trA", [128, 128], BF16), ctx.ps(es, "r_ps_trB", [128, 128], BF16)]

    emit_tr = ctx.make_transposer(ps_tr, c["ident"])
    st_guard = [0]
    for q in range(NQ):
        base = q * QT
        ctx.emit("sync", lambda e, q=q: e.dma_start(
            x_tok[:, :, :], ctx.dram["L0_attn_x"].ap()[q * QT:(q + 1) * QT, :].rearrange(
                "(t p) d -> p t d", p=128)),
            waits=[s_t.at(), s_v.at(), s_s.at()] + ([(s_st, st_guard[0])] if st_guard[0] > 0 else []),
            inc=s_lda, inc_n=16)
        ctx.emit("sync", lambda e, q=q: e.dma_start(
            s_sb[:, :, :], ctx.dram["s_feat"].ap()[:, :, q * QT:(q + 1) * QT]), inc=s_lda, inc_n=16)
        ctx.emit("sync", lambda e, q=q: e.dma_start(
            g_sb[:, :, :], ctx.dram["g_feat"].ap()[:, :, q * QT:(q + 1) * QT]), inc=s_lda, inc_n=16)
        wld = [s_lda.at()]
        for kc in range(8):
            ctx.emit("vector", lambda e, kc=kc: e.tensor_tensor(
                delta_f[:, kc, :], s_sb[:, kc, :], g_sb[:, kc, :], ALU.mult),
                waits=wld if kc == 0 else [])
        wdf = [s_v.at()]
        for t in range(NT8):
            for kc in range(8):
                emit_tr(delta_f[:, kc, t * 128:(t + 1) * 128],
                        delta_t[:, t, kc * 128:(kc + 1) * 128],
                        tw=(wdf if (t == 0 and kc == 0) else []))
        for t in range(NT8):
            ctx.emit("vector", lambda e, t=t: e.tensor_tensor(
                x_tok[:, t, :], x_tok[:, t, :], delta_t[:, t, :], ALU.add))
        wx1 = [s_v.at()]
        ctx.emit("sync", lambda e, q=q: e.dma_start(
            ctx.dram["x1m"].ap()[q * QT:(q + 1) * QT, :].rearrange("(t p) d -> p t d", p=128),
            x_tok[:, :, :]), waits=wx1, inc=s_st, inc_n=16)
        for t in range(NT8):
            ctx.emit("scalar", lambda e, t=t: e.activation(
                sq_buf[:, :], x_tok[:, t, :], AF.Square, accum_out=ssq[:, t:t + 1]),
                waits=wx1 if t == 0 else [])
            ctx.emit("vector", lambda e, t=t: e.tensor_reduce(
                ssm[:, t:t + 1], x_tok[:, t, :], AX.X, ALU.add))
        ctx.emit("vector", lambda e: e.tensor_single_scalar(mean[:, :], ssm[:, :], 1.0 / D, ALU.mult),
                 waits=[s_s.at()])
        ctx.emit("vector", lambda e: e.tensor_tensor(tstat[:, :], mean[:, :], mean[:, :], ALU.mult))
        ctx.emit("vector", lambda e: e.scalar_tensor_tensor(tstat[:, :], ssq[:, :], 1.0 / D, tstat[:, :],
                                                            ALU.mult, ALU.subtract))
        ctx.emit("scalar", lambda e: e.activation(rstd[:, :], tstat[:, :], AF.Ln, bias=c["cLN"][:, :]),
                 waits=[s_v.at()])
        ctx.emit("scalar", lambda e: e.activation(rstd[:, :], rstd[:, :], AF.Exp, scale=-0.5))
        wstat = [s_s.at()]
        for t in range(NT8):
            ctx.emit("vector", lambda e, t=t: e.tensor_scalar(
                xn1_tok[:, t, :], x_tok[:, t, :], mean[:, t:t + 1], rstd[:, t:t + 1],
                ALU.subtract, ALU.mult), waits=wstat if t == 0 else [])
        for t in range(NT8):
            for kc in range(8):
                tw = [(s_v, s_v.c)] if (t == 0 and kc == 0) else []
                cw = []
                if t == 0 and kc == 0 and st_guard[0] > 0:
                    cw = [(s_st, st_guard[0])]
                emit_tr(xn1_tok[:, t, kc * 128:(kc + 1) * 128],
                        xn1_stage[:, kc, t * 128:(t + 1) * 128], tw=tw, cw=cw)
        ctx.emit("sync", lambda e, q=q: e.dma_start(
            ctx.dram["xn1_feat"].ap()[:, :, q * QT:(q + 1) * QT], xn1_stage[:, :, :]),
            waits=[s_v.at()], inc=s_st, inc_n=16)
        st_guard[0] = s_st.c

    ctx.barrier()
    es.close()




def setup_l1_masks(ctx):
    c = ctx.consts
    es = ctx.scope["de"]
    s_v, s_g = ctx.sems["s_v"], ctx.sems["s_g"]
    es_tmp = contextlib.ExitStack()
    iota_i = ctx.sb(es_tmp, "m1_iota_i", [128, 512], I32, side="right")
    iota_f = ctx.sb(es_tmp, "m1_iota_f", [128, 512], F32, side="right")
    jcol_f = ctx.sb(es_tmp, "m1_jcol_f", [128, 512], F32, side="right")
    tmpa = ctx.sb(es_tmp, "m1_tmpa", [128, 512], F32, side="right")
    tmpb = ctx.sb(es_tmp, "m1_tmpb", [128, 512], F32, side="right")
    m_std = ctx.sb(es, "m_std", [128, 512], BF16)
    m_c0 = ctx.sb(es, "m_c0", [128, 128], BF16)
    m_c1 = ctx.sb(es, "m_c1", [128, 256], BF16)
    m_c2 = ctx.sb(es, "m_c2", [128, 384], BF16)
    c["masks"] = (m_c0, m_c1, m_c2, m_std)
    m_f = ctx.sb(es_tmp, "m1_mf", [128, 512], F32, side="right")

    ctx.emit("gpsimd", lambda e: e.iota(iota_i[:, :], [[1, 512]], channel_multiplier=-1),
             waits=[s_v.at()], inc=s_g)
    ctx.emit("vector", lambda e: e.tensor_copy(iota_f[:, :], iota_i[:, :]), waits=[s_g.at()], inc=s_v)
    ctx.emit("gpsimd", lambda e: e.iota(iota_i[:, :], [[1, 512]], channel_multiplier=0),
             waits=[s_v.at()], inc=s_g)
    ctx.emit("vector", lambda e: e.tensor_copy(jcol_f[:, :], iota_i[:, :]), waits=[s_g.at()], inc=s_v)
    ctx.emit("vector", lambda e: e.tensor_single_scalar(tmpa[:, 0:384], iota_f[:, 0:384], 1.0, ALU.is_ge))
    ctx.emit("vector", lambda e: e.tensor_single_scalar(tmpb[:, 0:384], iota_f[:, 0:384], 256.0, ALU.is_le))
    ctx.emit("vector", lambda e: e.tensor_tensor(m_f[:, 0:384], tmpa[:, 0:384], tmpb[:, 0:384], ALU.mult))
    ctx.emit("vector", lambda e: e.tensor_single_scalar(m_f[:, 384:512], jcol_f[:, 384:512], 388.0, ALU.is_lt))
    ctx.emit("vector", lambda e: e.tensor_copy(m_std[:, :], m_f[:, :]), inc=s_v)
    ctx.emit("vector", lambda e: e.tensor_single_scalar(m_f[:, 0:128], iota_f[:, 0:128], 0.0, ALU.is_le))
    ctx.emit("vector", lambda e: e.tensor_copy(m_c0[:, :], m_f[:, 0:128]), inc=s_v)
    ctx.emit("vector", lambda e: e.tensor_single_scalar(m_f[:, 0:256], iota_f[:, 0:256], 128.0, ALU.is_le))
    ctx.emit("vector", lambda e: e.tensor_copy(m_c1[:, :], m_f[:, 0:256]), inc=s_v)
    ctx.emit("vector", lambda e: e.tensor_single_scalar(tmpa[:, 0:384], iota_f[:, 0:384], 256.0, ALU.is_le))
    ctx.emit("vector", lambda e: e.tensor_single_scalar(tmpb[:, 0:384], jcol_f[:, 0:384], 4.0, ALU.is_lt))
    ctx.emit("vector", lambda e: e.tensor_single_scalar(m_f[:, 0:384], iota_f[:, 0:384], 1.0, ALU.is_ge))
    ctx.emit("vector", lambda e: e.tensor_tensor(m_f[:, 0:384], m_f[:, 0:384], tmpb[:, 0:384], ALU.max))
    ctx.emit("vector", lambda e: e.tensor_tensor(m_f[:, 0:384], m_f[:, 0:384], tmpa[:, 0:384], ALU.mult))
    ctx.emit("vector", lambda e: e.tensor_copy(m_c2[:, :], m_f[:, 0:384]), inc=s_v)

    ctx.barrier()
    es_tmp.close()




def phase_l1_kv(ctx):
    c = ctx.consts
    s_lda = ctx.sems["s_lda"]
    s_v, s_s, s_t = ctx.sems["s_v"], ctx.sems["s_s"], ctx.sems["s_t"]
    es_de = ctx.scope["de"]
    K_all = ctx.sb(es_de, "e_K_all", [128, 2, T], BF16)
    V_all = ctx.sb(es_de, "e_V_all", [128, 32, 256], BF16)
    c["K_all"], c["V_all"] = K_all, V_all
    es = contextlib.ExitStack()
    xn1c = ctx.sb(es, "d_xn1c", [128, 8, 512], BF16)
    ps = [ctx.ps(es, f"d_ps{i}", [128, 512], F32) for i in range(2)]
    bank_guard = [0, 0]
    nb = 0
    for ch in range(NCHUNK):
        ctx.emit("sync", lambda e, ch=ch: e.dma_start(
            xn1c[:, :, :], ctx.dram["xn1_feat"].ap()[:, :, ch * 512:(ch + 1) * 512]),
            waits=[s_t.at()], inc=s_lda, inc_n=16)
        wld = [s_lda.at()]
        for mc in range(2):
            bank = nb % 2
            wl = list(wld) if mc == 0 else []
            if bank_guard[bank] > 0:
                wl.append((s_s, bank_guard[bank]))
            for k in range(8):
                ctx.emit("tensor", lambda e, mc=mc, k=k, bank=bank: e.matmul(
                    ps[bank][:, :], c["Wk1"][:, k, mc * 128:(mc + 1) * 128], xn1c[:, k, :],
                    start=(k == 0), stop=(k == 7)), waits=wl if k == 0 else [],
                    inc=(s_t if k == 7 else None))
            ctx.emit("scalar", lambda e, ch=ch, mc=mc, bank=bank: e.activation(
                K_all[:, mc, ch * 512:(ch + 1) * 512], ps[bank][:, :], AF.Identity),
                waits=[s_t.at()])
            bank_guard[bank] = s_s.c
            nb += 1
        for t in range(NTILE):
            bank = nb % 2
            wl = [(s_s, bank_guard[bank])] if bank_guard[bank] > 0 else []
            for k in range(8):
                ctx.emit("tensor", lambda e, t=t, k=k, bank=bank: e.matmul(
                    ps[bank][:, 0:256], xn1c[:, k, t * 128:(t + 1) * 128], c["Wv1"][:, k, 0:256],
                    start=(k == 0), stop=(k == 7)), waits=wl if k == 0 else [],
                    inc=(s_t if k == 7 else None))
            ctx.emit("scalar", lambda e, ch=ch, t=t, bank=bank: e.activation(
                V_all[:, ch * 4 + t, :], ps[bank][:, 0:256], AF.Identity), waits=[s_t.at()])
            bank_guard[bank] = s_s.c
            nb += 1
    ctx.barrier()
    es.close()


def phase_l1_attn(ctx):
    c = ctx.consts
    s_lda, s_st = ctx.sems["s_lda"], ctx.sems["s_st"]
    s_v, s_s, s_t = ctx.sems["s_v"], ctx.sems["s_s"], ctx.sems["s_t"]
    m_c0, m_c1, m_c2, m_std = c["masks"]
    K_all, V_all = c["K_all"], c["V_all"]
    es = contextlib.ExitStack()
    xn1c = ctx.sb(es, "e_xn1c", [128, 8, 512], BF16)
    Q_sb = ctx.sb(es, "e_Q", [128, 8, 512], BF16)
    e_sb = ctx.sb(es, "e_exp", [128, 2, 512], BF16)
    pn_sb = ctx.sb(es, "e_pn", [128, 2, 512], BF16)
    den = ctx.sb(es, "e_den", [128, 2], F32)
    rcp = ctx.sb(es, "e_rcp", [128, 2], F32)
    pT_sb = ctx.sb(es, "e_pT", [128, 2, 4, 128], BF16)
    Y_sb = ctx.sb(es, "e_Y", [128, 8, 128], BF16)
    s1_stage = ctx.sb(es, "e_s1", [128, 8, 512], BF16)
    ps_lg = [ctx.ps(es, f"e_ps_lg{i}", [128, 512], F32) for i in range(4)]
    ps_pt = [ctx.ps(es, "e_ps_ptA", [128, 512], BF16)]
    ps_y = ctx.ps(es, "e_ps_y", [128, 128], F32)
    ps_wo = [ctx.ps(es, "e_ps_wo", [128, 512], F32)]
    pt_guard = [0]

    lg_guard = [None, None, None, None]
    pn_read_guard = [0, 0]
    exp_done = [0, 0]
    y_guard = [None]
    ywo_guard = [0]
    wo_guard = [None]
    s1_st_guard = [0]

    def head_info(hi):
        pr, pos = hi // 2, hi % 2
        grp, j = pr // 4, pr % 4
        h = grp * 8 + pos * 4 + j
        kvh = h // 4
        return dict(h=h, pr=pr, pos=pos, kvh=kvh, po=(kvh % 2) * 64,
                    slot=(h // 8) * 4 + (h % 4), kfo=kvh // 2)

    def tile_geom(g):
        i0 = g * 128
        if g == 0:
            return [(0, 0, 128)], m_c0, 128, [0]
        if g == 1:
            return [(0, 0, 256)], m_c1, 256, [0, 1]
        if g == 2:
            return [(0, 0, 384)], m_c2, 384, [0, 1, 2]
        return [(0, i0 - 256, 384), (384, 0, 128)], m_std, 512, [g - 2, g - 1, g, 0]

    for ch in range(NCHUNK):
        ctx.emit("sync", lambda e, ch=ch: e.dma_start(
            xn1c[:, :, :], ctx.dram["xn1_feat"].ap()[:, :, ch * 512:(ch + 1) * 512]),
            waits=[s_t.at(), s_v.at(), s_s.at()], inc=s_lda, inc_n=16)
        wld = [s_lda.at()]
        for m in range(8):
            bank = m % 4
            wl = list(wld) if m == 0 else []
            if lg_guard[bank] is not None:
                wl.append(lg_guard[bank])
            for k in range(8):
                ctx.emit("tensor", lambda e, m=m, k=k, bank=bank: e.matmul(
                    ps_lg[bank][:, :], c["Wq1"][:, k, m * 128:(m + 1) * 128], xn1c[:, k, :],
                    start=(k == 0), stop=(k == 7)), waits=wl if k == 0 else [],
                    inc=(s_t if k == 7 else None))
            ctx.emit("scalar", lambda e, m=m, bank=bank: e.activation(
                Q_sb[:, m, :], ps_lg[bank][:, :], AF.Identity, scale=0.125), waits=[s_t.at()])
            lg_guard[bank] = (s_s, s_s.c)
        qs_at = [s_s.at()]

        for t in range(NTILE):
            g = ch * 4 + t
            spans, mask, mw, scs = tile_geom(g)
            nsp = len(spans)
            lg_done = [0] * (N_HEADS1 + 1)

            def emit_logits(hi, first_tile):
                hd = head_info(hi)
                bank = hi % 4
                wl = []
                if lg_guard[bank] is not None:
                    wl.append(lg_guard[bank])
                if first_tile and hi < 2:
                    wl.extend(qs_at)
                for si, (co, s0, sw) in enumerate(spans):
                    ctx.emit("tensor", lambda e, t=t, hd=hd, co=co, s0=s0, sw=sw, bank=bank: e.matmul(
                        ps_lg[bank][:, co:co + sw],
                        Q_sb[hd["po"]:hd["po"] + 64, hd["slot"], t * 128:(t + 1) * 128],
                        K_all[hd["po"]:hd["po"] + 64, hd["kfo"], s0:s0 + sw],
                        start=True, stop=True),
                        waits=wl if si == 0 else [], inc=(s_t if si == nsp - 1 else None))
                lg_done[hi] = s_t.c

            emit_logits(0, t == 0)

            def emit_exp(hi):
                p, bank = hi % 2, hi % 4
                ctx.emit("scalar", lambda e, mw=mw, bank=bank, p=p: e.activation(
                    e_sb[:, p, 0:mw], ps_lg[bank][:, 0:mw], AF.Exp),
                    waits=[(s_t, lg_done[hi])])
                lg_guard[bank] = (s_s, s_s.c)
                exp_done[p] = s_s.c

            emit_exp(0)
            pn_ready = [0] * N_HEADS1
            copy_done = [0] * N_HEADS1
            for hi in range(N_HEADS1 + 1):
                if hi < N_HEADS1:
                    p = hi % 2
                    ctx.emit("vector", lambda e, mask=mask, mw=mw, p=p: e.scalar_tensor_tensor(
                        e_sb[:, p, 0:mw], e_sb[:, p, 0:mw], 1.0, mask[:, 0:mw], ALU.mult,
                        ALU.mult, accum_out=den[:, p:p + 1]), waits=[(s_s, exp_done[p])])
                    ctx.emit("vector", lambda e, p=p: e.reciprocal(rcp[:, p:p + 1], den[:, p:p + 1]))
                    rcp_done = s_v.c
                    if hi + 1 < N_HEADS1:
                        emit_logits(hi + 1, t == 0)
                        emit_exp(hi + 1)
                    pw = [(s_v, rcp_done)]
                    if pn_read_guard[p] > 0:
                        pw.append((s_t, pn_read_guard[p]))
                    ctx.emit("scalar", lambda e, mw=mw, p=p: e.activation(
                        pn_sb[:, p, 0:mw], e_sb[:, p, 0:mw], AF.Identity, scale=rcp[:, p:p + 1]),
                        waits=pw)
                    pn_ready[hi] = s_s.c
                if hi >= 1:
                    hj = hi - 1
                    hd = head_info(hj)
                    p = hj % 2
                    for j in range(len(scs)):
                        twl = []
                        if j == 0:
                            twl.append((s_s, pn_ready[hj]))
                            if pt_guard[0] > 0:
                                twl.append((s_v, pt_guard[0]))
                        ctx.emit("tensor", lambda e, j=j, p=p: e.transpose(
                            ps_pt[0][:, j * 128:(j + 1) * 128], pn_sb[:, p, j * 128:(j + 1) * 128],
                            c["ident"][:, :]), waits=twl, inc=(s_t if j == len(scs) - 1 else None))
                    ctx.emit("vector", lambda e, p=p, ns=len(scs): e.tensor_copy(
                        pT_sb[:, p, 0:ns, :], ps_pt[0][:, 0:ns * 128].rearrange(
                            "q (c n) -> q c n", n=128)), waits=[s_t.at()])
                    pt_guard[0] = s_v.c
                    copy_done[hj] = s_v.c
                    pn_read_guard[p] = s_t.c
                    wl2 = [(s_v, copy_done[hj])]
                    if hd["pos"] == 0 and y_guard[0] is not None:
                        wl2.append(y_guard[0])
                    for j, sc in enumerate(scs):
                        ctx.emit("tensor", lambda e, sc=sc, j=j, hd=hd, p=p,
                                 last=(j == len(scs) - 1): e.matmul(
                            ps_y[hd["po"]:hd["po"] + 64, :],
                            V_all[:, sc, hd["kvh"] * 64:(hd["kvh"] + 1) * 64],
                            pT_sb[:, p, j, :], start=(j == 0), stop=last),
                            waits=wl2 if j == 0 else [], inc=(s_t if j == len(scs) - 1 else None))
                    if hd["pos"] == 1:
                        ywl = [s_t.at()]
                        if ywo_guard[0] > 0 and hj == 1:
                            ywl.append((s_t, ywo_guard[0]))
                        ctx.emit("scalar", lambda e, pr=hd["pr"]: e.activation(
                            Y_sb[:, pr, :], ps_y[:, :], AF.Identity), waits=ywl)
                        y_guard[0] = (s_s, s_s.c)
            wl3 = [s_s.at()]
            for half in range(2):
                wlh = list(wl3) if half == 0 else []
                if wo_guard[0] is not None:
                    wlh.append(wo_guard[0])
                for mq in range(4):
                    mo = half * 4 + mq
                    for kh in range(8):
                        ctx.emit("tensor", lambda e, mo=mo, mq=mq, kh=kh: e.matmul(
                            ps_wo[0][:, mq * 128:(mq + 1) * 128],
                            c["Wo1"][:, kh, mo * 128:(mo + 1) * 128], Y_sb[:, kh, :],
                            start=(kh == 0), stop=(kh == 7)),
                            waits=wlh if (mq == 0 and kh == 0) else [],
                            inc=(s_t if (mq == 3 and kh == 7) else None))
                ewl = [s_t.at()]
                if t == 0 and half == 0 and s1_st_guard[0] > 0:
                    ewl.append((s_st, s1_st_guard[0]))
                ctx.emit("scalar", lambda e, half=half, t=t: e.activation(
                    s1_stage[:, half * 4:half * 4 + 4, t * 128:(t + 1) * 128],
                    ps_wo[0][:, :].rearrange("p (c n) -> p c n", c=4), AF.Identity),
                    waits=ewl)
                wo_guard[0] = (s_s, s_s.c)
            ywo_guard[0] = s_t.c
        ctx.emit("sync", lambda e, ch=ch: e.dma_start(
            ctx.dram["s1_feat"].ap()[:, :, ch * 512:(ch + 1) * 512], s1_stage[:, :, :]),
            waits=[s_s.at()], inc=s_st, inc_n=16)
        s1_st_guard[0] = s_st.c
    ctx.barrier()
    es.close()
    ctx.scope["de"].close()


def phase_l1_ffn(ctx):
    c = ctx.consts
    s_lda, s_st = ctx.sems["s_lda"], ctx.sems["s_st"]
    s_v, s_s, s_t = ctx.sems["s_v"], ctx.sems["s_s"], ctx.sems["s_t"]
    s_g = ctx.sems["s_g"]
    sw0, sw1 = ctx.sems["s_wb0"], ctx.sems["s_wb1"]
    es = contextlib.ExitStack()
    W1b = ctx.sb(es, "f_W1b", [128, 8, HID1], BF16)
    W2b = ctx.sb(es, "f_W2b", [128, 32, D], BF16)
    xn1c = ctx.sb(es, "f_xn1c", [128, 8, 512], BF16)
    hbq = ctx.sb(es, "f_hbq", [128, 16, 512], BF16)
    g1_sb = ctx.sb(es, "f_g1", [128, 8, 512], F32)
    g1b = ctx.sb(es, "f_g1b", [128, 8, 512], BF16)
    hq = ctx.sb(es, "f_hq", [128, 2, 512], F32)
    vv = ctx.sb(es, "f_vv", [128, 2, 512], F32)
    va = ctx.sb(es, "f_va", [128, 2, 512], F32)
    sn = ctx.sb(es, "f_sn", [128, 2, 512], F32)
    ps_h = [ctx.ps(es, f"f_ps_h{i}", [128, 512], F32) for i in range(3)]
    ps_o = [ctx.ps(es, f"f_ps_o{i}", [128, 512], F32) for i in range(2)]

    ctx.emit("gpsimd", lambda e: e.dma_start(
        W1b[:, :, :], rearr_feat_w(ctx.dram["L1_ffn_W1"].ap())), inc=sw0, inc_n=16)
    ctx.emit("gpsimd", lambda e: e.dma_start(
        W2b[:, :, :], rearr_feat_w(ctx.dram["L1_ffn_W2"].ap())), inc=sw1, inc_n=16)
    w1w, w2w = [sw0.at()], [sw1.at()]

    hband_guard = [None, None, None]
    oband_guard = [None, None]
    g1_st_guard = [0]
    hq_guard = [None, None]
    fold_done = [0, 0]
    sin_done = [0, 0]
    hbq_guard = [0]
    for ch in range(NCHUNK):
        ctx.emit("sync", lambda e, ch=ch: e.dma_start(
            xn1c[:, :, :], ctx.dram["xn1_feat"].ap()[:, :, ch * 512:(ch + 1) * 512]),
            waits=[s_t.at(), s_v.at(), s_s.at()], inc=s_lda, inc_n=16)
        wld = [s_lda.at()]
        for half in range(2):
            def emit_sin(mm):
                p2 = (half * 16 + mm) % 2
                ctx.emit("scalar", lambda e, p2=p2: e.activation(
                    sn[:, p2, :], vv[:, p2, :], AF.Sin, scale=TWOPI), waits=[(s_v, fold_done[p2])])
                sin_done[p2] = s_s.c

            def emit_tail(mm):
                p2 = (half * 16 + mm) % 2
                mg2 = half * 16 + mm
                ctx.emit("scalar", lambda e, mg2=mg2, p2=p2: e.activation(
                    sn[:, p2, :], sn[:, p2, :], AF.Identity, scale=c["Cv"][:, mg2:mg2 + 1]))
                ctx.emit("vector", lambda e, p2=p2: e.tensor_tensor(
                    sn[:, p2, :], hq[:, p2, :], sn[:, p2, :], ALU.add), waits=[s_s.at()])
                hwl = []
                if mm == 0 and hbq_guard[0] > 0:
                    hwl.append((s_t, hbq_guard[0]))
                ctx.emit("vector", lambda e, mm=mm, p2=p2: e.tensor_single_scalar(
                    hbq[:, mm, :], sn[:, p2, :], 0.0, ALU.max), waits=hwl)
                hq_guard[p2] = (s_v, s_v.c)

            for m in range(16):
                mg = half * 16 + m
                p = mg % 2
                bank = m % 3
                wl = list(wld) + w1w if (half == 0 and m == 0) else []
                if hband_guard[bank] is not None:
                    wl.append(hband_guard[bank])
                for k in range(8):
                    ctx.emit("tensor", lambda e, m=m, mg=mg, k=k, bank=bank: e.matmul(
                        ps_h[bank][:, :], W1b[:, k, mg * 128:(mg + 1) * 128], xn1c[:, k, :],
                        start=(k == 0), stop=(k == 7)), waits=wl if k == 0 else [],
                        inc=(s_t if k == 7 else None))
                hqw = [s_t.at()]
                if hq_guard[p] is not None:
                    hqw.append(hq_guard[p])
                ctx.emit("scalar", lambda e, mg=mg, bank=bank, p=p: e.activation(
                    hq[:, p, :], ps_h[bank][:, :], AF.Identity, bias=c["b1v"][:, mg:mg + 1]),
                    waits=hqw)
                ctx.emit("scalar", lambda e, mg=mg, bank=bank, p=p: e.activation(
                    vv[:, p, :], ps_h[bank][:, :], AF.Identity, scale=0.1 * INV2PI,
                    bias=c["qbias"][:, mg:mg + 1]))
                hband_guard[bank] = (s_s, s_s.c)
                ctx.emit("vector", lambda e, p=p: e.tensor_single_scalar(
                    va[:, p, :], vv[:, p, :], 0.5, ALU.is_gt), waits=[s_s.at()])
                ctx.emit("vector", lambda e, p=p: e.scalar_tensor_tensor(
                    vv[:, p, :], va[:, p, :], -1.0, vv[:, p, :], ALU.mult, ALU.add))
                ctx.emit("vector", lambda e, p=p: e.tensor_single_scalar(
                    va[:, p, :], vv[:, p, :], -0.5, ALU.is_lt))
                ctx.emit("vector", lambda e, p=p: e.scalar_tensor_tensor(
                    vv[:, p, :], va[:, p, :], 1.0, vv[:, p, :], ALU.mult, ALU.add))
                fold_done[p] = s_v.c
                if m > 0:
                    emit_sin(m - 1)
                    emit_tail(m - 1)
            emit_sin(15)
            emit_tail(15)
            whb = [s_v.at()]
            for mo in range(8):
                bank = mo % 2
                wl = list(whb) + w2w if mo == 0 else []
                if oband_guard[bank] is not None:
                    wl.append(oband_guard[bank])
                for kh in range(16):
                    khg = half * 16 + kh
                    ctx.emit("tensor", lambda e, mo=mo, kh=kh, khg=khg, bank=bank: e.matmul(
                        ps_o[bank][:, :], W2b[:, khg, mo * 128:(mo + 1) * 128], hbq[:, kh, :],
                        start=(kh == 0), stop=(kh == 15)), waits=wl if kh == 0 else [],
                        inc=(s_t if kh == 15 else None))
                if half == 0:
                    ctx.emit("scalar", lambda e, mo=mo, bank=bank: e.activation(
                        g1_sb[:, mo, :], ps_o[bank][:, :], AF.Identity,
                        bias=c["b2v"][:, mo:mo + 1]), waits=[s_t.at()])
                    oband_guard[bank] = (s_s, s_s.c)
                else:
                    ctx.emit("vector", lambda e, mo=mo, bank=bank: e.tensor_tensor(
                        g1_sb[:, mo, :], g1_sb[:, mo, :], ps_o[bank][:, :], ALU.add),
                        waits=[s_t.at()])
                    oband_guard[bank] = (s_v, s_v.c)
            hbq_guard[0] = s_t.c
        gwl = [s_v.at()]
        if g1_st_guard[0] > 0:
            gwl.append((s_st, g1_st_guard[0]))
        for mo in range(8):
            ctx.emit("scalar", lambda e, mo=mo: e.activation(
                g1b[:, mo, :], g1_sb[:, mo, :], AF.Silu), waits=gwl if mo == 0 else [])
        ctx.emit("sync", lambda e, ch=ch: e.dma_start(
            ctx.dram["g1_feat"].ap()[:, :, ch * 512:(ch + 1) * 512], g1b[:, :, :]),
            waits=[s_s.at()], inc=s_st, inc_n=16)
        g1_st_guard[0] = s_st.c
    ctx.barrier()
    es.close()


def phase_final(ctx):
    c = ctx.consts
    s_lda, s_st = ctx.sems["s_lda"], ctx.sems["s_st"]
    s_v, s_s, s_t = ctx.sems["s_v"], ctx.sems["s_s"], ctx.sems["s_t"]
    es = contextlib.ExitStack()
    NQ, QT, NT8 = 4, 1024, 8
    x_tok = ctx.sb(es, "o_x_tok", [128, NT8, D], F32)
    s_sb = ctx.sb(es, "o_s", [128, 8, QT], BF16)
    g_sb = ctx.sb(es, "o_g", [128, 8, QT], BF16)
    delta_f = ctx.sb(es, "o_delta_f", [128, 8, QT], BF16)
    delta_t = ctx.sb(es, "o_delta_t", [128, NT8, D], BF16)
    ps_tr = [ctx.ps(es, "o_ps_trA", [128, 128], BF16), ctx.ps(es, "o_ps_trB", [128, 128], BF16)]
    emit_tr = ctx.make_transposer(ps_tr, c["ident"])
    st_guard = [0]
    for q in range(NQ):
        ctx.emit("sync", lambda e, q=q: e.dma_start(
            x_tok[:, :, :], ctx.dram["x1m"].ap()[q * QT:(q + 1) * QT, :].rearrange(
                "(t p) d -> p t d", p=128)),
            waits=[s_t.at(), s_v.at()] + ([(s_st, st_guard[0])] if st_guard[0] > 0 else []),
            inc=s_lda, inc_n=16)
        ctx.emit("sync", lambda e, q=q: e.dma_start(
            s_sb[:, :, :], ctx.dram["s1_feat"].ap()[:, :, q * QT:(q + 1) * QT]), inc=s_lda, inc_n=16)
        ctx.emit("sync", lambda e, q=q: e.dma_start(
            g_sb[:, :, :], ctx.dram["g1_feat"].ap()[:, :, q * QT:(q + 1) * QT]), inc=s_lda, inc_n=16)
        wld = [s_lda.at()]
        for kc in range(8):
            ctx.emit("vector", lambda e, kc=kc: e.tensor_tensor(
                delta_f[:, kc, :], s_sb[:, kc, :], g_sb[:, kc, :], ALU.mult),
                waits=wld if kc == 0 else [])
        wdf = [s_v.at()]
        for t in range(NT8):
            for kc in range(8):
                emit_tr(delta_f[:, kc, t * 128:(t + 1) * 128],
                        delta_t[:, t, kc * 128:(kc + 1) * 128],
                        tw=(wdf if (t == 0 and kc == 0) else []))
        for t in range(NT8):
            ctx.emit("vector", lambda e, t=t: e.tensor_tensor(
                x_tok[:, t, :], x_tok[:, t, :], delta_t[:, t, :], ALU.add))
        ctx.emit("sync", lambda e, q=q: e.dma_start(
            ctx.dram["out"].ap()[q * QT:(q + 1) * QT, :].rearrange("(t p) d -> p t d", p=128),
            x_tok[:, :, :]), waits=[s_v.at()], inc=s_st, inc_n=16)
        st_guard[0] = s_st.c
    ctx.barrier()
    es.close()


def phase_dbg_a(ctx):
    s_v, s_st = ctx.sems["s_v"], ctx.sems["s_st"]
    es = contextlib.ExitStack()
    ctx.emit("sync", lambda e: e.dma_start(ctx.dram["kv_dbg"].ap(), ctx.consts["kv_acc"][:, :]),
             inc=s_st, inc_n=16)
    ctx.emit("sync", lambda e: e.dma_start(ctx.dram["wtok_dbg"].ap(), ctx.consts["w_tok"][:, :, :]),
             inc=s_st, inc_n=16)
    ctx.barrier()
    es.close()


DRAM_SPECS = {
    "kv_dbg": ([128, 512], F32),
    "xr_feat": ([128, 8, T], BF16),
    "q_feat": ([128, 8, T], BF16),
    "u_feat": ([128, 8, T], BF16),
    "s_feat": ([128, 8, T], BF16),
    "x1m": ([T, D], F32),
    "g_feat": ([128, 8, T], BF16),
    "xn1_feat": ([128, 8, T], BF16),
    "s1_feat": ([128, 8, T], BF16),
    "g1_feat": ([128, 8, T], BF16),
    "wtok_dbg": ([128, 32, 14], F32),
}

INPUT_SPECS = {
    "L0_attn_x": ([T, D], F32),
    "L0_attn_Wq": ([D, D], F32), "L0_attn_Wk": ([D, D], F32), "L0_attn_Wv": ([D, D], F32),
    "L0_attn_Wu": ([D, D], F32), "L0_attn_Wo": ([D, D], F32),
    "L0_ffn_router_w": ([D, E_ROUTED], F32),
    "L0_ffn_rms_w": ([E_ROUTED, D], F32),
    "L0_ffn_w1": ([E_ROUTED, D, HID0], F32), "L0_ffn_w2": ([E_ROUTED, HID0, D], F32),
    "L0_ffn_rms_w_s": ([E_SHARED, D], F32),
    "L0_ffn_w1_s": ([E_SHARED, D, HID0], F32), "L0_ffn_w2_s": ([E_SHARED, HID0, D], F32),
    "L1_attn_Wq": ([D, D], F32), "L1_attn_Wk": ([D, 256], F32), "L1_attn_Wv": ([D, 256], F32),
    "L1_attn_Wo": ([D, D], F32),
    "L1_ffn_W1": ([D, HID1], F32), "L1_ffn_b1": ([HID1], F32),
    "L1_ffn_quantum_phases": ([HID1], F32),
    "L1_ffn_quantum_weights_real": ([HID1], F32),
    "L1_ffn_quantum_weights_imag": ([HID1], F32),
    "L1_ffn_W2": ([HID1, D], F32), "L1_ffn_b2": ([D], F32),
}


def build_program(phases, expose=(), scratch_inputs=()):
    nc = bass.Bass("TRN2", target_bir_lowering=False)
    ctx = Ctx(nc)
    for name, (shape, dt) in INPUT_SPECS.items():
        ctx.dram[name] = nc.dram_tensor(name, shape, dt, kind="ExternalInput")
    ctx.dram["out"] = nc.dram_tensor("out", [T, D], F32, kind="ExternalOutput")
    for name, (shape, dt) in DRAM_SPECS.items():
        kind = ("ExternalInput" if name in scratch_inputs else
                ("ExternalOutput" if name in expose else "Internal"))
        ctx.dram[name] = nc.dram_tensor(name, shape, dt, kind=kind)
    with ctx.es_global:
        setup_consts(ctx)
        for sc in ("ab", "ac", "de"):
            ctx.scope[sc] = ctx.es_global.enter_context(contextlib.ExitStack())
        for ph in phases:
            ph(ctx)
        ctx.replay()
    return ctx


ALL_PHASES = None


def kernel(**inputs):
    x_full = np.ascontiguousarray(np.asarray(inputs["L0_attn_x"]))
    ctx = build_program([setup_l0_weights, phase_l0_a, phase_l0_b, phase_l0_moe, phase_r0,
                         setup_l1_weights, setup_l1_masks, phase_l1_kv, phase_l1_attn,
                         phase_l1_ffn, phase_final])
    weights = {k: np.ascontiguousarray(np.asarray(v)) for k, v in inputs.items()
               if k != "L0_attn_x"}
    in_maps = []
    for core in range(8):
        m = dict(weights)
        m["L0_attn_x"] = np.ascontiguousarray(x_full[core])
        in_maps.append(m)
    res = run_bass_kernel_spmd(ctx.nc, in_maps, core_ids=list(range(8)))
    return np.stack([r["out"] for r in res.results], axis=0)
